# version 08
# baseline (speedup 1.0000x reference)
from contextlib import ExitStack

import numpy as np

import concourse.bass as bass
import concourse.mybir as mybir
from concourse.bass_utils import run_bass_kernel_spmd
from concourse.bass_core_rust import IndirectOffsetOnAxis

F32 = mybir.dt.float32
BF16 = mybir.dt.bfloat16
I32 = mybir.dt.int32
ALU = mybir.AluOpType
AF = mybir.ActivationFunctionType
AX = mybir.AxisListType

NC = 8
T = 2048
D = 1024
H = 16
HD = 4096
CAP = 4096
NT = 4
CH = 512
TC = 8
TRASH = 32768


class Sem:
    def __init__(self, nc, ctx, name):
        self.h = ctx.enter_context(nc.semaphore(name))
        self.n = 0

    def inc(self, inst, k=1):
        inst.then_inc(self.h, k)
        self.n += k
        return self.n


def build_program():
    nc = bass.Bass("TRN2", target_bir_lowering=False,
                   detect_race_conditions=False)
    ctx = ExitStack()

    def din(name, shape, dt=F32):
        return nc.dram_tensor(name, shape, dt, kind="ExternalInput").ap()

    x_in = din("x", [T, D])
    wkt = din("wkt", [D, 48])
    wv2 = din("wv2", [D, D])
    ow2 = din("ow2", [D, D])
    wgm = din("wgm", [D, 8])
    w1m = din("w1m", [D, HD])
    w2m = din("w2m", [HD, D])
    b1m_i = din("b1m", [128, 32])
    b2r_i = din("b2row", [128, D])
    cM_i = din("cM", [128, 128])
    a1_i = din("a1c", [128, 8])
    b1_i = din("b1c", [128, 8])
    a2_i = din("a2c", [128, 8])
    b2_i = din("b2c", [128, 8])
    tri_i = din("tri", [128, 128])
    bsel_i = din("bsel", [16, 1024])
    nvec_i = din("nvec", [128, T])
    seqm_i = din("seqm", [8, 1])
    offm_i = din("offm", [16, 2])
    offr0_i = din("offmr0", [16, 128])
    offr1_i = din("offmr1", [16, 128])
    gtok_i = din("gtok", [128, 16], I32)
    evec_i = din("evec", [128, 256])
    e512_i = din("e512", [128, 256])
    thr_i = din("thr", [128, 256])
    ltm_i = din("ltm", [128, 1024])
    ones1_i = din("ones1", [1, 128])
    onesc_i = din("onesc", [128, 1])
    einv_i = din("einv", [128, 32], I32)

    y_out = nc.dram_tensor("y_out", [T, D], F32, kind="ExternalOutput").ap()

    def dint(name, shape, dt=F32):
        return nc.dram_tensor(name, shape, dt).ap()

    xnh_d = dint("xnh_d", [T, D], BF16)
    xnl_d = dint("xnl_d", [T, D], BF16)
    s_d = dint("s_d", [8, 128, T])
    v_d = dint("v_d", [8, 128, T])
    x1_d = dint("x1_d", [T, D])
    agcar_i = dint("agcar_i", [1, 1040])
    agcar_o = dint("agcar_o", [8, 1040])
    agx_i = dint("agx_i", [T, 2 * D], BF16)
    agx_o = dint("agx_o", [NC * T, 2 * D], BF16)
    agc_i = dint("agc_i", [2, 8])
    agc_o = dint("agc_o", [16, 8])
    invl_d = dint("invl_d", [33024, 1], I32)
    invg_d = dint("invg_d", [33024, 1], I32)
    agy_i = dint("agy_i", [CAP, D])
    agy_o = dint("agy_o", [NC * CAP, D])
    w1h_d = dint("w1h_d", [D, HD], BF16)
    w1l_d = dint("w1l_d", [D, HD], BF16)
    w2h_d = dint("w2h_d", [HD, D], BF16)
    w2l_d = dint("w2l_d", [HD, D], BF16)

    def sb(name, shape, dt=F32):
        return ctx.enter_context(nc.sbuf_tensor(name, shape, dt))

    wk_h = sb("wk_h", [128, 8, 48], BF16)
    wk_l = sb("wk_l", [128, 8, 48], BF16)
    wg_h = sb("wg_h", [128, 8, 8], BF16)
    wg_l = sb("wg_l", [128, 8, 8], BF16)
    cM_s = sb("cM_s", [128, 128])
    a1s = sb("a1s", [128, 8]); b1s = sb("b1s", [128, 8])
    a2s = sb("a2s", [128, 8]); b2s = sb("b2s", [128, 8])
    tri_s = sb("tri_s", [128, 128])
    bsel_s = sb("bsel_s", [16, 1024])
    nvec_s = sb("nvec_s", [128, T])
    seqm_s = sb("seqm_s", [8, 1])
    offm_s = sb("offm_s", [16, 2])
    gtok_s = sb("gtok_s", [128, 16], I32)
    evec_s = sb("evec_s", [128, 256])
    e512_s = sb("e512_s", [128, 256])
    thr_s = sb("thr_s", [128, 256])
    ltm_s = sb("ltm_s", [128, 1024])
    ones1_s = sb("ones1_s", [1, 128])
    onesc_s = sb("onesc_s", [128, 1])
    b1m_s = sb("b1m_s", [128, 32])
    b2r_s = sb("b2r_s", [128, D])
    einv_s = sb("einv_s", [128, 32], I32)
    scarL = [sb(f"scar{i}", [128, 1]) for i in range(8)]
    Zcar = sb("Zcar", [16, 1])
    car2 = sb("car2", [128, 8])
    zc2 = sb("zc2", [16, 1])
    agc_s = sb("agc_s", [16, 8])
    offmr0_s = sb("offmr0_s", [16, 128])
    offmr1_s = sb("offmr1_s", [16, 128])
    ohst = sb("ohst", [128, 2 * 16 * 8])
    plst = sb("plst", [128, 32])
    g1st = sb("g1st", [128, 16]); g2st = sb("g2st", [128, 16])
    v1st = sb("v1st", [128, 16]); v2st = sb("v2st", [128, 16])
    ybst = sb("ybst", [128, 32], I32)
    sct = sb("sct", [128, 32], I32)
    zer_s = sb("zer_s", [128, 258], I32)
    invt = sb("invt", [128, 32], I32)
    cnt_s = sb("cnt_s", [1, 16])

    pb = [ctx.enter_context(nc.psum_tensor(f"pb{i}", [128, 512], F32))
          for i in range(8)]

    LD = Sem(nc, ctx, "LD")
    SP = Sem(nc, ctx, "SP")
    VC = Sem(nc, ctx, "VC")
    SC = Sem(nc, ctx, "SC")
    TE = Sem(nc, ctx, "TE")
    CC = Sem(nc, ctx, "CC")
    WL = Sem(nc, ctx, "WL")

    block = ctx.enter_context(nc.Block())

    def SY(f): block.sync(f)
    def GP(f): block.gpsimd(f)
    def VE(f): block.vector(f)
    def SE(f): block.scalar(f)
    def TN(f): block.tensor(f)

    ctxA = ExitStack()
    def sba(name, shape, dt=F32):
        return ctxA.enter_context(nc.sbuf_tensor(name, shape, dt))

    wv_h = sba("wv_h", [128, 8 * D], BF16)
    wv_l = sba("wv_l", [128, 8 * D], BF16)
    ow_h = sba("ow_h", [128, 8 * D], BF16)
    ow_l = sba("ow_l", [128, 8 * D], BF16)
    Z_s = sba("Z_s", [16, T])
    r_s = sba("r_s", [16, T])
    dinv_s = sba("dinv_s", [16, T])
    r_s2 = sba("r_s2", [16, T])
    ss_t = sba("ss_t", [128, CH])
    vv_t = sba("vv_t", [128, CH])
    xt2 = [sba(f"xt{i}", [128, D]) for i in range(2)]
    xc2 = [sba(f"xc{i}", [128, D]) for i in range(2)]
    xn322 = [sba(f"xn32{i}", [128, D]) for i in range(2)]
    xnh2 = [sba(f"xnh{i}", [128, D], BF16) for i in range(2)]
    xnl2 = [sba(f"xnl{i}", [128, D], BF16) for i in range(2)]
    sm12 = [sba(f"sm1{i}", [128, 8]) for i in range(2)]

    ctxW = ExitStack()
    def sbw(name, shape, dt=F32):
        return ctxW.enter_context(nc.sbuf_tensor(name, shape, dt))

    wtmp2 = [sbw(f"wtmp{i}", [128, 2048]) for i in range(2)]
    wth2 = [sbw(f"wth{i}", [128, 2048], BF16) for i in range(2)]
    wtl2 = [sbw(f"wtl{i}", [128, 2048], BF16) for i in range(2)]
    wkt32 = sbw("wkt32", [128, 8, 48])
    wg32 = sbw("wg32", [128, 8, 8])

    @SY
    def _(sy):
        for t, d_ in ((cM_s, cM_i), (tri_s, tri_i), (bsel_s, bsel_i),
                      (nvec_s, nvec_i), (seqm_s, seqm_i), (offm_s, offm_i),
                      (gtok_s, gtok_i), (evec_s, evec_i), (e512_s, e512_i),
                      (thr_s, thr_i), (ltm_s, ltm_i), (ones1_s, ones1_i),
                      (onesc_s, onesc_i), (b1m_s, b1m_i), (b2r_s, b2r_i),
                      (a1s, a1_i), (b1s, b1_i), (a2s, a2_i), (b2s, b2_i),
                      (einv_s, einv_i), (offmr0_s, offr0_i),
                      (offmr1_s, offr1_i)):
            LD.inc(sy.dma_start(out=t[:], in_=d_[:, :]), 16)
        LD.inc(sy.dma_start(
            out=wkt32[:], in_=wkt.rearrange("(f p) c -> p f c", p=128)), 16)
        LD.inc(sy.dma_start(
            out=wg32[:], in_=wgm.rearrange("(f p) c -> p f c", p=128)), 16)
    ld_consts = LD.n

    @VE
    def _(v):
        v.wait_ge(LD.h, ld_consts)
        for i in range(8):
            v.memset(scarL[i][:], 0.0)
        v.memset(Zcar[:], 0.0)
        v.memset(cnt_s[:], 0.0)
        v.memset(zer_s[:], 0)
        v.tensor_copy(wk_h[:], wkt32[:])
        v.tensor_copy(wg_h[:], wg32[:])
        v.tensor_tensor(wk_l[:], wkt32[:], wk_h[:], ALU.subtract)
        VC.inc(v.tensor_tensor(wg_l[:], wg32[:], wg_h[:], ALU.subtract))
    vc_consts = VC.n

    vfree = [VC.n, VC.n]
    for ji, (wsrc, dsth, dstl) in enumerate(
            [(wv2, wv_h, wv_l), (ow2, ow_h, ow_l)]):
        for fc in range(8):
            b = (ji * 8 + fc) % 2
            @SY
            def _(sy, fc=fc, b=b, wsrc=wsrc):
                sy.wait_ge(VC.h, vfree[b])
                LD.inc(sy.dma_start(out=wtmp2[b][:, 0:1024],
                                    in_=wsrc[fc * 128:(fc + 1) * 128, :]), 16)
            ldc = LD.n

            @VE
            def _(v, fc=fc, b=b, ldc=ldc, dsth=dsth, dstl=dstl):
                v.wait_ge(LD.h, ldc)
                v.tensor_copy(dsth[:, fc * D:(fc + 1) * D],
                              wtmp2[b][:, 0:1024])
                VC.inc(v.tensor_tensor(dstl[:, fc * D:(fc + 1) * D],
                                       wtmp2[b][:, 0:1024],
                                       dsth[:, fc * D:(fc + 1) * D],
                                       ALU.subtract))
                vfree[b] = VC.n



    def emit_w12_prep():
        vfree2 = [VC.n, VC.n]
        spfree = [SP.n, SP.n]
        jobs = []
        for fc in range(8):
            for hh in range(2):
                jobs.append(("w1", fc, hh))
        for hc2 in range(16):
            jobs.append(("w2", hc2, 0))
        for ji, job in enumerate(jobs):
            b = ji % 2
            kind, i0, i1 = job

            @SE
            def _(sy, kind=kind, i0=i0, i1=i1, b=b, ji=ji):
                sy.wait_ge(VC.h, vfree2[b])
                if kind == "w1":
                    WL.inc(sy.dma_start(
                        out=wtmp2[b][:],
                        in_=w1m[i0 * 128:(i0 + 1) * 128,
                                i1 * 2048:(i1 + 1) * 2048]), 16)
                else:
                    WL.inc(sy.dma_start(
                        out=wtmp2[b][:],
                        in_=w2m.rearrange("(c p) m -> p c m", p=128)[
                            :, i0 * 2:(i0 + 1) * 2, :]), 16)
            ldc = WL.n

            @VE
            def _(v, b=b, ldc=ldc):
                v.wait_ge(WL.h, ldc)
                v.wait_ge(SP.h, spfree[b])
                v.tensor_copy(wth2[b][:], wtmp2[b][:])
                v.drain()
                VC.inc(v.tensor_tensor(wtl2[b][:], wtmp2[b][:], wth2[b][:],
                                       ALU.subtract))
                vfree2[b] = VC.n
            vcc = VC.n

            @GP
            def _(g, kind=kind, i0=i0, i1=i1, b=b, vcc=vcc):
                g.wait_ge(VC.h, vcc)
                if kind == "w1":
                    SP.inc(g.dma_start(
                        out=w1h_d[i0 * 128:(i0 + 1) * 128,
                                  i1 * 2048:(i1 + 1) * 2048],
                        in_=wth2[b][:]), 16)
                    SP.inc(g.dma_start(
                        out=w1l_d[i0 * 128:(i0 + 1) * 128,
                                  i1 * 2048:(i1 + 1) * 2048],
                        in_=wtl2[b][:]), 16)
                else:
                    SP.inc(g.dma_start(
                        out=w2h_d.rearrange("(c p) m -> p c m", p=128)[
                            :, i0 * 2:(i0 + 1) * 2, :], in_=wth2[b][:]), 16)
                    SP.inc(g.dma_start(
                        out=w2l_d.rearrange("(c p) m -> p c m", p=128)[
                            :, i0 * 2:(i0 + 1) * 2, :], in_=wtl2[b][:]), 16)
                spfree[b] = SP.n


    emit_w12_prep()
    sp_w12 = SP.n
    ctxW.close()

    vc_apply = [0] * 18
    sp_p1_tiles = [0] * 16
    for tt in range(16):
        b = tt % 2
        xt = xt2[b]; xc_ = xc2[b]; xn32 = xn322[b]
        xnh_t = xnh2[b]; xnl_t = xnl2[b]; sm = sm12[b]
        s1_ = sm[:, 0:1]; s2_ = sm[:, 1:2]; mu_ = sm[:, 2:3]
        var_ = sm[:, 3:4]; isd_ = sm[:, 4:5]; s2b_ = sm[:, 5:6]
        nmu_ = sm[:, 6:7]

        @GP
        def _(g, tt=tt, xt=xt):
            if tt > 1:
                g.wait_ge(VC.h, vc_apply[tt])
            SP.inc(g.dma_start(out=xt[:],
                               in_=x_in[tt * 128:(tt + 1) * 128, :]), 16)
        sp_x = SP.n

        @SE
        def _(s, sp_x=sp_x, xt=xt, xn32=xn32, s1_=s1_, s2_=s2_):
            s.wait_ge(SP.h, sp_x)
            s.activation(xn32[:], xt[:], AF.Square, accum_out=s2_)
            SC.inc(s.activation(xn32[:], xt[:], AF.Identity, accum_out=s1_))
        sc_sq = SC.n

        @VE
        def _(v, sc_sq=sc_sq, s1_=s1_, s2_=s2_, mu_=mu_, var_=var_,
              s2b_=s2b_, nmu_=nmu_):
            v.wait_ge(SC.h, sc_sq)
            v.tensor_scalar(mu_, s1_, 1.0 / D, None, ALU.mult)
            v.tensor_scalar(s2b_, s2_, 1.0 / D, 1e-5, ALU.mult, ALU.add)
            v.tensor_scalar(nmu_, mu_, -1.0, None, ALU.mult)
            v.tensor_tensor(var_, mu_, mu_, ALU.mult)
            v.drain()
            v.tensor_tensor(s1_, s2b_, var_, ALU.subtract)
            v.drain()
            VC.inc(v.reciprocal(var_, s1_))
        vc_st = VC.n

        @SE
        def _(s, vc_st=vc_st, isd_=isd_, var_=var_):
            s.wait_ge(VC.h, vc_st)
            SC.inc(s.activation(isd_, var_, AF.Sqrt))
        sc_ap = SC.n

        @VE
        def _(v, tt=tt, sc_ap=sc_ap, xt=xt, xc_=xc_, xn32=xn32,
              xnh_t=xnh_t, xnl_t=xnl_t, nmu_=nmu_, isd_=isd_):
            v.wait_ge(SC.h, sc_ap)
            if tt > 1:
                v.wait_ge(SP.h, sp_p1_tiles[tt - 2])
            v.tensor_scalar(xc_[:], xt[:], nmu_, None, ALU.add)
            v.tensor_scalar(xn32[:], xc_[:], isd_, None, ALU.mult)
            v.tensor_copy(xnh_t[:], xn32[:])
            VC.inc(v.tensor_tensor(xnl_t[:], xn32[:], xnh_t[:],
                                   ALU.subtract))
        vc_apply[tt + 2] = VC.n

        @GP
        def _(g, tt=tt, xnh_t=xnh_t, xnl_t=xnl_t, vcc=VC.n):
            g.wait_ge(VC.h, vcc)
            SP.inc(g.dma_start(out=xnh_d[tt * 128:(tt + 1) * 128, :],
                               in_=xnh_t[:]), 16)
            SP.inc(g.dma_start(out=xnl_d[tt * 128:(tt + 1) * 128, :],
                               in_=xnl_t[:]), 16)
        sp_p1_tiles[tt] = SP.n

    xt = xt2[0]; xc_ = xc2[0]; xn32 = xn322[0]
    xnh_t = xnh2[0]; xnl_t = xnl2[0]
    sm = sm12[0]
    s1_ = sm[:, 0:1]; s2_ = sm[:, 1:2]; mu_ = sm[:, 2:3]
    var_ = sm[:, 3:4]; isd_ = sm[:, 4:5]; s2b_ = sm[:, 5:6]
    nmu_ = sm[:, 6:7]

    ctxP2 = ExitStack()
    def sb2(name, shape, dt=F32):
        return ctxP2.enter_context(nc.sbuf_tensor(name, shape, dt))

    xfm_h = sb2("xfm_h", [128, 8 * CH], BF16)
    xfm_l = sb2("xfm_l", [128, 8 * CH], BF16)
    ssin = sb2("ssin", [128, 4 * CH])
    c_t = sb2("c_t", [16, CH]); d_t = sb2("d_t", [16, CH])
    e_t = sb2("e_t", [16, CH]); p2_t = sb2("p2_t", [16, CH])
    rt1 = sb2("rt1", [16, CH])
    cx_t = sb2("cx_t", [128, CH])
    cv_t = sb2("cv_t", [128, CH])

    cx_evac = [VC.n]
    vbank_free = {4: VC.n, 5: VC.n, 6: VC.n, 7: VC.n}
    sc_scar = [SC.n]

    for nt in range(NT):
        t0 = nt * CH

        @SY
        def _(sy, nt=nt, t0=t0):
            sy.wait_ge(SP.h, sp_p1_tiles[4 * nt + 3])
            if nt > 0:
                sy.wait_ge(TE.h, TE.n)
            for fc in range(8):
                LD.inc(sy.dma_start_transpose(
                    xfm_h[:, fc * CH:(fc + 1) * CH],
                    xnh_d[t0:t0 + CH, fc * 128:(fc + 1) * 128]), 16)
                LD.inc(sy.dma_start_transpose(
                    xfm_l[:, fc * CH:(fc + 1) * CH],
                    xnl_d[t0:t0 + CH, fc * 128:(fc + 1) * 128]), 16)
        ld_xfm = LD.n

        sin_sc = []
        for rnd in range(4):
            aa, bb = (a1s, b1s) if rnd < 2 else (a2s, b2s)
            base = (rnd % 2) * 4

            @VE
            def _(v, t0=t0, rnd=rnd, aa=aa, bb=bb, base=base):
                v.wait_ge(TE.h, TE.n)
                nv = nvec_s[:, t0:t0 + CH]
                for q in range(4):
                    VC.inc(v.tensor_scalar(ssin[:, q * CH:(q + 1) * CH], nv,
                                           aa[:, base + q:base + q + 1],
                                           bb[:, base + q:base + q + 1],
                                           ALU.mult, ALU.add))
            vc_a = VC.n

            @SE
            def _(s, vc_a=vc_a):
                s.wait_ge(VC.h, vc_a)
                for q in range(4):
                    SC.inc(s.activation(ssin[:, q * CH:(q + 1) * CH],
                                        ssin[:, q * CH:(q + 1) * CH],
                                        AF.Sin))
            sin_sc.append(SC.n)

            if rnd == 0:
                @TN
                def _(t, nt=nt, ld_xfm=ld_xfm):
                    t.wait_ge(LD.h, ld_xfm)
                    t.wait_ge(VC.h, VC.n)
                    for kind in range(3):
                        for fc in range(8):
                            xh = xfm_h[:, fc * CH:(fc + 1) * CH]
                            xl = xfm_l[:, fc * CH:(fc + 1) * CH]
                            kh = wk_h[:, fc, kind * 16:(kind + 1) * 16]
                            kl = wk_l[:, fc, kind * 16:(kind + 1) * 16]
                            t.matmul(pb[kind][0:16, :], kh, xh,
                                     start=(fc == 0), stop=False)
                            t.matmul(pb[kind][0:16, :], kh, xl, start=False,
                                     stop=False)
                            TE.inc(t.matmul(pb[kind][0:16, :], kl, xh,
                                            start=False,
                                            stop=(kind > 0 and fc == 7)))

            @TN
            def _(t, rnd=rnd, sc_sin=sin_sc[rnd]):
                t.wait_ge(SC.h, sc_sin)
                bank = 0 if rnd < 2 else 3
                for q in range(4):
                    sc = (rnd % 2) * 4 + q
                    TE.inc(t.matmul(
                        pb[bank][0:16, :], cM_s[:, sc * 16:(sc + 1) * 16],
                        ssin[:, q * CH:(q + 1) * CH],
                        start=(bank == 3 and sc == 0),
                        stop=(sc == 7)))
        te_p2m = TE.n

        @VE
        def _(v, te_p2m=te_p2m):
            v.wait_ge(TE.h, te_p2m)
            v.tensor_scalar(c_t[:], pb[0][0:16, :], 20.0, -20.0, ALU.min,
                            ALU.max)
            v.tensor_scalar(d_t[:], pb[1][0:16, :], 20.0, -20.0, ALU.min,
                            ALU.max)
            v.tensor_scalar(e_t[:], pb[2][0:16, :], 20.0, -20.0, ALU.min,
                            ALU.max)
            VC.inc(v.tensor_scalar(p2_t[:], pb[3][0:16, :], 20.0, -20.0,
                                   ALU.min, ALU.max))
        vc_clip = VC.n

        @SE
        def _(s, vc_clip=vc_clip):
            s.wait_ge(VC.h, vc_clip)
            s.activation(c_t[:], c_t[:], AF.Exp)
            s.activation(d_t[:], d_t[:], AF.Exp)
            s.activation(e_t[:], e_t[:], AF.Exp)
            SC.inc(s.activation(p2_t[:], p2_t[:], AF.Exp))
        sc_exp = SC.n

        @VE
        def _(v, t0=t0, sc_exp=sc_exp):
            v.wait_ge(SC.h, sc_exp)
            v.reciprocal(rt1[:], d_t[:])
            v.tensor_tensor(ss_t[0:16, :], p2_t[:], e_t[:], ALU.mult)
            v.tensor_tensor(r_s[:, t0:t0 + CH], ss_t[0:16, :], rt1[:],
                            ALU.mult)
            VC.inc(v.tensor_tensor_scan(Z_s[:, t0:t0 + CH], onesc_s[0:16, 0:1].to_broadcast([16, CH]),
                                        c_t[:], Zcar[:, 0:1], ALU.mult,
                                        ALU.add))
        vc_zr = VC.n

        @SE
        def _(s, t0=t0, vc_zr=vc_zr):
            s.wait_ge(VC.h, vc_zr)
            SC.inc(s.copy(Zcar[:], Z_s[:, t0 + CH - 1:t0 + CH]))
        sc_zcar = SC.n

        for oc in range(8):
            bank = 4 + (oc % 4)

            @TN
            def _(t, nt=nt, oc=oc, bank=bank, vc_zr=vc_zr):
                t.wait_ge(VC.h, vc_zr)
                t.wait_ge(VC.h, cx_evac[0])
                t.matmul(pb[0][:], bsel_s[:, oc * 128:(oc + 1) * 128],
                         c_t[:], start=True, stop=True)
                t.wait_ge(VC.h, vbank_free[bank])
                for fc in range(8):
                    xh = xfm_h[:, fc * CH:(fc + 1) * CH]
                    xl = xfm_l[:, fc * CH:(fc + 1) * CH]
                    wvh = wv_h[:, fc * D + oc * 128:fc * D + (oc + 1) * 128]
                    wvl = wv_l[:, fc * D + oc * 128:fc * D + (oc + 1) * 128]
                    t.matmul(pb[bank][:], wvh, xh, start=(fc == 0),
                             stop=False)
                    t.matmul(pb[bank][:], wvh, xl, start=False, stop=False)
                    TE.inc(t.matmul(pb[bank][:], wvl, xh, start=False,
                                    stop=(fc == 7)))
            te_v = TE.n

            @VE
            def _(v, nt=nt, oc=oc, bank=bank, te_v=te_v):
                v.wait_ge(TE.h, te_v)
                v.wait_ge(SP.h, SP.n)
                v.wait_ge(SC.h, sc_scar[0])
                VC.inc(v.tensor_copy(cx_t[:], pb[0][:]))
                cx_evac[0] = VC.n
                v.tensor_tensor(cv_t[:], pb[bank][:], cx_t[:], ALU.mult)
                v.tensor_copy(vv_t[:], pb[bank][:])
                VC.inc(v.tensor_tensor_scan(ss_t[:], onesc_s[:, 0:1].to_broadcast([128, CH]), cv_t[:],
                                            scarL[oc][:, 0:1], ALU.mult,
                                            ALU.add))
                vbank_free[bank] = VC.n
            vc_soc = VC.n

            @SE
            def _(s, oc=oc, vc_soc=vc_soc):
                s.wait_ge(VC.h, vc_soc)
                SC.inc(s.copy(scarL[oc][:], ss_t[:, CH - 1:CH]))
            sc_scar[0] = SC.n

            @GP
            def _(g, nt=nt, oc=oc, t0=t0, vc_soc=vc_soc):
                g.wait_ge(VC.h, vc_soc)
                SP.inc(g.dma_start(out=s_d[oc, :, t0:t0 + CH], in_=ss_t[:]),
                       16)
                SP.inc(g.dma_start(out=v_d[oc, :, t0:t0 + CH], in_=vv_t[:]),
                       16)
    ctxP2.close()

    ctxP4 = ExitStack()
    def sb4(name, shape, dt=F32):
        return ctxP4.enter_context(nc.sbuf_tensor(name, shape, dt))

    o_h = sb4("o_h", [128, 8 * CH], BF16)
    o_l = sb4("o_l", [128, 8 * CH], BF16)
    rx_t = sb4("rx_t", [128, CH])
    dx_t = sb4("dx_t", [128, CH])
    o32 = sb4("o32", [128, CH])
    tp4 = sb4("tp4", [128, CH])
    x1t = sb4("x1t", [128, D])
    xrow = sb4("xrow", [128, D])
    agcar_s = sb4("agcar_s", [8, 1040])

    @GP
    def _(g):
        g.wait_ge(VC.h, VC.n)
        g.wait_ge(SC.h, SC.n)
        for oc in range(8):
            SP.inc(g.dma_start(out=agcar_i[0:1, oc * 128:(oc + 1) * 128],
                               in_=scarL[oc][:, 0:1],
                               allow_slow_non_contiguous=True), 16)
        SP.inc(g.dma_start(out=agcar_i[0:1, 1024:1040], in_=Zcar[:, 0:1],
                           allow_slow_non_contiguous=True), 16)
        g.wait_ge(SP.h, SP.n)
        CC.inc(g.collective_compute(
            "AllGather", ALU.bypass, replica_groups=[list(range(NC))],
            ins=[agcar_i[:, :]], outs=[agcar_o[:, :]]))
        g.wait_ge(CC.h, CC.n)
        SP.inc(g.dma_start(out=agcar_s[:], in_=agcar_o[:, :]), 16)
    sp_p3 = SP.n

    @TN
    def _(t):
        t.wait_ge(SP.h, sp_p3)
        for oc in range(8):
            t.matmul(pb[3][:, oc:oc + 1],
                     agcar_s[:, oc * 128:(oc + 1) * 128], seqm_s[:, :],
                     start=(oc == 0), stop=(oc == 7))
        TE.inc(t.matmul(pb[2][0:16, 0:1], agcar_s[:, 1024:1040],
                        seqm_s[:, :], start=True, stop=True))
    te_p3 = TE.n

    @VE
    def _(v, te_p3=te_p3):
        v.wait_ge(TE.h, te_p3)
        v.tensor_copy(car2[:], pb[3][:, 0:8])
        v.tensor_copy(zc2[:], pb[2][0:16, 0:1])
        v.drain()
        v.tensor_scalar(r_s2[:], Z_s[:], zc2[:, 0:1], None, ALU.add)
        v.tensor_tensor(dinv_s[:], r_s2[:], r_s[:], ALU.mult)
        v.tensor_scalar(r_s2[:], dinv_s[:], 1.0, None, ALU.add)
        VC.inc(v.reciprocal(dinv_s[:], r_s2[:]))
    vc_p3 = VC.n


    rx_evac = [VC.n]
    o_used = [TE.n]
    ybank_free = {4: VC.n, 5: VC.n, 6: VC.n, 7: VC.n}

    for nt in range(NT):
        t0 = nt * CH
        for oc in range(8):
            @GP
            def _(g, oc=oc, t0=t0):
                g.wait_ge(VC.h, VC.n)
                SP.inc(g.dma_start(out=ss_t[:], in_=s_d[oc, :, t0:t0 + CH]),
                       16)
                SP.inc(g.dma_start(out=vv_t[:], in_=v_d[oc, :, t0:t0 + CH]),
                       16)
            sp_sv = SP.n

            @TN
            def _(t, oc=oc, t0=t0):
                t.wait_ge(VC.h, max(vc_p3, rx_evac[0]))
                t.matmul(pb[0][:], bsel_s[:, oc * 128:(oc + 1) * 128],
                         r_s[:, t0:t0 + CH], start=True, stop=True)
                TE.inc(t.matmul(pb[1][:], bsel_s[:, oc * 128:(oc + 1) * 128],
                                dinv_s[:, t0:t0 + CH], start=True, stop=True))
            te_rd = TE.n

            @VE
            def _(v, oc=oc, te_rd=te_rd, sp_sv=sp_sv):
                v.wait_ge(TE.h, te_rd)
                v.wait_ge(SP.h, sp_sv)
                v.wait_ge(TE.h, o_used[0])
                v.tensor_copy(rx_t[:], pb[0][:])
                VC.inc(v.tensor_copy(dx_t[:], pb[1][:]))
                rx_evac[0] = VC.n
                v.tensor_scalar(tp4[:], ss_t[:], car2[:, oc:oc + 1], None,
                                ALU.add)
                v.tensor_tensor(o32[:], tp4[:], rx_t[:], ALU.mult)
                v.tensor_tensor(tp4[:], o32[:], vv_t[:], ALU.add)
                v.tensor_tensor(o32[:], tp4[:], dx_t[:], ALU.mult)
                v.tensor_copy(o_h[:, oc * CH:(oc + 1) * CH], o32[:])
                VC.inc(v.tensor_tensor(o_l[:, oc * CH:(oc + 1) * CH], o32[:],
                                       o_h[:, oc * CH:(oc + 1) * CH],
                                       ALU.subtract))
        vc_o = VC.n

        for q in range(4):
            tt = nt * 4 + q

            @GP
            def _(g, tt=tt):
                g.wait_ge(VC.h, VC.n)
                SP.inc(g.dma_start(out=xrow[:],
                                   in_=x_in[tt * 128:(tt + 1) * 128, :]), 16)
            sp_xr = SP.n

            @TN
            def _(t, q=q, vc_o=vc_o):
                t.wait_ge(VC.h, vc_o)
                for wc in range(2):
                    bank = 4 + (q * 2 + wc) % 4
                    t.wait_ge(VC.h, ybank_free[bank])
                    for oc in range(8):
                        oh = o_h[:, oc * CH + q * 128:oc * CH + (q + 1) * 128]
                        ol = o_l[:, oc * CH + q * 128:oc * CH + (q + 1) * 128]
                        owh = ow_h[:, oc * D + wc * 512:oc * D + (wc + 1) * 512]
                        owl = ow_l[:, oc * D + wc * 512:oc * D + (wc + 1) * 512]
                        t.matmul(pb[bank][:], oh, owh, start=(oc == 0),
                                 stop=False)
                        t.matmul(pb[bank][:], oh, owl, start=False,
                                 stop=False)
                        TE.inc(t.matmul(pb[bank][:], ol, owh, start=False,
                                        stop=(oc == 7)))
            te_y = TE.n

            @VE
            def _(v, q=q, te_y=te_y, sp_xr=sp_xr):
                v.wait_ge(TE.h, te_y)
                v.wait_ge(SP.h, SP.n)
                for wc in range(2):
                    bank = 4 + (q * 2 + wc) % 4
                    VC.inc(v.tensor_tensor(x1t[:, wc * 512:(wc + 1) * 512],
                                           pb[bank][:],
                                           xrow[:, wc * 512:(wc + 1) * 512],
                                           ALU.add))
                    ybank_free[bank] = VC.n
            vc_x1 = VC.n

            @SE
            def _(s, vc_x1=vc_x1):
                s.wait_ge(VC.h, vc_x1)
                s.activation(xn32[:], x1t[:], AF.Square,
                             accum_out=s2_)
                SC.inc(s.activation(xn32[:], x1t[:], AF.Identity,
                                    accum_out=s1_))
            sc_sq2 = SC.n

            @VE
            def _(v, sc_sq2=sc_sq2):
                v.wait_ge(SC.h, sc_sq2)
                v.tensor_scalar(mu_, s1_, 1.0 / D, None, ALU.mult)
                v.tensor_scalar(s2b_, s2_, 1.0 / D, 1e-5, ALU.mult,
                                ALU.add)
                v.tensor_scalar(nmu_, mu_, -1.0, None, ALU.mult)
                v.tensor_tensor(var_, mu_, mu_, ALU.mult)
                v.drain()
                v.tensor_tensor(s1_, s2b_, var_, ALU.subtract)
                v.drain()
                VC.inc(v.reciprocal(var_, s1_))
            vc_st2 = VC.n

            @SE
            def _(s, vc_st2=vc_st2):
                s.wait_ge(VC.h, vc_st2)
                SC.inc(s.activation(isd_, var_, AF.Sqrt))
            sc_ap2 = SC.n

            @VE
            def _(v, sc_ap2=sc_ap2):
                v.wait_ge(SC.h, sc_ap2)
                v.wait_ge(SP.h, SP.n)
                v.tensor_scalar(xc_[:], x1t[:], nmu_, None, ALU.add)
                v.tensor_scalar(xn32[:], xc_[:], isd_, None, ALU.mult)
                v.tensor_copy(xnh_t[:], xn32[:])
                VC.inc(v.tensor_tensor(xnl_t[:], xn32[:], xnh_t[:],
                                       ALU.subtract))
            vc_sp2 = VC.n

            @GP
            def _(g, tt=tt, vc_sp2=vc_sp2):
                g.wait_ge(VC.h, vc_sp2)
                SP.inc(g.dma_start(out=x1_d[tt * 128:(tt + 1) * 128, :],
                                   in_=x1t[:]), 16)
                SP.inc(g.dma_start(out=agx_i[tt * 128:(tt + 1) * 128, 0:D],
                                   in_=xnh_t[:]), 16)
                SP.inc(g.dma_start(out=agx_i[tt * 128:(tt + 1) * 128,
                                             D:2 * D], in_=xnl_t[:]), 16)
        o_used[0] = TE.n
    sp_p4 = SP.n

    @GP
    def _(g):
        g.wait_ge(SP.h, sp_p4)
        CC.inc(g.collective_compute(
            "AllGather", ALU.bypass, replica_groups=[list(range(NC))],
            ins=[agx_i[:, :]], outs=[agx_o[:, :]]))
    cc_agx = CC.n

    ctxP4.close()
    ctxA.close()

    ctxG = ExitStack()
    def sbg(name, shape, dt=F32):
        return ctxG.enter_context(nc.sbuf_tensor(name, shape, dt))

    xg_h = sbg("xg_h", [128, 8 * T], BF16)
    xg_l = sbg("xg_l", [128, 8 * T], BF16)
    lg_t = sbg("lg_t", [128, 128])
    es_t = sbg("es_t", [128, 128])
    m8_t = sbg("m8_t", [128, 16])
    ssum_t = sbg("ssum_t", [128, 16])
    rec_t = sbg("rec_t", [128, 16])
    sm_w = sbg("sm_w", [128, 128])
    gt_w = sbg("gt_w", [128, 1024])
    eq_w = sbg("eq_w", [128, 1024])
    eqm_w = sbg("eqm_w", [128, 1024])
    rks_w = sbg("rks_w", [128, 1024])
    rk_w = sbg("rk_w", [128, 128])
    a1t = sbg("a1t", [128, 8]); a2t = sbg("a2t", [128, 8])
    voff = sbg("voff", [128, 256])
    selo = sbg("selo", [128, 256]); sele = sbg("sele", [128, 256])
    sel5 = sbg("sel5", [128, 256]); cmpq = sbg("cmpq", [128, 256])
    pos_add = sbg("pos_add", [128, 32]); esum = sbg("esum", [128, 32])
    posga = sbg("posga", [128, 32]); valid_w = sbg("valid_w", [128, 32])
    e512s = sbg("e512s", [128, 32]); q_w = sbg("q_w", [128, 32])
    posc_w = sbg("posc_w", [128, 32]); cdif = sbg("cdif", [128, 32])
    w32a = sbg("w32a", [128, 32]); w32b = sbg("w32b", [128, 32])
    s12 = sbg("s12", [128, 16]); rec12 = sbg("rec12", [128, 16])
    g1a = sbg("g1a", [128, 16]); g2a = sbg("g2a", [128, 16])

    @SY
    def _(sy):
        sy.wait_ge(SP.h, sp_p4)
        for fc in range(8):
            LD.inc(sy.dma_start_transpose(
                xg_h[:, fc * T:(fc + 1) * T],
                agx_i[:, fc * 128:(fc + 1) * 128]), 16)
            LD.inc(sy.dma_start_transpose(
                xg_l[:, fc * T:(fc + 1) * T],
                agx_i[:, D + fc * 128:D + (fc + 1) * 128]), 16)
    ld_xg = LD.n

    lg_read = [VC.n]
    for tt in range(16):
        @TN
        def _(t, tt=tt, ld_xg=ld_xg):
            t.wait_ge(LD.h, ld_xg)
            t.wait_ge(VC.h, lg_read[0])
            for fc in range(8):
                xh = xg_h[:, fc * T + tt * 128:fc * T + (tt + 1) * 128]
                xl = xg_l[:, fc * T + tt * 128:fc * T + (tt + 1) * 128]
                t.matmul(pb[0][:, 0:8], xh, wg_h[:, fc, :], start=(fc == 0),
                         stop=False)
                t.matmul(pb[0][:, 0:8], xh, wg_l[:, fc, :], start=False,
                         stop=False)
                TE.inc(t.matmul(pb[0][:, 0:8], xl, wg_h[:, fc, :],
                                start=False, stop=(fc == 7)))
        te_lg = TE.n

        @VE
        def _(v, tt=tt, te_lg=te_lg):
            v.wait_ge(TE.h, te_lg)
            VC.inc(v.tensor_copy(lg_t[:, tt * 8:tt * 8 + 8], pb[0][:, 0:8]))
            lg_read[0] = VC.n

    lg3 = lg_t.ap().rearrange("p (t k) -> p t k", k=8)
    es3 = es_t.ap().rearrange("p (t k) -> p t k", k=8)
    sm3 = sm_w.ap().rearrange("p (t k) -> p t k", k=8)

    @VE
    def _(v):
        v.tensor_reduce(m8_t[:], lg3, AX.X, ALU.max)
        v.drain()
        VC.inc(v.tensor_tensor(
            es3, lg3, m8_t[:, :, None].to_broadcast([128, 16, 8]),
            ALU.subtract))
    vc_es = VC.n

    @SE
    def _(s, vc_es=vc_es):
        s.wait_ge(VC.h, vc_es)
        SC.inc(s.activation(es_t[:], es_t[:], AF.Exp))
    sc_ex = SC.n

    @VE
    def _(v, sc_ex=sc_ex):
        v.wait_ge(SC.h, sc_ex)
        v.tensor_reduce(ssum_t[:], es3, AX.X, ALU.add)
        v.drain()
        v.reciprocal(rec_t[:], ssum_t[:])
        v.drain()
        v.tensor_tensor(sm3, es3,
                        rec_t[:, :, None].to_broadcast([128, 16, 8]),
                        ALU.mult)
        v.drain()
        smA = sm_w.ap().rearrange("p (t k) -> p t k", k=8)[
            :, :, None, :].to_broadcast([128, 16, 8, 8])
        smB = sm_w.ap().rearrange("p (t j) -> p t j", j=8)[
            :, :, :, None].to_broadcast([128, 16, 8, 8])
        gt4 = gt_w.ap().rearrange("p (t j k) -> p t j k", j=8, k=8)
        eq4 = eq_w.ap().rearrange("p (t j k) -> p t j k", j=8, k=8)
        v.tensor_tensor(gt4, smA, smB, ALU.is_gt)
        v.tensor_tensor(eq4, smA, smB, ALU.is_equal)
        v.tensor_tensor(eqm_w[:], eq_w[:], ltm_s[:, :], ALU.mult)
        v.tensor_tensor(rks_w[:], gt_w[:], eqm_w[:], ALU.add)
        v.tensor_reduce(rk_w[:],
                        rks_w.ap().rearrange("p (t j k) -> p t j k", j=8,
                                             k=8).rearrange(
                            "p t j k -> p (t j) k"),
                        AX.X, ALU.add)
        v.tensor_scalar(ohst[:, 0:128], rk_w[:], 0.0, None, ALU.is_equal)
        v.tensor_scalar(ohst[:, 128:256], rk_w[:], 1.0, None, ALU.is_equal)
        v.tensor_tensor(lg_t[:], ohst[:, 0:128], sm_w[:], ALU.mult)
        v.tensor_tensor(es_t[:], ohst[:, 128:256], sm_w[:], ALU.mult)
        v.tensor_reduce(v1st[:],
                        lg_t.ap().rearrange("p (t k) -> p t k", k=8),
                        AX.X, ALU.add)
        VC.inc(v.tensor_reduce(v2st[:],
                               es_t.ap().rearrange("p (t k) -> p t k", k=8),
                               AX.X, ALU.add))
    vc_top = VC.n

    for tt in range(16):
        @TN
        def _(t, tt=tt, vc_top=vc_top):
            t.wait_ge(VC.h, max(vc_top, VC.n))
            oh1s = ohst[:, tt * 8:tt * 8 + 8]
            oh2s = ohst[:, 128 + tt * 8:128 + tt * 8 + 8]
            t.matmul(pb[4][:, 0:8], tri_s[:], oh1s, start=True, stop=False)
            t.matmul(pb[4][:, 0:8], ones1_s[:], cnt_s[0:1, 0:8], start=False,
                     stop=True)
            t.matmul(pb[5][:, 0:8], tri_s[:], oh2s, start=True, stop=False)
            t.matmul(pb[5][:, 0:8], ones1_s[:], cnt_s[0:1, 8:16], start=False,
                     stop=True)
            t.matmul(pb[6][0:1, 0:8], onesc_s[:], oh1s, start=(tt == 0),
                     stop=True, skip_group_check=True)
            TE.inc(t.matmul(pb[6][0:1, 8:16], onesc_s[:], oh2s,
                            start=False, stop=True, skip_group_check=True))
        te_pos = TE.n

        @VE
        def _(v, tt=tt, te_pos=te_pos):
            v.wait_ge(TE.h, te_pos)
            v.tensor_tensor(a1t[:], ohst[:, tt * 8:tt * 8 + 8],
                            pb[4][:, 0:8], ALU.mult)
            v.tensor_tensor(a2t[:], ohst[:, 128 + tt * 8:128 + tt * 8 + 8],
                            pb[5][:, 0:8], ALU.mult)
            v.drain()
            v.reduce_sum(plst[:, tt:tt + 1], a1t[:], axis=AX.X)
            v.reduce_sum(plst[:, 16 + tt:16 + tt + 1], a2t[:], axis=AX.X)
            VC.inc(v.tensor_copy(cnt_s[:], pb[6][0:1, 0:16]))
    vc_cnt = VC.n

    @GP
    def _(g, vc_cnt=vc_cnt):
        g.wait_ge(VC.h, vc_cnt)
        SP.inc(g.dma_start(out=agc_i[:, :],
                           in_=cnt_s.ap().rearrange("p (a b) -> p a b", a=2)),
               16)
        g.wait_ge(SP.h, SP.n)
        CC.inc(g.collective_compute(
            "AllGather", ALU.bypass, replica_groups=[list(range(NC))],
            ins=[agc_i[:, :]], outs=[agc_o[:, :]]))
        g.wait_ge(CC.h, CC.n)
        SP.inc(g.dma_start(out=agc_s[:], in_=agc_o[:, :]), 16)
    sp_agc = SP.n

    @TN
    def _(t):
        t.wait_ge(SP.h, sp_agc)
        t.matmul(pb[7][:, 0:8], offmr0_s[:], agc_s[:], start=True, stop=False)
        TE.inc(t.matmul(pb[7][:, 8:16], offmr1_s[:], agc_s[:], start=False,
                        stop=True))
    te_off = TE.n

    @VE
    def _(v, te_off=te_off):
        v.wait_ge(TE.h, te_off)
        v.tensor_copy(voff.ap().rearrange("p (k t e) -> p k t e", k=2, t=16,
                                          e=8)[:, 0:1, :, :],
                      pb[7][:, 0:8][:, None, :].to_broadcast([128, 16, 8])[
                          :, None, :, :].to_broadcast([128, 1, 16, 8]))
        VC.inc(v.tensor_copy(
            voff.ap().rearrange("p (k t e) -> p k t e", k=2, t=16,
                                e=8)[:, 1:2, :, :],
            pb[7][:, 8:16][:, None, :].to_broadcast([128, 16, 8])[
                :, None, :, :].to_broadcast([128, 1, 16, 8])))
    vc_voff = VC.n

    oh32 = ohst.ap().rearrange("p (c e) -> p c e", e=8)
    @VE
    def _(v, vc_voff=vc_voff):
        v.wait_ge(VC.h, vc_voff)
        v.drain()
        v.tensor_tensor(selo[:], ohst[:], voff[:], ALU.mult)
        v.tensor_tensor(sele[:], ohst[:], evec_s[:], ALU.mult)
        v.tensor_reduce(pos_add[:],
                        selo.ap().rearrange("p (c e) -> p c e", e=8),
                        AX.X, ALU.add)
        v.tensor_reduce(esum[:],
                        sele.ap().rearrange("p (c e) -> p c e", e=8),
                        AX.X, ALU.add)
        v.tensor_tensor(posga[:], pos_add[:], plst[:], ALU.add)
        v.tensor_tensor(sel5[:], ohst[:], e512_s[:], ALU.mult)
        v.tensor_scalar(valid_w[:], posga[:], 4096.0, None, ALU.is_lt)
        v.tensor_reduce(e512s[:],
                        sel5.ap().rearrange("p (c e) -> p c e", e=8),
                        AX.X, ALU.add)
        v.tensor_tensor(w32a[:], esum[:], posga[:], ALU.add)
        v.tensor_tensor(s12[:], v1st[:], v2st[:], ALU.add)
        v.tensor_scalar(w32b[:], w32a[:], float(TRASH), None, ALU.subtract)
        v.reciprocal(rec12[:], s12[:])
        v.tensor_tensor(w32a[:], w32b[:], valid_w[:], ALU.mult)
        v.tensor_tensor(g1a[:], v1st[:], rec12[:], ALU.mult)
        v.tensor_scalar(w32b[:], w32a[:], float(TRASH), None, ALU.add)
        v.tensor_tensor(g2a[:], v2st[:], rec12[:], ALU.mult)
        v.tensor_copy(sct[:], w32b[:])
        v.tensor_scalar(posc_w[:], posga[:], 4095.0, None, ALU.min)
        v.tensor_tensor(g1st[:], g1a[:], valid_w[:, 0:16], ALU.mult)
        v.tensor_tensor(g2st[:], g2a[:], valid_w[:, 16:32], ALU.mult)
        v.tensor_tensor(cmpq.ap().rearrange("p (c e) -> p c e", e=8),
                        posc_w[:, :, None].to_broadcast([128, 32, 8]),
                        thr_s.ap().rearrange("p (c e) -> p c e", e=8),
                        ALU.is_ge)
        v.drain()
        v.tensor_reduce(q_w[:],
                        cmpq.ap().rearrange("p (c e) -> p c e", e=8),
                        AX.X, ALU.add)
        v.drain()
        v.tensor_scalar(w32a[:], q_w[:], 512.0, None, ALU.mult)
        v.tensor_scalar(w32b[:], q_w[:], 4096.0, None, ALU.mult)
        v.tensor_tensor(cdif[:], posc_w[:], w32a[:], ALU.subtract)
        v.tensor_tensor(w32a[:], w32b[:], e512s[:], ALU.add)
        v.drain()
        v.tensor_tensor(w32b[:], w32a[:], cdif[:], ALU.add)
        v.drain()
        v.tensor_tensor(w32a[:], w32b[:], valid_w[:], ALU.mult)
        v.drain()
        VC.inc(v.tensor_copy(ybst[:], w32a[:]))
    vc_route = VC.n

    @GP
    def _(g, vc_route=vc_route):
        g.wait_ge(VC.h, vc_route)
        SP.inc(g.dma_start(
            out=invl_d.rearrange("(p c) one -> p (c one)", p=128),
            in_=zer_s[:]), 16)
        g.wait_ge(SP.h, SP.n)
        for col in range(32):
            SP.inc(g.indirect_dma_start(
                out=invl_d[:, :],
                out_offset=IndirectOffsetOnAxis(sct[:, col:col + 1], 0),
                in_=gtok_s[:, col % 16:col % 16 + 1], in_offset=None), 16)
        g.wait_ge(SP.h, SP.n)
        CC.inc(g.collective_compute(
            "AllReduce", ALU.add, replica_groups=[list(range(NC))],
            ins=[invl_d[:, :]], outs=[invg_d[:, :]]))
        g.wait_ge(CC.h, CC.n)
        for col in range(32):
            SP.inc(g.indirect_dma_start(
                out=invt[:, col:col + 1], out_offset=None,
                in_=invg_d[:, :],
                in_offset=IndirectOffsetOnAxis(einv_s[:, col:col + 1], 0)),
                16)
    sp_inv = SP.n

    ctxG.close()

    ctxF = ExitStack()
    def sbf(name, shape, dt=F32):
        return ctxF.enter_context(nc.sbuf_tensor(name, shape, dt))

    h_h = sbf("h_h", [128, 32 * CH], BF16)
    h_l = sbf("h_l", [128, 32 * CH], BF16)
    xf_h = sbf("xf_h", [128, 8 * CH], BF16)
    xf_l = sbf("xf_l", [128, 8 * CH], BF16)
    w1s_h = [sbf(f"w1s_h{i}", [128, 8, 128], BF16) for i in range(2)]
    w1s_l = [sbf(f"w1s_l{i}", [128, 8, 128], BF16) for i in range(2)]
    w2s_h = [sbf(f"w2s_h{i}", [128, D], BF16) for i in range(2)]
    w2s_l = [sbf(f"w2s_l{i}", [128, D], BF16) for i in range(2)]
    dsb = sbf("dsb", [128, 4 * 2 * D], BF16)
    h32a = sbf("h32a", [128, CH])
    h32b = sbf("h32b", [128, CH])
    yt_ = sbf("yt_", [128, CH])
    invf = sbf("invf", [128, 32])
    invg2 = sbf("invg2", [128, 32])

    @VE
    def _(v):
        v.wait_ge(SP.h, sp_inv)
        v.tensor_copy(invf[:], invt[:])
        v.drain()
        v.tensor_scalar(invg2[:], invf[:], -1.0, 0.0, ALU.add, ALU.max)
        v.drain()
        VC.inc(v.tensor_copy(invt[:], invg2[:]))
    vc_invt = VC.n

    xf_free = [TE.n]
    w1_free = [TE.n, TE.n]
    w2_free = [TE.n, TE.n]
    h_free = [SC.n] * 4
    hsplit = [VC.n, VC.n]
    y2_free = [VC.n] * 8

    for tc in range(TC):
        s0 = tc * CH

        @GP
        def _(g, tc=tc, s0=s0):
            if tc == 0:
                g.wait_ge(VC.h, vc_invt)
                g.wait_ge(CC.h, cc_agx)
            else:
                g.wait_ge(LD.h, LD.n)
            for q in range(4):
                SP.inc(g.indirect_dma_start(
                    out=dsb[:, q * 2 * D:(q + 1) * 2 * D],
                    out_offset=None,
                    in_=agx_o[:, :],
                    in_offset=IndirectOffsetOnAxis(
                        invt[:, tc * 4 + q:tc * 4 + q + 1], 0)), 16)
        sp_disp = SP.n

        @SY
        def _(sy, tc=tc, sp_disp=sp_disp):
            sy.wait_ge(SP.h, sp_disp)
            if tc > 0:
                sy.wait_ge(TE.h, xf_free[0])
            for q in range(4):
                for fc in range(8):
                    LD.inc(sy.dma_start_transpose(
                        xf_h[:, fc * CH + q * 128:fc * CH + (q + 1) * 128],
                        dsb[:, q * 2 * D + fc * 128:
                            q * 2 * D + (fc + 1) * 128]), 16)
                    LD.inc(sy.dma_start_transpose(
                        xf_l[:, fc * CH + q * 128:fc * CH + (q + 1) * 128],
                        dsb[:, q * 2 * D + D + fc * 128:
                            q * 2 * D + D + (fc + 1) * 128]), 16)
        ld_xf = LD.n

        for hc in range(32):
            buf = hc % 2
            bank = hc % 4

            @SE
            def _(sy, tc=tc, hc=hc, buf=buf):
                if tc == 0 and hc == 0:
                    sy.wait_ge(SP.h, sp_w12)
                sy.wait_ge(TE.h, w1_free[buf])
                WL.inc(sy.dma_start(
                    out=w1s_h[buf][:],
                    in_=w1h_d.rearrange("(f p) (c h) -> p f c h", p=128,
                                        h=128)[:, :, hc, :]), 16)
                WL.inc(sy.dma_start(
                    out=w1s_l[buf][:],
                    in_=w1l_d.rearrange("(f p) (c h) -> p f c h", p=128,
                                        h=128)[:, :, hc, :]), 16)
            ld_w1 = WL.n

            @TN
            def _(t, tc=tc, hc=hc, buf=buf, bank=bank, ld_w1=ld_w1,
                  ld_xf=ld_xf):
                t.wait_ge(WL.h, ld_w1)
                if hc == 0:
                    t.wait_ge(LD.h, ld_xf)
                    if tc == 0:
                        t.wait_ge(VC.h, vc_route)
                t.wait_ge(SC.h, h_free[bank])
                for fc in range(8):
                    xh = xf_h[:, fc * CH:(fc + 1) * CH]
                    xl = xf_l[:, fc * CH:(fc + 1) * CH]
                    t.matmul(pb[bank][:], w1s_h[buf][:, fc, :], xh,
                             start=(fc == 0), stop=False)
                    t.matmul(pb[bank][:], w1s_h[buf][:, fc, :], xl,
                             start=False, stop=False)
                    TE.inc(t.matmul(pb[bank][:], w1s_l[buf][:, fc, :], xh,
                                    start=False, stop=(fc == 7)))
                w1_free[buf] = TE.n
                if hc == 31:
                    xf_free[0] = TE.n
            te_h1 = TE.n

            @SE
            def _(s, hc=hc, bank=bank, te_h1=te_h1):
                s.wait_ge(TE.h, te_h1)
                s.wait_ge(VC.h, hsplit[hc % 2])
                h32 = h32a if hc % 2 == 0 else h32b
                SC.inc(s.activation(h32[:], pb[bank][:], AF.Relu,
                                    bias=b1m_s[:, hc:hc + 1]))
                h_free[bank] = SC.n
            sc_h = SC.n

            @VE
            def _(v, hc=hc, sc_h=sc_h):
                v.wait_ge(SC.h, sc_h)
                h32 = h32a if hc % 2 == 0 else h32b
                v.tensor_copy(h_h[:, hc * CH:(hc + 1) * CH], h32[:])
                VC.inc(v.tensor_tensor(h_l[:, hc * CH:(hc + 1) * CH], h32[:],
                                       h_h[:, hc * CH:(hc + 1) * CH],
                                       ALU.subtract))
                hsplit[hc % 2] = VC.n
        vc_hdone = VC.n

        for hc in range(32):
            buf = hc % 2

            @SE
            def _(sy, hc=hc, buf=buf):
                sy.wait_ge(TE.h, w2_free[buf])
                WL.inc(sy.dma_start(
                    out=w2s_h[buf][:],
                    in_=w2h_d[hc * 128:(hc + 1) * 128, :]), 16)
                WL.inc(sy.dma_start(
                    out=w2s_l[buf][:],
                    in_=w2l_d[hc * 128:(hc + 1) * 128, :]), 16)
            ld_w2 = WL.n

            @TN
            def _(t, hc=hc, buf=buf, ld_w2=ld_w2, vc_hdone=vc_hdone):
                t.wait_ge(WL.h, ld_w2)
                if hc == 0:
                    t.wait_ge(VC.h, vc_hdone)
                    for b8 in range(8):
                        t.wait_ge(VC.h, y2_free[b8])
                for q in range(4):
                    hh = h_h[:, hc * CH + q * 128:hc * CH + (q + 1) * 128]
                    hl = h_l[:, hc * CH + q * 128:hc * CH + (q + 1) * 128]
                    for mc in range(2):
                        bank = q * 2 + mc
                        wh = w2s_h[buf][:, mc * 512:(mc + 1) * 512]
                        wl = w2s_l[buf][:, mc * 512:(mc + 1) * 512]
                        t.matmul(pb[bank][:], hh, wh, start=(hc == 0),
                                 stop=False, skip_group_check=True)
                        t.matmul(pb[bank][:], hh, wl, start=False,
                                 stop=False, skip_group_check=True)
                        TE.inc(t.matmul(pb[bank][:], hl, wh, start=False,
                                        stop=(hc == 31),
                                        skip_group_check=True))
                w2_free[buf] = TE.n
        te_y2 = TE.n

        for q in range(4):
            for mc in range(2):
                bank = q * 2 + mc

                @VE
                def _(v, bank=bank, mc=mc, te_y2=te_y2):
                    v.wait_ge(TE.h, te_y2)
                    v.wait_ge(SP.h, SP.n)
                    VC.inc(v.tensor_tensor(
                        yt_[:], pb[bank][:],
                        b2r_s[:, mc * 512:(mc + 1) * 512], ALU.add))
                    y2_free[bank] = VC.n
                vc_y = VC.n

                @GP
                def _(g, q=q, mc=mc, s0=s0, vc_y=vc_y):
                    g.wait_ge(VC.h, vc_y)
                    SP.inc(g.dma_start(
                        out=agy_i[s0 + q * 128:s0 + (q + 1) * 128,
                                  mc * 512:(mc + 1) * 512], in_=yt_[:]), 16)

        @GP
        def _(g, tc=tc, s0=s0):
            g.wait_ge(SP.h, SP.n)
            CC.inc(g.collective_compute(
                "AllGather", ALU.bypass, replica_groups=[list(range(NC))],
                ins=[agy_i[s0:s0 + CH, :]],
                outs=[agy_o[tc * CAP:(tc + 1) * CAP, :]]))
    cc_ally = CC.n

    y1t2 = [sbf(f"y1t{i}", [128, D]) for i in range(2)]
    y2t2 = [sbf(f"y2t{i}", [128, D]) for i in range(2)]
    x1r2 = [sbf(f"x1r{i}", [128, D]) for i in range(2)]
    fin2 = [sbf(f"fin{i}", [128, D]) for i in range(2)]
    fin0s = sbf("fin0s", [128, D])

    vc_fin2 = [0] * 18
    sp_st2 = [0] * 18
    for tt in range(16):
        b = tt % 2
        y1t = y1t2[b]; y2t = y2t2[b]; x1r = x1r2[b]
        fin = fin2[b]; fin0 = fin0s

        @GP
        def _(g, tt=tt, y1t=y1t, y2t=y2t, x1r=x1r):
            if tt == 0:
                g.wait_ge(CC.h, cc_ally)
            if tt > 1:
                g.wait_ge(VC.h, vc_fin2[tt])
            SP.inc(g.indirect_dma_start(
                out=y1t[:, :], out_offset=None, in_=agy_o[:, :],
                in_offset=IndirectOffsetOnAxis(ybst[:, tt:tt + 1], 0)), 16)
            SP.inc(g.indirect_dma_start(
                out=y2t[:, :], out_offset=None, in_=agy_o[:, :],
                in_offset=IndirectOffsetOnAxis(ybst[:, 16 + tt:16 + tt + 1],
                                               0)), 16)
            SP.inc(g.dma_start(out=x1r[:],
                               in_=x1_d[tt * 128:(tt + 1) * 128, :]), 16)
        sp_y = SP.n

        @VE
        def _(v, tt=tt, sp_y=sp_y, y1t=y1t, y2t=y2t, x1r=x1r, fin=fin,
              fin0=fin0):
            v.wait_ge(SP.h, sp_y)
            if tt > 1:
                v.wait_ge(SP.h, sp_st2[tt])
            v.scalar_tensor_tensor(fin0[:], y1t[:], g1st[:, tt:tt + 1],
                                   x1r[:], ALU.mult, ALU.add)
            VC.inc(v.scalar_tensor_tensor(fin[:], y2t[:], g2st[:, tt:tt + 1],
                                          fin0[:], ALU.mult, ALU.add))
        vc_fin2[tt + 2] = VC.n
        vc_fin = VC.n

        @GP
        def _(g, tt=tt, vc_fin=vc_fin, fin=fin):
            g.wait_ge(VC.h, vc_fin)
            SP.inc(g.dma_start(out=y_out[tt * 128:(tt + 1) * 128, :],
                               in_=fin[:]), 16)
        sp_st2[tt + 2] = SP.n

    @GP
    def _(g):
        g.wait_ge(SP.h, SP.n)

    ctxF.close()
    ctx.close()
    return nc


def prep_inputs(inputs):
    x = np.ascontiguousarray(np.asarray(inputs["L0_attn_x"], np.float32)
                             .reshape(NC * T, D))
    k1 = np.asarray(inputs["L0_attn_k1"], np.float32)
    k2 = np.asarray(inputs["L0_attn_k2"], np.float32)
    k3 = np.asarray(inputs["L0_attn_k3"], np.float32)
    a1 = np.asarray(inputs["L0_attn_a1"], np.float32).reshape(H, 64)
    a2 = np.asarray(inputs["L0_attn_a2"], np.float32).reshape(H, 64)
    b1 = np.asarray(inputs["L0_attn_b1"], np.float32).reshape(H, 64)
    b2 = np.asarray(inputs["L0_attn_b2"], np.float32).reshape(H, 64)
    c_ = np.asarray(inputs["L0_attn_c"], np.float32).reshape(H, 64)
    vw = np.asarray(inputs["L0_attn_value_weight"], np.float32)
    ow = np.asarray(inputs["L0_attn_output_weight"], np.float32)
    wg = np.asarray(inputs["L0_ffn_wg"], np.float32)
    w1 = np.asarray(inputs["L0_ffn_w1"], np.float32)
    fb1 = np.asarray(inputs["L0_ffn_b1"], np.float32)
    w2 = np.asarray(inputs["L0_ffn_w2"], np.float32)
    fb2 = np.asarray(inputs["L0_ffn_b2"], np.float32)

    wkt = np.ascontiguousarray(np.concatenate([k1, k2, k3], 0).T)
    wv2 = np.ascontiguousarray(vw.transpose(1, 0, 2).reshape(D, D))
    ow2 = np.ascontiguousarray(ow.transpose(0, 2, 1).reshape(D, D))

    cMm = np.zeros((128, 128), np.float32)
    for sc in range(8):
        for lh in range(2):
            h = sc * 2 + lh
            cMm[lh * 64:(lh + 1) * 64, sc * 16 + h] = c_[h]
    pc = {}
    for nm, arr in (("a1c", a1), ("b1c", b1), ("a2c", a2), ("b2c", b2)):
        pc[nm] = np.ascontiguousarray(arr.reshape(8, 128).T)

    trim = np.ascontiguousarray(np.triu(np.ones((128, 128), np.float32), 1))
    bselm = np.zeros((16, 1024), np.float32)
    for oc in range(8):
        for lh in range(2):
            h = oc * 2 + lh
            bselm[h, oc * 128 + lh * 64:oc * 128 + (lh + 1) * 64] = 1.0
    evec = np.ascontiguousarray(np.broadcast_to(
        np.tile(np.arange(8, dtype=np.float32) * 4096.0, 32), (128, 256)))
    e512 = np.ascontiguousarray(np.broadcast_to(
        np.tile(np.arange(8, dtype=np.float32) * 512.0, 32), (128, 256)))
    thr = np.ascontiguousarray(np.broadcast_to(
        np.tile(np.arange(1, 9, dtype=np.float32) * 512.0, 32), (128, 256)))
    ltm = np.zeros((8, 8), np.float32)
    for j in range(8):
        ltm[j, :j] = 1.0
    ltm = np.ascontiguousarray(np.broadcast_to(
        np.tile(ltm.reshape(64), 16), (128, 1024)))
    ones1 = np.ones((1, 128), np.float32)
    onesc = np.ones((128, 1), np.float32)

    in_maps = []
    for j in range(NC):
        b1mh = np.ascontiguousarray(fb1[j].reshape(32, 128).T)
        b2rh = np.ascontiguousarray(
            np.broadcast_to(fb2[j].reshape(1, D), (128, D)))
        b = j // 4
        nloc = np.ascontiguousarray(np.broadcast_to(
            ((np.arange(T) + (j % 4) * T).astype(np.float32) / 8191.0)
            .reshape(1, T), (128, T)))
        seqm = np.zeros((8, 1), np.float32)
        seqm[4 * b:j, 0] = 1.0
        offm = np.zeros((16, 2), np.float32)
        offm[0:2 * j:2, 0] = 1.0
        offm[0::2, 1] = 1.0
        offm[1:2 * j:2, 1] = 1.0
        offmr0 = np.ascontiguousarray(np.broadcast_to(offm[:, 0:1], (16, 128)))
        offmr1 = np.ascontiguousarray(np.broadcast_to(offm[:, 1:2], (16, 128)))
        gt = (np.arange(T, dtype=np.int64) + j * T + 1).astype(np.int32)
        gtok = np.ascontiguousarray(gt.reshape(16, 128).T)
        einv = np.ascontiguousarray(
            (j * 4096 + np.arange(32, dtype=np.int32)[None, :] * 128
             + np.arange(128, dtype=np.int32)[:, None]).astype(np.int32))
        in_maps.append({
            "x": x[j * T:(j + 1) * T],
            "wkt": wkt, "wv2": wv2, "ow2": ow2, "wgm": wg,
            "w1m": w1[j], "w2m": w2[j], "b1m": b1mh, "b2row": b2rh,
            "cM": cMm, "a1c": pc["a1c"], "b1c": pc["b1c"],
            "a2c": pc["a2c"], "b2c": pc["b2c"],
            "tri": trim, "bsel": bselm,
            "nvec": nloc, "seqm": seqm,
            "offm": offm, "offmr0": offmr0, "offmr1": offmr1,
            "gtok": gtok, "evec": evec, "e512": e512,
            "thr": thr, "ltm": ltm, "ones1": ones1, "onesc": onesc,
            "einv": einv,
        })
    return in_maps


def kernel(**inputs):
    nc = build_program()
    in_maps = prep_inputs(inputs)
    res = run_bass_kernel_spmd(nc, in_maps, core_ids=list(range(NC)))
    out = np.concatenate([res.results[j]["y_out"] for j in range(NC)], axis=0)
    return out.reshape(2, 8192, 1024)


# version 09
# speedup vs baseline: 1.0297x; 1.0297x over previous
from contextlib import ExitStack

import numpy as np

import concourse.bass as bass
import concourse.mybir as mybir
from concourse.bass_utils import run_bass_kernel_spmd
from concourse.bass_core_rust import IndirectOffsetOnAxis

F32 = mybir.dt.float32
BF16 = mybir.dt.bfloat16
I32 = mybir.dt.int32
ALU = mybir.AluOpType
AF = mybir.ActivationFunctionType
AX = mybir.AxisListType

NC = 8
T = 2048
D = 1024
H = 16
HD = 4096
CAP = 4096
NT = 4
CH = 512
TC = 8
TRASH = 32768


class Sem:
    def __init__(self, nc, ctx, name):
        self.h = ctx.enter_context(nc.semaphore(name))
        self.n = 0

    def inc(self, inst, k=1):
        inst.then_inc(self.h, k)
        self.n += k
        return self.n


def build_program():
    nc = bass.Bass("TRN2", target_bir_lowering=False,
                   detect_race_conditions=False)
    ctx = ExitStack()

    def din(name, shape, dt=F32):
        return nc.dram_tensor(name, shape, dt, kind="ExternalInput").ap()

    x_in = din("x", [T, D])
    wkt = din("wkt", [D, 48])
    wv2 = din("wv2", [D, D])
    ow2 = din("ow2", [D, D])
    wgm = din("wgm", [D, 8])
    w1m = din("w1m", [D, HD])
    w2m = din("w2m", [HD, D])
    b1m_i = din("b1m", [128, 32])
    b2r_i = din("b2row", [128, D])
    cM_i = din("cM", [128, 128])
    a1_i = din("a1c", [128, 8])
    b1_i = din("b1c", [128, 8])
    a2_i = din("a2c", [128, 8])
    b2_i = din("b2c", [128, 8])
    tri_i = din("tri", [128, 128])
    bsel_i = din("bsel", [16, 1024])
    nvec_i = din("nvec", [128, T])
    seqm_i = din("seqm", [8, 1])
    offm_i = din("offm", [16, 2])
    offr0_i = din("offmr0", [16, 128])
    offr1_i = din("offmr1", [16, 128])
    gtok_i = din("gtok", [128, 16], I32)
    evec_i = din("evec", [128, 256])
    e512_i = din("e512", [128, 256])
    thr_i = din("thr", [128, 256])
    ltm_i = din("ltm", [128, 1024])
    ones1_i = din("ones1", [1, 128])
    onesc_i = din("onesc", [128, 1])
    einv_i = din("einv", [128, 32], I32)

    y_out = nc.dram_tensor("y_out", [T, D], F32, kind="ExternalOutput").ap()

    def dint(name, shape, dt=F32):
        return nc.dram_tensor(name, shape, dt).ap()

    xnh_d = dint("xnh_d", [T, D], BF16)
    xnl_d = dint("xnl_d", [T, D], BF16)
    s_d = dint("s_d", [8, 128, T])
    v_d = dint("v_d", [8, 128, T])
    x1_d = dint("x1_d", [T, D])
    p12_d = dint("p12_d", [16, 2 * T])
    agcar_i = dint("agcar_i", [1, 1040])
    agcar_o = dint("agcar_o", [8, 1040])
    agx_i = dint("agx_i", [T, 2 * D], BF16)
    agx_o = dint("agx_o", [NC * T, 2 * D], BF16)
    agc_i = dint("agc_i", [2, 8])
    agc_o = dint("agc_o", [16, 8])
    invl_d = dint("invl_d", [33024, 1], I32)
    invg_d = dint("invg_d", [33024, 1], I32)
    agy_i = dint("agy_i", [CAP, D])
    agy_o = dint("agy_o", [NC * CAP, D])
    w1h_d = dint("w1h_d", [D, HD], BF16)
    w1l_d = dint("w1l_d", [D, HD], BF16)
    w2h_d = dint("w2h_d", [HD, D], BF16)
    w2l_d = dint("w2l_d", [HD, D], BF16)

    def sb(name, shape, dt=F32):
        return ctx.enter_context(nc.sbuf_tensor(name, shape, dt))

    wk_h = sb("wk_h", [128, 8, 48], BF16)
    wk_l = sb("wk_l", [128, 8, 48], BF16)
    wg_h = sb("wg_h", [128, 8, 8], BF16)
    wg_l = sb("wg_l", [128, 8, 8], BF16)
    cM_s = sb("cM_s", [128, 128])
    a1s = sb("a1s", [128, 8]); b1s = sb("b1s", [128, 8])
    a2s = sb("a2s", [128, 8]); b2s = sb("b2s", [128, 8])
    tri_s = sb("tri_s", [128, 128])
    bsel_s = sb("bsel_s", [16, 1024])
    nvec_s = sb("nvec_s", [128, T])
    seqm_s = sb("seqm_s", [8, 1])
    offm_s = sb("offm_s", [16, 2])
    gtok_s = sb("gtok_s", [128, 16], I32)
    evec_s = sb("evec_s", [128, 256])
    e512_s = sb("e512_s", [128, 256])
    thr_s = sb("thr_s", [128, 256])
    ltm_s = sb("ltm_s", [128, 1024])
    ones1_s = sb("ones1_s", [1, 128])
    onesc_s = sb("onesc_s", [128, 1])
    b1m_s = sb("b1m_s", [128, 32])
    b2r_s = sb("b2r_s", [128, D])
    einv_s = sb("einv_s", [128, 32], I32)
    scarL = [sb(f"scar{i}", [128, 1]) for i in range(8)]
    Zcar = sb("Zcar", [16, 1])
    car2 = sb("car2", [128, 8])
    zc2 = sb("zc2", [16, 1])
    agc_s = sb("agc_s", [16, 8])
    offmr0_s = sb("offmr0_s", [16, 128])
    offmr1_s = sb("offmr1_s", [16, 128])
    ohst = sb("ohst", [128, 2 * 16 * 8])
    plst = sb("plst", [128, 32])
    g1st = sb("g1st", [128, 16]); g2st = sb("g2st", [128, 16])
    v1st = sb("v1st", [128, 16]); v2st = sb("v2st", [128, 16])
    ybst = sb("ybst", [128, 32], I32)
    sct = sb("sct", [128, 32], I32)
    zer_s = sb("zer_s", [128, 258], I32)
    invt = sb("invt", [128, 32], I32)
    cnt_s = sb("cnt_s", [1, 16])

    pb = [ctx.enter_context(nc.psum_tensor(f"pb{i}", [128, 512], F32))
          for i in range(8)]

    LD = Sem(nc, ctx, "LD")
    SP = Sem(nc, ctx, "SP")
    VC = Sem(nc, ctx, "VC")
    SC = Sem(nc, ctx, "SC")
    TE = Sem(nc, ctx, "TE")
    CC = Sem(nc, ctx, "CC")
    WL = Sem(nc, ctx, "WL")

    block = ctx.enter_context(nc.Block())

    def SY(f): block.sync(f)
    def GP(f): block.gpsimd(f)
    def VE(f): block.vector(f)
    def SE(f): block.scalar(f)
    def TN(f): block.tensor(f)

    ctxA = ExitStack()
    def sba(name, shape, dt=F32):
        return ctxA.enter_context(nc.sbuf_tensor(name, shape, dt))

    wv_h = sba("wv_h", [128, 8 * D], BF16)
    wv_l = sba("wv_l", [128, 8 * D], BF16)
    ow_h = sba("ow_h", [128, 8 * D], BF16)
    ow_l = sba("ow_l", [128, 8 * D], BF16)
    Z_s = sba("Z_s", [16, T])
    r_s = sba("r_s", [16, T])
    dinv_s = sba("dinv_s", [16, T])
    r_s2 = sba("r_s2", [16, T])
    ss_t = sba("ss_t", [128, CH])
    vv_t = sba("vv_t", [128, CH])
    xt2 = [sba(f"xt{i}", [128, D]) for i in range(2)]
    xc2 = [sba(f"xc{i}", [128, D]) for i in range(2)]
    xn322 = [sba(f"xn32{i}", [128, D]) for i in range(2)]
    xnh2 = [sba(f"xnh{i}", [128, D], BF16) for i in range(2)]
    xnl2 = [sba(f"xnl{i}", [128, D], BF16) for i in range(2)]
    sm12 = [sba(f"sm1{i}", [128, 8]) for i in range(2)]

    ctxW = ExitStack()
    def sbw(name, shape, dt=F32):
        return ctxW.enter_context(nc.sbuf_tensor(name, shape, dt))

    wtmp2 = [sbw(f"wtmp{i}", [128, 2048]) for i in range(2)]
    wth2 = [sbw(f"wth{i}", [128, 2048], BF16) for i in range(2)]
    wtl2 = [sbw(f"wtl{i}", [128, 2048], BF16) for i in range(2)]
    wkt32 = sbw("wkt32", [128, 8, 48])
    wg32 = sbw("wg32", [128, 8, 8])

    @SY
    def _(sy):
        for t, d_ in ((cM_s, cM_i), (tri_s, tri_i), (bsel_s, bsel_i),
                      (nvec_s, nvec_i), (seqm_s, seqm_i), (offm_s, offm_i),
                      (gtok_s, gtok_i), (evec_s, evec_i), (e512_s, e512_i),
                      (thr_s, thr_i), (ltm_s, ltm_i), (ones1_s, ones1_i),
                      (onesc_s, onesc_i), (b1m_s, b1m_i), (b2r_s, b2r_i),
                      (a1s, a1_i), (b1s, b1_i), (a2s, a2_i), (b2s, b2_i),
                      (einv_s, einv_i), (offmr0_s, offr0_i),
                      (offmr1_s, offr1_i)):
            LD.inc(sy.dma_start(out=t[:], in_=d_[:, :]), 16)
        LD.inc(sy.dma_start(
            out=wkt32[:], in_=wkt.rearrange("(f p) c -> p f c", p=128)), 16)
        LD.inc(sy.dma_start(
            out=wg32[:], in_=wgm.rearrange("(f p) c -> p f c", p=128)), 16)
    ld_consts = LD.n

    @VE
    def _(v):
        v.wait_ge(LD.h, ld_consts)
        for i in range(8):
            v.memset(scarL[i][:], 0.0)
        v.memset(Zcar[:], 0.0)
        v.memset(cnt_s[:], 0.0)
        v.memset(zer_s[:], 0)
        v.tensor_copy(wk_h[:], wkt32[:])
        v.tensor_copy(wg_h[:], wg32[:])
        v.tensor_tensor(wk_l[:], wkt32[:], wk_h[:], ALU.subtract)
        VC.inc(v.tensor_tensor(wg_l[:], wg32[:], wg_h[:], ALU.subtract))
    vc_consts = VC.n

    vfree = [VC.n, VC.n]
    for ji, (wsrc, dsth, dstl) in enumerate(
            [(wv2, wv_h, wv_l), (ow2, ow_h, ow_l)]):
        for fc in range(8):
            b = (ji * 8 + fc) % 2
            @SY
            def _(sy, fc=fc, b=b, wsrc=wsrc):
                sy.wait_ge(VC.h, vfree[b])
                LD.inc(sy.dma_start(out=wtmp2[b][:, 0:1024],
                                    in_=wsrc[fc * 128:(fc + 1) * 128, :]), 16)
            ldc = LD.n

            @VE
            def _(v, fc=fc, b=b, ldc=ldc, dsth=dsth, dstl=dstl):
                v.wait_ge(LD.h, ldc)
                v.tensor_copy(dsth[:, fc * D:(fc + 1) * D],
                              wtmp2[b][:, 0:1024])
                VC.inc(v.tensor_tensor(dstl[:, fc * D:(fc + 1) * D],
                                       wtmp2[b][:, 0:1024],
                                       dsth[:, fc * D:(fc + 1) * D],
                                       ALU.subtract))
                vfree[b] = VC.n



    def emit_w12_prep():
        vfree2 = [VC.n, VC.n]
        spfree = [SP.n, SP.n]
        jobs = []
        for fc in range(8):
            for hh in range(2):
                jobs.append(("w1", fc, hh))
        for hc2 in range(16):
            jobs.append(("w2", hc2, 0))
        for ji, job in enumerate(jobs):
            b = ji % 2
            kind, i0, i1 = job

            @SE
            def _(sy, kind=kind, i0=i0, i1=i1, b=b, ji=ji):
                sy.wait_ge(VC.h, vfree2[b])
                if kind == "w1":
                    WL.inc(sy.dma_start(
                        out=wtmp2[b][:],
                        in_=w1m[i0 * 128:(i0 + 1) * 128,
                                i1 * 2048:(i1 + 1) * 2048]), 16)
                else:
                    WL.inc(sy.dma_start(
                        out=wtmp2[b][:],
                        in_=w2m.rearrange("(c p) m -> p c m", p=128)[
                            :, i0 * 2:(i0 + 1) * 2, :]), 16)
            ldc = WL.n

            @VE
            def _(v, b=b, ldc=ldc):
                v.wait_ge(WL.h, ldc)
                v.wait_ge(SP.h, spfree[b])
                v.tensor_copy(wth2[b][:], wtmp2[b][:])
                v.drain()
                VC.inc(v.tensor_tensor(wtl2[b][:], wtmp2[b][:], wth2[b][:],
                                       ALU.subtract))
                vfree2[b] = VC.n
            vcc = VC.n

            @GP
            def _(g, kind=kind, i0=i0, i1=i1, b=b, vcc=vcc):
                g.wait_ge(VC.h, vcc)
                if kind == "w1":
                    SP.inc(g.dma_start(
                        out=w1h_d[i0 * 128:(i0 + 1) * 128,
                                  i1 * 2048:(i1 + 1) * 2048],
                        in_=wth2[b][:]), 16)
                    SP.inc(g.dma_start(
                        out=w1l_d[i0 * 128:(i0 + 1) * 128,
                                  i1 * 2048:(i1 + 1) * 2048],
                        in_=wtl2[b][:]), 16)
                else:
                    SP.inc(g.dma_start(
                        out=w2h_d.rearrange("(c p) m -> p c m", p=128)[
                            :, i0 * 2:(i0 + 1) * 2, :], in_=wth2[b][:]), 16)
                    SP.inc(g.dma_start(
                        out=w2l_d.rearrange("(c p) m -> p c m", p=128)[
                            :, i0 * 2:(i0 + 1) * 2, :], in_=wtl2[b][:]), 16)
                spfree[b] = SP.n


    def emit_p12():
        p12e = ctxW.enter_context(nc.sbuf_tensor("p12e", [16, CH], F32))
        sfree = [TE.n, TE.n]
        for k in range(2):
            aa, bb = (a1s, b1s) if k == 0 else (a2s, b2s)
            for nt in range(NT):
                t0 = nt * CH
                for sc8 in range(8):
                    b = sc8 % 2

                    @VE
                    def _(v, t0=t0, sc8=sc8, b=b, aa=aa, bb=bb):
                        v.wait_ge(LD.h, ld_consts)
                        v.wait_ge(TE.h, sfree[b])
                        VC.inc(v.tensor_scalar(
                            wtmp2[b][:, 0:CH], nvec_s[:, t0:t0 + CH],
                            aa[:, sc8:sc8 + 1], bb[:, sc8:sc8 + 1],
                            ALU.mult, ALU.add))
                    vca = VC.n

                    @SE
                    def _(s, b=b, vca=vca):
                        s.wait_ge(VC.h, vca)
                        SC.inc(s.activation(wtmp2[b][:, 0:CH],
                                            wtmp2[b][:, 0:CH], AF.Sin))
                    @TN
                    def _(t, sc8=sc8, b=b, scn=SC.n, k=k, nt=nt):
                        t.wait_ge(SC.h, scn)
                        if sc8 == 0:
                            t.wait_ge(VC.h, p12free[0])
                        TE.inc(t.matmul(pb[7][0:16, :],
                                        cM_s[:, sc8 * 16:(sc8 + 1) * 16],
                                        wtmp2[b][:, 0:CH], start=(sc8 == 0),
                                        stop=(sc8 == 7)))
                    sfree[b] = TE.n

                @VE
                def _(v, ten=TE.n):
                    v.wait_ge(TE.h, ten)
                    VC.inc(v.tensor_copy(p12e[:], pb[7][0:16, :]))
                p12free[0] = VC.n

                @GP
                def _(g, k=k, t0=t0, vcn=VC.n):
                    g.wait_ge(VC.h, vcn)
                    SP.inc(g.dma_start(
                        out=p12_d[:, k * T + t0:k * T + t0 + CH],
                        in_=p12e[:]), 16)

    p12free = [VC.n]
    emit_w12_prep()
    emit_p12()
    sp_w12 = SP.n
    ctxW.close()

    vc_apply = [0] * 18
    sp_p1_tiles = [0] * 16
    for tt in range(16):
        b = tt % 2
        xt = xt2[b]; xc_ = xc2[b]; xn32 = xn322[b]
        xnh_t = xnh2[b]; xnl_t = xnl2[b]; sm = sm12[b]
        s1_ = sm[:, 0:1]; s2_ = sm[:, 1:2]; mu_ = sm[:, 2:3]
        var_ = sm[:, 3:4]; isd_ = sm[:, 4:5]; s2b_ = sm[:, 5:6]
        nmu_ = sm[:, 6:7]

        @GP
        def _(g, tt=tt, xt=xt):
            if tt > 1:
                g.wait_ge(VC.h, vc_apply[tt])
            SP.inc(g.dma_start(out=xt[:],
                               in_=x_in[tt * 128:(tt + 1) * 128, :]), 16)
        sp_x = SP.n

        @SE
        def _(s, sp_x=sp_x, xt=xt, xn32=xn32, s1_=s1_, s2_=s2_):
            s.wait_ge(SP.h, sp_x)
            s.activation(xn32[:], xt[:], AF.Square, accum_out=s2_)
            SC.inc(s.activation(xn32[:], xt[:], AF.Identity, accum_out=s1_))
        sc_sq = SC.n

        @VE
        def _(v, sc_sq=sc_sq, s1_=s1_, s2_=s2_, mu_=mu_, var_=var_,
              s2b_=s2b_, nmu_=nmu_):
            v.wait_ge(SC.h, sc_sq)
            v.tensor_scalar(mu_, s1_, 1.0 / D, None, ALU.mult)
            v.tensor_scalar(s2b_, s2_, 1.0 / D, 1e-5, ALU.mult, ALU.add)
            v.tensor_scalar(nmu_, mu_, -1.0, None, ALU.mult)
            v.tensor_tensor(var_, mu_, mu_, ALU.mult)
            v.drain()
            v.tensor_tensor(s1_, s2b_, var_, ALU.subtract)
            v.drain()
            VC.inc(v.reciprocal(var_, s1_))
        vc_st = VC.n

        @SE
        def _(s, vc_st=vc_st, isd_=isd_, var_=var_):
            s.wait_ge(VC.h, vc_st)
            SC.inc(s.activation(isd_, var_, AF.Sqrt))
        sc_ap = SC.n

        @VE
        def _(v, tt=tt, sc_ap=sc_ap, xt=xt, xc_=xc_, xn32=xn32,
              xnh_t=xnh_t, xnl_t=xnl_t, nmu_=nmu_, isd_=isd_):
            v.wait_ge(SC.h, sc_ap)
            if tt > 1:
                v.wait_ge(SP.h, sp_p1_tiles[tt - 2])
            v.tensor_scalar(xc_[:], xt[:], nmu_, None, ALU.add)
            v.tensor_scalar(xn32[:], xc_[:], isd_, None, ALU.mult)
            v.tensor_copy(xnh_t[:], xn32[:])
            VC.inc(v.tensor_tensor(xnl_t[:], xn32[:], xnh_t[:],
                                   ALU.subtract))
        vc_apply[tt + 2] = VC.n

        @GP
        def _(g, tt=tt, xnh_t=xnh_t, xnl_t=xnl_t, vcc=VC.n):
            g.wait_ge(VC.h, vcc)
            SP.inc(g.dma_start(out=xnh_d[tt * 128:(tt + 1) * 128, :],
                               in_=xnh_t[:]), 16)
            SP.inc(g.dma_start(out=xnl_d[tt * 128:(tt + 1) * 128, :],
                               in_=xnl_t[:]), 16)
        sp_p1_tiles[tt] = SP.n

    xt = xt2[0]; xc_ = xc2[0]; xn32 = xn322[0]
    xnh_t = xnh2[0]; xnl_t = xnl2[0]
    sm = sm12[0]
    s1_ = sm[:, 0:1]; s2_ = sm[:, 1:2]; mu_ = sm[:, 2:3]
    var_ = sm[:, 3:4]; isd_ = sm[:, 4:5]; s2b_ = sm[:, 5:6]
    nmu_ = sm[:, 6:7]

    ctxP2 = ExitStack()
    def sb2(name, shape, dt=F32):
        return ctxP2.enter_context(nc.sbuf_tensor(name, shape, dt))

    xfm_h = sb2("xfm_h", [128, 8 * CH], BF16)
    xfm_l = sb2("xfm_l", [128, 8 * CH], BF16)
    p1_t = sb2("p1_t", [16, CH]); p2raw = sb2("p2raw", [16, CH])
    c_t = sb2("c_t", [16, CH]); d_t = sb2("d_t", [16, CH])
    e_t = sb2("e_t", [16, CH]); p2_t = sb2("p2_t", [16, CH])
    rt1 = sb2("rt1", [16, CH])
    cx_t = sb2("cx_t", [128, CH])
    cv_t = sb2("cv_t", [128, CH])

    cx_evac = [VC.n]
    vbank_free = {4: VC.n, 5: VC.n, 6: VC.n, 7: VC.n}
    sc_scar = [SC.n]
    vc_p12used = [VC.n]

    for nt in range(NT):
        t0 = nt * CH

        @SY
        def _(sy, nt=nt, t0=t0):
            sy.wait_ge(SP.h, sp_p1_tiles[4 * nt + 3])
            if nt > 0:
                sy.wait_ge(TE.h, TE.n)
            for fc in range(8):
                LD.inc(sy.dma_start_transpose(
                    xfm_h[:, fc * CH:(fc + 1) * CH],
                    xnh_d[t0:t0 + CH, fc * 128:(fc + 1) * 128]), 16)
                LD.inc(sy.dma_start_transpose(
                    xfm_l[:, fc * CH:(fc + 1) * CH],
                    xnl_d[t0:t0 + CH, fc * 128:(fc + 1) * 128]), 16)
        ld_xfm = LD.n

        @GP
        def _(g, t0=t0, nt=nt):
            if nt > 0:
                g.wait_ge(VC.h, vc_p12used[0])
            SP.inc(g.dma_start(out=p1_t[:], in_=p12_d[:, t0:t0 + CH]), 16)
            SP.inc(g.dma_start(out=p2raw[:],
                               in_=p12_d[:, T + t0:T + t0 + CH]), 16)
        sp_p12 = SP.n

        @TN
        def _(t, nt=nt, ld_xfm=ld_xfm):
            t.wait_ge(LD.h, ld_xfm)
            t.wait_ge(VC.h, VC.n)
            for kind in range(3):
                for fc in range(8):
                    xh = xfm_h[:, fc * CH:(fc + 1) * CH]
                    xl = xfm_l[:, fc * CH:(fc + 1) * CH]
                    kh = wk_h[:, fc, kind * 16:(kind + 1) * 16]
                    kl = wk_l[:, fc, kind * 16:(kind + 1) * 16]
                    t.matmul(pb[kind][0:16, :], kh, xh,
                             start=(fc == 0), stop=False)
                    t.matmul(pb[kind][0:16, :], kh, xl, start=False,
                             stop=False)
                    TE.inc(t.matmul(pb[kind][0:16, :], kl, xh,
                                    start=False, stop=(fc == 7)))
        te_p2m = TE.n

        @VE
        def _(v, te_p2m=te_p2m, sp_p12=sp_p12):
            v.wait_ge(TE.h, te_p2m)
            v.wait_ge(SP.h, sp_p12)
            v.tensor_tensor(rt1[:], pb[0][0:16, :], p1_t[:], ALU.add)
            v.tensor_scalar(d_t[:], pb[1][0:16, :], 20.0, -20.0, ALU.min,
                            ALU.max)
            v.tensor_scalar(e_t[:], pb[2][0:16, :], 20.0, -20.0, ALU.min,
                            ALU.max)
            v.tensor_scalar(p2_t[:], p2raw[:], 20.0, -20.0, ALU.min,
                            ALU.max)
            VC.inc(v.tensor_scalar(c_t[:], rt1[:], 20.0, -20.0, ALU.min,
                                   ALU.max))
        vc_clip = VC.n
        vc_p12used[0] = VC.n

        @SE
        def _(s, vc_clip=vc_clip):
            s.wait_ge(VC.h, vc_clip)
            s.activation(c_t[:], c_t[:], AF.Exp)
            s.activation(d_t[:], d_t[:], AF.Exp)
            s.activation(e_t[:], e_t[:], AF.Exp)
            SC.inc(s.activation(p2_t[:], p2_t[:], AF.Exp))
        sc_exp = SC.n

        @VE
        def _(v, t0=t0, sc_exp=sc_exp):
            v.wait_ge(SC.h, sc_exp)
            v.reciprocal(rt1[:], d_t[:])
            v.tensor_tensor(ss_t[0:16, :], p2_t[:], e_t[:], ALU.mult)
            v.tensor_tensor(r_s[:, t0:t0 + CH], ss_t[0:16, :], rt1[:],
                            ALU.mult)
            VC.inc(v.tensor_tensor_scan(Z_s[:, t0:t0 + CH], onesc_s[0:16, 0:1].to_broadcast([16, CH]),
                                        c_t[:], Zcar[:, 0:1], ALU.mult,
                                        ALU.add))
        vc_zr = VC.n

        @SE
        def _(s, t0=t0, vc_zr=vc_zr):
            s.wait_ge(VC.h, vc_zr)
            SC.inc(s.copy(Zcar[:], Z_s[:, t0 + CH - 1:t0 + CH]))
        sc_zcar = SC.n

        for oc in range(8):
            bank = 4 + (oc % 4)

            @TN
            def _(t, nt=nt, oc=oc, bank=bank, vc_zr=vc_zr):
                t.wait_ge(VC.h, vc_zr)
                t.wait_ge(VC.h, cx_evac[0])
                t.matmul(pb[0][:], bsel_s[:, oc * 128:(oc + 1) * 128],
                         c_t[:], start=True, stop=True)
                t.wait_ge(VC.h, vbank_free[bank])
                for fc in range(8):
                    xh = xfm_h[:, fc * CH:(fc + 1) * CH]
                    xl = xfm_l[:, fc * CH:(fc + 1) * CH]
                    wvh = wv_h[:, fc * D + oc * 128:fc * D + (oc + 1) * 128]
                    wvl = wv_l[:, fc * D + oc * 128:fc * D + (oc + 1) * 128]
                    t.matmul(pb[bank][:], wvh, xh, start=(fc == 0),
                             stop=False)
                    t.matmul(pb[bank][:], wvh, xl, start=False, stop=False)
                    TE.inc(t.matmul(pb[bank][:], wvl, xh, start=False,
                                    stop=(fc == 7)))
            te_v = TE.n

            @VE
            def _(v, nt=nt, oc=oc, bank=bank, te_v=te_v):
                v.wait_ge(TE.h, te_v)
                v.wait_ge(SP.h, SP.n)
                v.wait_ge(SC.h, sc_scar[0])
                VC.inc(v.tensor_copy(cx_t[:], pb[0][:]))
                cx_evac[0] = VC.n
                v.tensor_tensor(cv_t[:], pb[bank][:], cx_t[:], ALU.mult)
                v.tensor_copy(vv_t[:], pb[bank][:])
                VC.inc(v.tensor_tensor_scan(ss_t[:], onesc_s[:, 0:1].to_broadcast([128, CH]), cv_t[:],
                                            scarL[oc][:, 0:1], ALU.mult,
                                            ALU.add))
                vbank_free[bank] = VC.n
            vc_soc = VC.n

            @SE
            def _(s, oc=oc, vc_soc=vc_soc):
                s.wait_ge(VC.h, vc_soc)
                SC.inc(s.copy(scarL[oc][:], ss_t[:, CH - 1:CH]))
            sc_scar[0] = SC.n

            @GP
            def _(g, nt=nt, oc=oc, t0=t0, vc_soc=vc_soc):
                g.wait_ge(VC.h, vc_soc)
                SP.inc(g.dma_start(out=s_d[oc, :, t0:t0 + CH], in_=ss_t[:]),
                       16)
                SP.inc(g.dma_start(out=v_d[oc, :, t0:t0 + CH], in_=vv_t[:]),
                       16)
    ctxP2.close()

    ctxP4 = ExitStack()
    def sb4(name, shape, dt=F32):
        return ctxP4.enter_context(nc.sbuf_tensor(name, shape, dt))

    o_h = sb4("o_h", [128, 8 * CH], BF16)
    o_l = sb4("o_l", [128, 8 * CH], BF16)
    rx_t = sb4("rx_t", [128, CH])
    dx_t = sb4("dx_t", [128, CH])
    o32 = sb4("o32", [128, CH])
    tp4 = sb4("tp4", [128, CH])
    x1t = sb4("x1t", [128, D])
    xrow = sb4("xrow", [128, D])
    agcar_s = sb4("agcar_s", [8, 1040])

    @GP
    def _(g):
        g.wait_ge(VC.h, VC.n)
        g.wait_ge(SC.h, SC.n)
        for oc in range(8):
            SP.inc(g.dma_start(out=agcar_i[0:1, oc * 128:(oc + 1) * 128],
                               in_=scarL[oc][:, 0:1],
                               allow_slow_non_contiguous=True), 16)
        SP.inc(g.dma_start(out=agcar_i[0:1, 1024:1040], in_=Zcar[:, 0:1],
                           allow_slow_non_contiguous=True), 16)
        g.wait_ge(SP.h, SP.n)
        CC.inc(g.collective_compute(
            "AllGather", ALU.bypass, replica_groups=[list(range(NC))],
            ins=[agcar_i[:, :]], outs=[agcar_o[:, :]]))
        g.wait_ge(CC.h, CC.n)
        SP.inc(g.dma_start(out=agcar_s[:], in_=agcar_o[:, :]), 16)
    sp_p3 = SP.n

    @TN
    def _(t):
        t.wait_ge(SP.h, sp_p3)
        for oc in range(8):
            t.matmul(pb[3][:, oc:oc + 1],
                     agcar_s[:, oc * 128:(oc + 1) * 128], seqm_s[:, :],
                     start=(oc == 0), stop=(oc == 7))
        TE.inc(t.matmul(pb[2][0:16, 0:1], agcar_s[:, 1024:1040],
                        seqm_s[:, :], start=True, stop=True))
    te_p3 = TE.n

    @VE
    def _(v, te_p3=te_p3):
        v.wait_ge(TE.h, te_p3)
        v.tensor_copy(car2[:], pb[3][:, 0:8])
        v.tensor_copy(zc2[:], pb[2][0:16, 0:1])
        v.drain()
        v.tensor_scalar(r_s2[:], Z_s[:], zc2[:, 0:1], None, ALU.add)
        v.tensor_tensor(dinv_s[:], r_s2[:], r_s[:], ALU.mult)
        v.tensor_scalar(r_s2[:], dinv_s[:], 1.0, None, ALU.add)
        VC.inc(v.reciprocal(dinv_s[:], r_s2[:]))
    vc_p3 = VC.n


    rx_evac = [VC.n]
    o_used = [TE.n]
    ybank_free = {4: VC.n, 5: VC.n, 6: VC.n, 7: VC.n}

    for nt in range(NT):
        t0 = nt * CH
        for oc in range(8):
            @GP
            def _(g, oc=oc, t0=t0):
                g.wait_ge(VC.h, VC.n)
                SP.inc(g.dma_start(out=ss_t[:], in_=s_d[oc, :, t0:t0 + CH]),
                       16)
                SP.inc(g.dma_start(out=vv_t[:], in_=v_d[oc, :, t0:t0 + CH]),
                       16)
            sp_sv = SP.n

            @TN
            def _(t, oc=oc, t0=t0):
                t.wait_ge(VC.h, max(vc_p3, rx_evac[0]))
                t.matmul(pb[0][:], bsel_s[:, oc * 128:(oc + 1) * 128],
                         r_s[:, t0:t0 + CH], start=True, stop=True)
                TE.inc(t.matmul(pb[1][:], bsel_s[:, oc * 128:(oc + 1) * 128],
                                dinv_s[:, t0:t0 + CH], start=True, stop=True))
            te_rd = TE.n

            @VE
            def _(v, oc=oc, te_rd=te_rd, sp_sv=sp_sv):
                v.wait_ge(TE.h, te_rd)
                v.wait_ge(SP.h, sp_sv)
                v.wait_ge(TE.h, o_used[0])
                v.tensor_copy(rx_t[:], pb[0][:])
                VC.inc(v.tensor_copy(dx_t[:], pb[1][:]))
                rx_evac[0] = VC.n
                v.tensor_scalar(tp4[:], ss_t[:], car2[:, oc:oc + 1], None,
                                ALU.add)
                v.tensor_tensor(o32[:], tp4[:], rx_t[:], ALU.mult)
                v.tensor_tensor(tp4[:], o32[:], vv_t[:], ALU.add)
                v.tensor_tensor(o32[:], tp4[:], dx_t[:], ALU.mult)
                v.tensor_copy(o_h[:, oc * CH:(oc + 1) * CH], o32[:])
                VC.inc(v.tensor_tensor(o_l[:, oc * CH:(oc + 1) * CH], o32[:],
                                       o_h[:, oc * CH:(oc + 1) * CH],
                                       ALU.subtract))
        vc_o = VC.n

        for q in range(4):
            tt = nt * 4 + q

            @GP
            def _(g, tt=tt):
                g.wait_ge(VC.h, VC.n)
                SP.inc(g.dma_start(out=xrow[:],
                                   in_=x_in[tt * 128:(tt + 1) * 128, :]), 16)
            sp_xr = SP.n

            @TN
            def _(t, q=q, vc_o=vc_o):
                t.wait_ge(VC.h, vc_o)
                for wc in range(2):
                    bank = 4 + (q * 2 + wc) % 4
                    t.wait_ge(VC.h, ybank_free[bank])
                    for oc in range(8):
                        oh = o_h[:, oc * CH + q * 128:oc * CH + (q + 1) * 128]
                        ol = o_l[:, oc * CH + q * 128:oc * CH + (q + 1) * 128]
                        owh = ow_h[:, oc * D + wc * 512:oc * D + (wc + 1) * 512]
                        owl = ow_l[:, oc * D + wc * 512:oc * D + (wc + 1) * 512]
                        t.matmul(pb[bank][:], oh, owh, start=(oc == 0),
                                 stop=False)
                        t.matmul(pb[bank][:], oh, owl, start=False,
                                 stop=False)
                        TE.inc(t.matmul(pb[bank][:], ol, owh, start=False,
                                        stop=(oc == 7)))
            te_y = TE.n

            @VE
            def _(v, q=q, te_y=te_y, sp_xr=sp_xr):
                v.wait_ge(TE.h, te_y)
                v.wait_ge(SP.h, SP.n)
                for wc in range(2):
                    bank = 4 + (q * 2 + wc) % 4
                    VC.inc(v.tensor_tensor(x1t[:, wc * 512:(wc + 1) * 512],
                                           pb[bank][:],
                                           xrow[:, wc * 512:(wc + 1) * 512],
                                           ALU.add))
                    ybank_free[bank] = VC.n
            vc_x1 = VC.n

            @SE
            def _(s, vc_x1=vc_x1):
                s.wait_ge(VC.h, vc_x1)
                s.activation(xn32[:], x1t[:], AF.Square,
                             accum_out=s2_)
                SC.inc(s.activation(xn32[:], x1t[:], AF.Identity,
                                    accum_out=s1_))
            sc_sq2 = SC.n

            @VE
            def _(v, sc_sq2=sc_sq2):
                v.wait_ge(SC.h, sc_sq2)
                v.tensor_scalar(mu_, s1_, 1.0 / D, None, ALU.mult)
                v.tensor_scalar(s2b_, s2_, 1.0 / D, 1e-5, ALU.mult,
                                ALU.add)
                v.tensor_scalar(nmu_, mu_, -1.0, None, ALU.mult)
                v.tensor_tensor(var_, mu_, mu_, ALU.mult)
                v.drain()
                v.tensor_tensor(s1_, s2b_, var_, ALU.subtract)
                v.drain()
                VC.inc(v.reciprocal(var_, s1_))
            vc_st2 = VC.n

            @SE
            def _(s, vc_st2=vc_st2):
                s.wait_ge(VC.h, vc_st2)
                SC.inc(s.activation(isd_, var_, AF.Sqrt))
            sc_ap2 = SC.n

            @VE
            def _(v, sc_ap2=sc_ap2):
                v.wait_ge(SC.h, sc_ap2)
                v.wait_ge(SP.h, SP.n)
                v.tensor_scalar(xc_[:], x1t[:], nmu_, None, ALU.add)
                v.tensor_scalar(xn32[:], xc_[:], isd_, None, ALU.mult)
                v.tensor_copy(xnh_t[:], xn32[:])
                VC.inc(v.tensor_tensor(xnl_t[:], xn32[:], xnh_t[:],
                                       ALU.subtract))
            vc_sp2 = VC.n

            @GP
            def _(g, tt=tt, vc_sp2=vc_sp2):
                g.wait_ge(VC.h, vc_sp2)
                SP.inc(g.dma_start(out=x1_d[tt * 128:(tt + 1) * 128, :],
                                   in_=x1t[:]), 16)
                SP.inc(g.dma_start(out=agx_i[tt * 128:(tt + 1) * 128, 0:D],
                                   in_=xnh_t[:]), 16)
                SP.inc(g.dma_start(out=agx_i[tt * 128:(tt + 1) * 128,
                                             D:2 * D], in_=xnl_t[:]), 16)
        o_used[0] = TE.n
    sp_p4 = SP.n

    @GP
    def _(g):
        g.wait_ge(SP.h, sp_p4)
        CC.inc(g.collective_compute(
            "AllGather", ALU.bypass, replica_groups=[list(range(NC))],
            ins=[agx_i[:, :]], outs=[agx_o[:, :]]))
    cc_agx = CC.n

    ctxP4.close()
    ctxA.close()

    ctxG = ExitStack()
    def sbg(name, shape, dt=F32):
        return ctxG.enter_context(nc.sbuf_tensor(name, shape, dt))

    xg_h = sbg("xg_h", [128, 8 * T], BF16)
    xg_l = sbg("xg_l", [128, 8 * T], BF16)
    lg_t = sbg("lg_t", [128, 128])
    es_t = sbg("es_t", [128, 128])
    m8_t = sbg("m8_t", [128, 16])
    ssum_t = sbg("ssum_t", [128, 16])
    rec_t = sbg("rec_t", [128, 16])
    sm_w = sbg("sm_w", [128, 128])
    gt_w = sbg("gt_w", [128, 1024])
    eq_w = sbg("eq_w", [128, 1024])
    eqm_w = sbg("eqm_w", [128, 1024])
    rks_w = sbg("rks_w", [128, 1024])
    rk_w = sbg("rk_w", [128, 128])
    a1t = sbg("a1t", [128, 8]); a2t = sbg("a2t", [128, 8])
    voff = sbg("voff", [128, 256])
    selo = sbg("selo", [128, 256]); sele = sbg("sele", [128, 256])
    sel5 = sbg("sel5", [128, 256]); cmpq = sbg("cmpq", [128, 256])
    pos_add = sbg("pos_add", [128, 32]); esum = sbg("esum", [128, 32])
    posga = sbg("posga", [128, 32]); valid_w = sbg("valid_w", [128, 32])
    e512s = sbg("e512s", [128, 32]); q_w = sbg("q_w", [128, 32])
    posc_w = sbg("posc_w", [128, 32]); cdif = sbg("cdif", [128, 32])
    w32a = sbg("w32a", [128, 32]); w32b = sbg("w32b", [128, 32])
    s12 = sbg("s12", [128, 16]); rec12 = sbg("rec12", [128, 16])
    g1a = sbg("g1a", [128, 16]); g2a = sbg("g2a", [128, 16])

    @SY
    def _(sy):
        sy.wait_ge(SP.h, sp_p4)
        for fc in range(8):
            LD.inc(sy.dma_start_transpose(
                xg_h[:, fc * T:(fc + 1) * T],
                agx_i[:, fc * 128:(fc + 1) * 128]), 16)
            LD.inc(sy.dma_start_transpose(
                xg_l[:, fc * T:(fc + 1) * T],
                agx_i[:, D + fc * 128:D + (fc + 1) * 128]), 16)
    ld_xg = LD.n

    lg_read = [VC.n]
    for tt in range(16):
        @TN
        def _(t, tt=tt, ld_xg=ld_xg):
            t.wait_ge(LD.h, ld_xg)
            t.wait_ge(VC.h, lg_read[0])
            for fc in range(8):
                xh = xg_h[:, fc * T + tt * 128:fc * T + (tt + 1) * 128]
                xl = xg_l[:, fc * T + tt * 128:fc * T + (tt + 1) * 128]
                t.matmul(pb[0][:, 0:8], xh, wg_h[:, fc, :], start=(fc == 0),
                         stop=False)
                t.matmul(pb[0][:, 0:8], xh, wg_l[:, fc, :], start=False,
                         stop=False)
                TE.inc(t.matmul(pb[0][:, 0:8], xl, wg_h[:, fc, :],
                                start=False, stop=(fc == 7)))
        te_lg = TE.n

        @VE
        def _(v, tt=tt, te_lg=te_lg):
            v.wait_ge(TE.h, te_lg)
            VC.inc(v.tensor_copy(lg_t[:, tt * 8:tt * 8 + 8], pb[0][:, 0:8]))
            lg_read[0] = VC.n

    lg3 = lg_t.ap().rearrange("p (t k) -> p t k", k=8)
    es3 = es_t.ap().rearrange("p (t k) -> p t k", k=8)
    sm3 = sm_w.ap().rearrange("p (t k) -> p t k", k=8)

    @VE
    def _(v):
        v.tensor_reduce(m8_t[:], lg3, AX.X, ALU.max)
        v.drain()
        VC.inc(v.tensor_tensor(
            es3, lg3, m8_t[:, :, None].to_broadcast([128, 16, 8]),
            ALU.subtract))
    vc_es = VC.n

    @SE
    def _(s, vc_es=vc_es):
        s.wait_ge(VC.h, vc_es)
        SC.inc(s.activation(es_t[:], es_t[:], AF.Exp))
    sc_ex = SC.n

    @VE
    def _(v, sc_ex=sc_ex):
        v.wait_ge(SC.h, sc_ex)
        v.tensor_reduce(ssum_t[:], es3, AX.X, ALU.add)
        v.drain()
        v.reciprocal(rec_t[:], ssum_t[:])
        v.drain()
        v.tensor_tensor(sm3, es3,
                        rec_t[:, :, None].to_broadcast([128, 16, 8]),
                        ALU.mult)
        v.drain()
        smA = sm_w.ap().rearrange("p (t k) -> p t k", k=8)[
            :, :, None, :].to_broadcast([128, 16, 8, 8])
        smB = sm_w.ap().rearrange("p (t j) -> p t j", j=8)[
            :, :, :, None].to_broadcast([128, 16, 8, 8])
        gt4 = gt_w.ap().rearrange("p (t j k) -> p t j k", j=8, k=8)
        eq4 = eq_w.ap().rearrange("p (t j k) -> p t j k", j=8, k=8)
        v.tensor_tensor(gt4, smA, smB, ALU.is_gt)
        v.tensor_tensor(eq4, smA, smB, ALU.is_equal)
        v.tensor_tensor(eqm_w[:], eq_w[:], ltm_s[:, :], ALU.mult)
        v.tensor_tensor(rks_w[:], gt_w[:], eqm_w[:], ALU.add)
        v.tensor_reduce(rk_w[:],
                        rks_w.ap().rearrange("p (t j k) -> p t j k", j=8,
                                             k=8).rearrange(
                            "p t j k -> p (t j) k"),
                        AX.X, ALU.add)
        v.tensor_scalar(ohst[:, 0:128], rk_w[:], 0.0, None, ALU.is_equal)
        v.tensor_scalar(ohst[:, 128:256], rk_w[:], 1.0, None, ALU.is_equal)
        v.tensor_tensor(lg_t[:], ohst[:, 0:128], sm_w[:], ALU.mult)
        v.tensor_tensor(es_t[:], ohst[:, 128:256], sm_w[:], ALU.mult)
        v.tensor_reduce(v1st[:],
                        lg_t.ap().rearrange("p (t k) -> p t k", k=8),
                        AX.X, ALU.add)
        VC.inc(v.tensor_reduce(v2st[:],
                               es_t.ap().rearrange("p (t k) -> p t k", k=8),
                               AX.X, ALU.add))
    vc_top = VC.n

    for tt in range(16):
        @TN
        def _(t, tt=tt, vc_top=vc_top):
            t.wait_ge(VC.h, max(vc_top, VC.n))
            oh1s = ohst[:, tt * 8:tt * 8 + 8]
            oh2s = ohst[:, 128 + tt * 8:128 + tt * 8 + 8]
            t.matmul(pb[4][:, 0:8], tri_s[:], oh1s, start=True, stop=False)
            t.matmul(pb[4][:, 0:8], ones1_s[:], cnt_s[0:1, 0:8], start=False,
                     stop=True)
            t.matmul(pb[5][:, 0:8], tri_s[:], oh2s, start=True, stop=False)
            t.matmul(pb[5][:, 0:8], ones1_s[:], cnt_s[0:1, 8:16], start=False,
                     stop=True)
            t.matmul(pb[6][0:1, 0:8], onesc_s[:], oh1s, start=(tt == 0),
                     stop=True, skip_group_check=True)
            TE.inc(t.matmul(pb[6][0:1, 8:16], onesc_s[:], oh2s,
                            start=False, stop=True, skip_group_check=True))
        te_pos = TE.n

        @VE
        def _(v, tt=tt, te_pos=te_pos):
            v.wait_ge(TE.h, te_pos)
            v.tensor_tensor(a1t[:], ohst[:, tt * 8:tt * 8 + 8],
                            pb[4][:, 0:8], ALU.mult)
            v.tensor_tensor(a2t[:], ohst[:, 128 + tt * 8:128 + tt * 8 + 8],
                            pb[5][:, 0:8], ALU.mult)
            v.drain()
            v.reduce_sum(plst[:, tt:tt + 1], a1t[:], axis=AX.X)
            v.reduce_sum(plst[:, 16 + tt:16 + tt + 1], a2t[:], axis=AX.X)
            VC.inc(v.tensor_copy(cnt_s[:], pb[6][0:1, 0:16]))
    vc_cnt = VC.n

    @GP
    def _(g, vc_cnt=vc_cnt):
        g.wait_ge(VC.h, vc_cnt)
        SP.inc(g.dma_start(out=agc_i[:, :],
                           in_=cnt_s.ap().rearrange("p (a b) -> p a b", a=2)),
               16)
        g.wait_ge(SP.h, SP.n)
        CC.inc(g.collective_compute(
            "AllGather", ALU.bypass, replica_groups=[list(range(NC))],
            ins=[agc_i[:, :]], outs=[agc_o[:, :]]))
        g.wait_ge(CC.h, CC.n)
        SP.inc(g.dma_start(out=agc_s[:], in_=agc_o[:, :]), 16)
    sp_agc = SP.n

    @TN
    def _(t):
        t.wait_ge(SP.h, sp_agc)
        t.matmul(pb[7][:, 0:8], offmr0_s[:], agc_s[:], start=True, stop=False)
        TE.inc(t.matmul(pb[7][:, 8:16], offmr1_s[:], agc_s[:], start=False,
                        stop=True))
    te_off = TE.n

    @VE
    def _(v, te_off=te_off):
        v.wait_ge(TE.h, te_off)
        v.tensor_copy(voff.ap().rearrange("p (k t e) -> p k t e", k=2, t=16,
                                          e=8)[:, 0:1, :, :],
                      pb[7][:, 0:8][:, None, :].to_broadcast([128, 16, 8])[
                          :, None, :, :].to_broadcast([128, 1, 16, 8]))
        VC.inc(v.tensor_copy(
            voff.ap().rearrange("p (k t e) -> p k t e", k=2, t=16,
                                e=8)[:, 1:2, :, :],
            pb[7][:, 8:16][:, None, :].to_broadcast([128, 16, 8])[
                :, None, :, :].to_broadcast([128, 1, 16, 8])))
    vc_voff = VC.n

    oh32 = ohst.ap().rearrange("p (c e) -> p c e", e=8)
    @VE
    def _(v, vc_voff=vc_voff):
        v.wait_ge(VC.h, vc_voff)
        v.drain()
        v.tensor_tensor(selo[:], ohst[:], voff[:], ALU.mult)
        v.tensor_tensor(sele[:], ohst[:], evec_s[:], ALU.mult)
        v.tensor_reduce(pos_add[:],
                        selo.ap().rearrange("p (c e) -> p c e", e=8),
                        AX.X, ALU.add)
        v.tensor_reduce(esum[:],
                        sele.ap().rearrange("p (c e) -> p c e", e=8),
                        AX.X, ALU.add)
        v.tensor_tensor(posga[:], pos_add[:], plst[:], ALU.add)
        v.tensor_tensor(sel5[:], ohst[:], e512_s[:], ALU.mult)
        v.tensor_scalar(valid_w[:], posga[:], 4096.0, None, ALU.is_lt)
        v.tensor_reduce(e512s[:],
                        sel5.ap().rearrange("p (c e) -> p c e", e=8),
                        AX.X, ALU.add)
        v.tensor_tensor(w32a[:], esum[:], posga[:], ALU.add)
        v.tensor_tensor(s12[:], v1st[:], v2st[:], ALU.add)
        v.tensor_scalar(w32b[:], w32a[:], float(TRASH), None, ALU.subtract)
        v.reciprocal(rec12[:], s12[:])
        v.tensor_tensor(w32a[:], w32b[:], valid_w[:], ALU.mult)
        v.tensor_tensor(g1a[:], v1st[:], rec12[:], ALU.mult)
        v.tensor_scalar(w32b[:], w32a[:], float(TRASH), None, ALU.add)
        v.tensor_tensor(g2a[:], v2st[:], rec12[:], ALU.mult)
        v.tensor_copy(sct[:], w32b[:])
        v.tensor_scalar(posc_w[:], posga[:], 4095.0, None, ALU.min)
        v.tensor_tensor(g1st[:], g1a[:], valid_w[:, 0:16], ALU.mult)
        v.tensor_tensor(g2st[:], g2a[:], valid_w[:, 16:32], ALU.mult)
        v.tensor_tensor(cmpq.ap().rearrange("p (c e) -> p c e", e=8),
                        posc_w[:, :, None].to_broadcast([128, 32, 8]),
                        thr_s.ap().rearrange("p (c e) -> p c e", e=8),
                        ALU.is_ge)
        v.drain()
        v.tensor_reduce(q_w[:],
                        cmpq.ap().rearrange("p (c e) -> p c e", e=8),
                        AX.X, ALU.add)
        v.drain()
        v.tensor_scalar(w32a[:], q_w[:], 512.0, None, ALU.mult)
        v.tensor_scalar(w32b[:], q_w[:], 4096.0, None, ALU.mult)
        v.tensor_tensor(cdif[:], posc_w[:], w32a[:], ALU.subtract)
        v.tensor_tensor(w32a[:], w32b[:], e512s[:], ALU.add)
        v.drain()
        v.tensor_tensor(w32b[:], w32a[:], cdif[:], ALU.add)
        v.drain()
        v.tensor_tensor(w32a[:], w32b[:], valid_w[:], ALU.mult)
        v.drain()
        VC.inc(v.tensor_copy(ybst[:], w32a[:]))
    vc_route = VC.n

    @GP
    def _(g, vc_route=vc_route):
        g.wait_ge(VC.h, vc_route)
        SP.inc(g.dma_start(
            out=invl_d.rearrange("(p c) one -> p (c one)", p=128),
            in_=zer_s[:]), 16)
        g.wait_ge(SP.h, SP.n)
        for col in range(32):
            SP.inc(g.indirect_dma_start(
                out=invl_d[:, :],
                out_offset=IndirectOffsetOnAxis(sct[:, col:col + 1], 0),
                in_=gtok_s[:, col % 16:col % 16 + 1], in_offset=None), 16)
        g.wait_ge(SP.h, SP.n)
        CC.inc(g.collective_compute(
            "AllReduce", ALU.add, replica_groups=[list(range(NC))],
            ins=[invl_d[:, :]], outs=[invg_d[:, :]]))
        g.wait_ge(CC.h, CC.n)
        for col in range(32):
            SP.inc(g.indirect_dma_start(
                out=invt[:, col:col + 1], out_offset=None,
                in_=invg_d[:, :],
                in_offset=IndirectOffsetOnAxis(einv_s[:, col:col + 1], 0)),
                16)
    sp_inv = SP.n

    ctxG.close()

    ctxF = ExitStack()
    def sbf(name, shape, dt=F32):
        return ctxF.enter_context(nc.sbuf_tensor(name, shape, dt))

    h_h = sbf("h_h", [128, 32 * CH], BF16)
    h_l = sbf("h_l", [128, 32 * CH], BF16)
    xf_h = sbf("xf_h", [128, 8 * CH], BF16)
    xf_l = sbf("xf_l", [128, 8 * CH], BF16)
    w1s_h = [sbf(f"w1s_h{i}", [128, 8, 128], BF16) for i in range(2)]
    w1s_l = [sbf(f"w1s_l{i}", [128, 8, 128], BF16) for i in range(2)]
    w2s_h = [sbf(f"w2s_h{i}", [128, D], BF16) for i in range(2)]
    w2s_l = [sbf(f"w2s_l{i}", [128, D], BF16) for i in range(2)]
    dsb = sbf("dsb", [128, 4 * 2 * D], BF16)
    h32a = sbf("h32a", [128, CH])
    h32b = sbf("h32b", [128, CH])
    yt_ = sbf("yt_", [128, CH])
    invf = sbf("invf", [128, 32])
    invg2 = sbf("invg2", [128, 32])

    @VE
    def _(v):
        v.wait_ge(SP.h, sp_inv)
        v.tensor_copy(invf[:], invt[:])
        v.drain()
        v.tensor_scalar(invg2[:], invf[:], -1.0, 0.0, ALU.add, ALU.max)
        v.drain()
        VC.inc(v.tensor_copy(invt[:], invg2[:]))
    vc_invt = VC.n

    xf_free = [TE.n]
    w1_free = [TE.n, TE.n]
    w2_free = [TE.n, TE.n]
    h_free = [SC.n] * 4
    hsplit = [VC.n, VC.n]
    y2_free = [VC.n] * 8

    for tc in range(TC):
        s0 = tc * CH

        @GP
        def _(g, tc=tc, s0=s0):
            if tc == 0:
                g.wait_ge(VC.h, vc_invt)
                g.wait_ge(CC.h, cc_agx)
            else:
                g.wait_ge(LD.h, LD.n)
            for q in range(4):
                SP.inc(g.indirect_dma_start(
                    out=dsb[:, q * 2 * D:(q + 1) * 2 * D],
                    out_offset=None,
                    in_=agx_o[:, :],
                    in_offset=IndirectOffsetOnAxis(
                        invt[:, tc * 4 + q:tc * 4 + q + 1], 0)), 16)
        sp_disp = SP.n

        @SY
        def _(sy, tc=tc, sp_disp=sp_disp):
            sy.wait_ge(SP.h, sp_disp)
            if tc > 0:
                sy.wait_ge(TE.h, xf_free[0])
            for q in range(4):
                for fc in range(8):
                    LD.inc(sy.dma_start_transpose(
                        xf_h[:, fc * CH + q * 128:fc * CH + (q + 1) * 128],
                        dsb[:, q * 2 * D + fc * 128:
                            q * 2 * D + (fc + 1) * 128]), 16)
                    LD.inc(sy.dma_start_transpose(
                        xf_l[:, fc * CH + q * 128:fc * CH + (q + 1) * 128],
                        dsb[:, q * 2 * D + D + fc * 128:
                            q * 2 * D + D + (fc + 1) * 128]), 16)
        ld_xf = LD.n

        for hc in range(32):
            buf = hc % 2
            bank = hc % 4

            @SE
            def _(sy, tc=tc, hc=hc, buf=buf):
                if tc == 0 and hc == 0:
                    sy.wait_ge(SP.h, sp_w12)
                sy.wait_ge(TE.h, w1_free[buf])
                WL.inc(sy.dma_start(
                    out=w1s_h[buf][:],
                    in_=w1h_d.rearrange("(f p) (c h) -> p f c h", p=128,
                                        h=128)[:, :, hc, :]), 16)
                WL.inc(sy.dma_start(
                    out=w1s_l[buf][:],
                    in_=w1l_d.rearrange("(f p) (c h) -> p f c h", p=128,
                                        h=128)[:, :, hc, :]), 16)
            ld_w1 = WL.n

            @TN
            def _(t, tc=tc, hc=hc, buf=buf, bank=bank, ld_w1=ld_w1,
                  ld_xf=ld_xf):
                t.wait_ge(WL.h, ld_w1)
                if hc == 0:
                    t.wait_ge(LD.h, ld_xf)
                    if tc == 0:
                        t.wait_ge(VC.h, vc_route)
                t.wait_ge(SC.h, h_free[bank])
                for fc in range(8):
                    xh = xf_h[:, fc * CH:(fc + 1) * CH]
                    xl = xf_l[:, fc * CH:(fc + 1) * CH]
                    t.matmul(pb[bank][:], w1s_h[buf][:, fc, :], xh,
                             start=(fc == 0), stop=False)
                    t.matmul(pb[bank][:], w1s_h[buf][:, fc, :], xl,
                             start=False, stop=False)
                    TE.inc(t.matmul(pb[bank][:], w1s_l[buf][:, fc, :], xh,
                                    start=False, stop=(fc == 7)))
                w1_free[buf] = TE.n
                if hc == 31:
                    xf_free[0] = TE.n
            te_h1 = TE.n

            @SE
            def _(s, hc=hc, bank=bank, te_h1=te_h1):
                s.wait_ge(TE.h, te_h1)
                s.wait_ge(VC.h, hsplit[hc % 2])
                h32 = h32a if hc % 2 == 0 else h32b
                SC.inc(s.activation(h32[:], pb[bank][:], AF.Relu,
                                    bias=b1m_s[:, hc:hc + 1]))
                h_free[bank] = SC.n
            sc_h = SC.n

            @VE
            def _(v, hc=hc, sc_h=sc_h):
                v.wait_ge(SC.h, sc_h)
                h32 = h32a if hc % 2 == 0 else h32b
                v.tensor_copy(h_h[:, hc * CH:(hc + 1) * CH], h32[:])
                VC.inc(v.tensor_tensor(h_l[:, hc * CH:(hc + 1) * CH], h32[:],
                                       h_h[:, hc * CH:(hc + 1) * CH],
                                       ALU.subtract))
                hsplit[hc % 2] = VC.n
        vc_hdone = VC.n

        for hc in range(32):
            buf = hc % 2

            @SE
            def _(sy, hc=hc, buf=buf):
                sy.wait_ge(TE.h, w2_free[buf])
                WL.inc(sy.dma_start(
                    out=w2s_h[buf][:],
                    in_=w2h_d[hc * 128:(hc + 1) * 128, :]), 16)
                WL.inc(sy.dma_start(
                    out=w2s_l[buf][:],
                    in_=w2l_d[hc * 128:(hc + 1) * 128, :]), 16)
            ld_w2 = WL.n

            @TN
            def _(t, hc=hc, buf=buf, ld_w2=ld_w2, vc_hdone=vc_hdone):
                t.wait_ge(WL.h, ld_w2)
                if hc == 0:
                    t.wait_ge(VC.h, vc_hdone)
                    for b8 in range(8):
                        t.wait_ge(VC.h, y2_free[b8])
                for q in range(4):
                    hh = h_h[:, hc * CH + q * 128:hc * CH + (q + 1) * 128]
                    hl = h_l[:, hc * CH + q * 128:hc * CH + (q + 1) * 128]
                    for mc in range(2):
                        bank = q * 2 + mc
                        wh = w2s_h[buf][:, mc * 512:(mc + 1) * 512]
                        wl = w2s_l[buf][:, mc * 512:(mc + 1) * 512]
                        t.matmul(pb[bank][:], hh, wh, start=(hc == 0),
                                 stop=False, skip_group_check=True)
                        t.matmul(pb[bank][:], hh, wl, start=False,
                                 stop=False, skip_group_check=True)
                        TE.inc(t.matmul(pb[bank][:], hl, wh, start=False,
                                        stop=(hc == 31),
                                        skip_group_check=True))
                w2_free[buf] = TE.n
        te_y2 = TE.n

        for q in range(4):
            for mc in range(2):
                bank = q * 2 + mc

                @VE
                def _(v, bank=bank, mc=mc, te_y2=te_y2):
                    v.wait_ge(TE.h, te_y2)
                    v.wait_ge(SP.h, SP.n)
                    VC.inc(v.tensor_tensor(
                        yt_[:], pb[bank][:],
                        b2r_s[:, mc * 512:(mc + 1) * 512], ALU.add))
                    y2_free[bank] = VC.n
                vc_y = VC.n

                @GP
                def _(g, q=q, mc=mc, s0=s0, vc_y=vc_y):
                    g.wait_ge(VC.h, vc_y)
                    SP.inc(g.dma_start(
                        out=agy_i[s0 + q * 128:s0 + (q + 1) * 128,
                                  mc * 512:(mc + 1) * 512], in_=yt_[:]), 16)

        @GP
        def _(g, tc=tc, s0=s0):
            g.wait_ge(SP.h, SP.n)
            CC.inc(g.collective_compute(
                "AllGather", ALU.bypass, replica_groups=[list(range(NC))],
                ins=[agy_i[s0:s0 + CH, :]],
                outs=[agy_o[tc * CAP:(tc + 1) * CAP, :]]))
    cc_ally = CC.n

    y1t2 = [sbf(f"y1t{i}", [128, D]) for i in range(2)]
    y2t2 = [sbf(f"y2t{i}", [128, D]) for i in range(2)]
    x1r2 = [sbf(f"x1r{i}", [128, D]) for i in range(2)]
    fin2 = [sbf(f"fin{i}", [128, D]) for i in range(2)]
    fin0s = sbf("fin0s", [128, D])

    vc_fin2 = [0] * 18
    sp_st2 = [0] * 18
    for tt in range(16):
        b = tt % 2
        y1t = y1t2[b]; y2t = y2t2[b]; x1r = x1r2[b]
        fin = fin2[b]; fin0 = fin0s

        @GP
        def _(g, tt=tt, y1t=y1t, y2t=y2t, x1r=x1r):
            if tt == 0:
                g.wait_ge(CC.h, cc_ally)
            if tt > 1:
                g.wait_ge(VC.h, vc_fin2[tt])
            SP.inc(g.indirect_dma_start(
                out=y1t[:, :], out_offset=None, in_=agy_o[:, :],
                in_offset=IndirectOffsetOnAxis(ybst[:, tt:tt + 1], 0)), 16)
            SP.inc(g.indirect_dma_start(
                out=y2t[:, :], out_offset=None, in_=agy_o[:, :],
                in_offset=IndirectOffsetOnAxis(ybst[:, 16 + tt:16 + tt + 1],
                                               0)), 16)
            SP.inc(g.dma_start(out=x1r[:],
                               in_=x1_d[tt * 128:(tt + 1) * 128, :]), 16)
        sp_y = SP.n

        @VE
        def _(v, tt=tt, sp_y=sp_y, y1t=y1t, y2t=y2t, x1r=x1r, fin=fin,
              fin0=fin0):
            v.wait_ge(SP.h, sp_y)
            if tt > 1:
                v.wait_ge(SP.h, sp_st2[tt])
            v.scalar_tensor_tensor(fin0[:], y1t[:], g1st[:, tt:tt + 1],
                                   x1r[:], ALU.mult, ALU.add)
            VC.inc(v.scalar_tensor_tensor(fin[:], y2t[:], g2st[:, tt:tt + 1],
                                          fin0[:], ALU.mult, ALU.add))
        vc_fin2[tt + 2] = VC.n
        vc_fin = VC.n

        @GP
        def _(g, tt=tt, vc_fin=vc_fin, fin=fin):
            g.wait_ge(VC.h, vc_fin)
            SP.inc(g.dma_start(out=y_out[tt * 128:(tt + 1) * 128, :],
                               in_=fin[:]), 16)
        sp_st2[tt + 2] = SP.n

    @GP
    def _(g):
        g.wait_ge(SP.h, SP.n)

    ctxF.close()
    ctx.close()
    return nc


def prep_inputs(inputs):
    x = np.ascontiguousarray(np.asarray(inputs["L0_attn_x"], np.float32)
                             .reshape(NC * T, D))
    k1 = np.asarray(inputs["L0_attn_k1"], np.float32)
    k2 = np.asarray(inputs["L0_attn_k2"], np.float32)
    k3 = np.asarray(inputs["L0_attn_k3"], np.float32)
    a1 = np.asarray(inputs["L0_attn_a1"], np.float32).reshape(H, 64)
    a2 = np.asarray(inputs["L0_attn_a2"], np.float32).reshape(H, 64)
    b1 = np.asarray(inputs["L0_attn_b1"], np.float32).reshape(H, 64)
    b2 = np.asarray(inputs["L0_attn_b2"], np.float32).reshape(H, 64)
    c_ = np.asarray(inputs["L0_attn_c"], np.float32).reshape(H, 64)
    vw = np.asarray(inputs["L0_attn_value_weight"], np.float32)
    ow = np.asarray(inputs["L0_attn_output_weight"], np.float32)
    wg = np.asarray(inputs["L0_ffn_wg"], np.float32)
    w1 = np.asarray(inputs["L0_ffn_w1"], np.float32)
    fb1 = np.asarray(inputs["L0_ffn_b1"], np.float32)
    w2 = np.asarray(inputs["L0_ffn_w2"], np.float32)
    fb2 = np.asarray(inputs["L0_ffn_b2"], np.float32)

    wkt = np.ascontiguousarray(np.concatenate([k1, k2, k3], 0).T)
    wv2 = np.ascontiguousarray(vw.transpose(1, 0, 2).reshape(D, D))
    ow2 = np.ascontiguousarray(ow.transpose(0, 2, 1).reshape(D, D))

    cMm = np.zeros((128, 128), np.float32)
    for sc in range(8):
        for lh in range(2):
            h = sc * 2 + lh
            cMm[lh * 64:(lh + 1) * 64, sc * 16 + h] = c_[h]
    pc = {}
    for nm, arr in (("a1c", a1), ("b1c", b1), ("a2c", a2), ("b2c", b2)):
        pc[nm] = np.ascontiguousarray(arr.reshape(8, 128).T)

    trim = np.ascontiguousarray(np.triu(np.ones((128, 128), np.float32), 1))
    bselm = np.zeros((16, 1024), np.float32)
    for oc in range(8):
        for lh in range(2):
            h = oc * 2 + lh
            bselm[h, oc * 128 + lh * 64:oc * 128 + (lh + 1) * 64] = 1.0
    evec = np.ascontiguousarray(np.broadcast_to(
        np.tile(np.arange(8, dtype=np.float32) * 4096.0, 32), (128, 256)))
    e512 = np.ascontiguousarray(np.broadcast_to(
        np.tile(np.arange(8, dtype=np.float32) * 512.0, 32), (128, 256)))
    thr = np.ascontiguousarray(np.broadcast_to(
        np.tile(np.arange(1, 9, dtype=np.float32) * 512.0, 32), (128, 256)))
    ltm = np.zeros((8, 8), np.float32)
    for j in range(8):
        ltm[j, :j] = 1.0
    ltm = np.ascontiguousarray(np.broadcast_to(
        np.tile(ltm.reshape(64), 16), (128, 1024)))
    ones1 = np.ones((1, 128), np.float32)
    onesc = np.ones((128, 1), np.float32)

    in_maps = []
    for j in range(NC):
        b1mh = np.ascontiguousarray(fb1[j].reshape(32, 128).T)
        b2rh = np.ascontiguousarray(
            np.broadcast_to(fb2[j].reshape(1, D), (128, D)))
        b = j // 4
        nloc = np.ascontiguousarray(np.broadcast_to(
            ((np.arange(T) + (j % 4) * T).astype(np.float32) / 8191.0)
            .reshape(1, T), (128, T)))
        seqm = np.zeros((8, 1), np.float32)
        seqm[4 * b:j, 0] = 1.0
        offm = np.zeros((16, 2), np.float32)
        offm[0:2 * j:2, 0] = 1.0
        offm[0::2, 1] = 1.0
        offm[1:2 * j:2, 1] = 1.0
        offmr0 = np.ascontiguousarray(np.broadcast_to(offm[:, 0:1], (16, 128)))
        offmr1 = np.ascontiguousarray(np.broadcast_to(offm[:, 1:2], (16, 128)))
        gt = (np.arange(T, dtype=np.int64) + j * T + 1).astype(np.int32)
        gtok = np.ascontiguousarray(gt.reshape(16, 128).T)
        einv = np.ascontiguousarray(
            (j * 4096 + np.arange(32, dtype=np.int32)[None, :] * 128
             + np.arange(128, dtype=np.int32)[:, None]).astype(np.int32))
        in_maps.append({
            "x": x[j * T:(j + 1) * T],
            "wkt": wkt, "wv2": wv2, "ow2": ow2, "wgm": wg,
            "w1m": w1[j], "w2m": w2[j], "b1m": b1mh, "b2row": b2rh,
            "cM": cMm, "a1c": pc["a1c"], "b1c": pc["b1c"],
            "a2c": pc["a2c"], "b2c": pc["b2c"],
            "tri": trim, "bsel": bselm,
            "nvec": nloc, "seqm": seqm,
            "offm": offm, "offmr0": offmr0, "offmr1": offmr1,
            "gtok": gtok, "evec": evec, "e512": e512,
            "thr": thr, "ltm": ltm, "ones1": ones1, "onesc": onesc,
            "einv": einv,
        })
    return in_maps


def kernel(**inputs):
    nc = build_program()
    in_maps = prep_inputs(inputs)
    res = run_bass_kernel_spmd(nc, in_maps, core_ids=list(range(NC)))
    out = np.concatenate([res.results[j]["y_out"] for j in range(NC)], axis=0)
    return out.reshape(2, 8192, 1024)


# version 10
# speedup vs baseline: 1.1423x; 1.1094x over previous
from contextlib import ExitStack

import numpy as np

import concourse.bass as bass
import concourse.mybir as mybir
from concourse.bass_utils import run_bass_kernel_spmd
from concourse.bass_core_rust import IndirectOffsetOnAxis

F32 = mybir.dt.float32
BF16 = mybir.dt.bfloat16
I32 = mybir.dt.int32
ALU = mybir.AluOpType
AF = mybir.ActivationFunctionType
AX = mybir.AxisListType

NC = 8
T = 2048
D = 1024
H = 16
HD = 4096
CAP = 4096
NT = 4
CH = 512
TC = 8
TRASH = 32768


class Sem:
    def __init__(self, nc, ctx, name):
        self.h = ctx.enter_context(nc.semaphore(name))
        self.n = 0

    def inc(self, inst, k=1):
        inst.then_inc(self.h, k)
        self.n += k
        return self.n


def build_program():
    nc = bass.Bass("TRN2", target_bir_lowering=False,
                   detect_race_conditions=False)
    ctx = ExitStack()

    def din(name, shape, dt=F32):
        return nc.dram_tensor(name, shape, dt, kind="ExternalInput").ap()

    x_in = din("x", [T, D])
    wkt = din("wkt", [D, 48])
    wv2 = din("wv2", [D, D])
    ow2 = din("ow2", [D, D])
    wgm = din("wgm", [D, 8])
    w1m = din("w1m", [D, HD])
    w2m = din("w2m", [HD, D])
    b1m_i = din("b1m", [128, 32])
    b2r_i = din("b2row", [128, D])
    cM_i = din("cM", [128, 128])
    a1_i = din("a1c", [128, 8])
    b1_i = din("b1c", [128, 8])
    a2_i = din("a2c", [128, 8])
    b2_i = din("b2c", [128, 8])
    tri_i = din("tri", [128, 128])
    bsel_i = din("bsel", [16, 1024])
    nvec_i = din("nvec", [128, T])
    seqm_i = din("seqm", [8, 1])
    offm_i = din("offm", [16, 2])
    offr0_i = din("offmr0", [16, 128])
    offr1_i = din("offmr1", [16, 128])
    gtok_i = din("gtok", [128, 16], I32)
    evec_i = din("evec", [128, 256])
    e512_i = din("e512", [128, 256])
    thr_i = din("thr", [128, 256])
    ltm_i = din("ltm", [128, 1024])
    ones1_i = din("ones1", [1, 128])
    onesc_i = din("onesc", [128, 1])
    einv_i = din("einv", [128, 32], I32)

    y_out = nc.dram_tensor("y_out", [T, D], F32, kind="ExternalOutput").ap()

    def dint(name, shape, dt=F32):
        return nc.dram_tensor(name, shape, dt).ap()

    xnh_d = dint("xnh_d", [T, D], BF16)
    xnl_d = dint("xnl_d", [T, D], BF16)
    s_d = dint("s_d", [8, 128, T])
    v_d = dint("v_d", [8, 128, T])
    x1_d = dint("x1_d", [T, D])
    p12_d = dint("p12_d", [16, 2 * T])
    agcar_i = dint("agcar_i", [1, 1040])
    agcar_o = dint("agcar_o", [8, 1040])
    agx_i = dint("agx_i", [T, 2 * D], BF16)
    agx_o = dint("agx_o", [NC * T, 2 * D], BF16)
    agc_i = dint("agc_i", [2, 8])
    agc_o = dint("agc_o", [16, 8])
    invl_d = dint("invl_d", [33024, 1], I32)
    invg_d = dint("invg_d", [33024, 1], I32)
    agy_i = dint("agy_i", [CAP, D])
    agy_o = dint("agy_o", [NC * CAP, D])
    w1h_d = dint("w1h_d", [HD, D], BF16)
    w1l_d = dint("w1l_d", [HD, D], BF16)
    w2h_d = dint("w2h_d", [HD, D], BF16)
    w2l_d = dint("w2l_d", [HD, D], BF16)

    def sb(name, shape, dt=F32):
        return ctx.enter_context(nc.sbuf_tensor(name, shape, dt))

    wk_h = sb("wk_h", [128, 8, 48], BF16)
    wk_l = sb("wk_l", [128, 8, 48], BF16)
    wg_h = sb("wg_h", [128, 8, 8], BF16)
    wg_l = sb("wg_l", [128, 8, 8], BF16)
    a1s = sb("a1s", [128, 8]); b1s = sb("b1s", [128, 8])
    a2s = sb("a2s", [128, 8]); b2s = sb("b2s", [128, 8])
    tri_s = sb("tri_s", [128, 128])
    bsel_s = sb("bsel_s", [16, 1024])
    seqm_s = sb("seqm_s", [8, 1])
    offm_s = sb("offm_s", [16, 2])
    gtok_s = sb("gtok_s", [128, 16], I32)
    evec_s = sb("evec_s", [128, 256])
    e512_s = sb("e512_s", [128, 256])
    thr_s = sb("thr_s", [128, 256])
    ltm_s = sb("ltm_s", [128, 1024])
    ones1_s = sb("ones1_s", [1, 128])
    onesc_s = sb("onesc_s", [128, 1])
    b1m_s = sb("b1m_s", [128, 32])
    b2r_s = sb("b2r_s", [128, D])
    einv_s = sb("einv_s", [128, 32], I32)
    scarL = [sb(f"scar{i}", [128, 1]) for i in range(8)]
    Zcar = sb("Zcar", [16, 1])
    car2 = sb("car2", [128, 8])
    zc2 = sb("zc2", [16, 1])
    agc_s = sb("agc_s", [16, 8])
    offmr0_s = sb("offmr0_s", [16, 128])
    offmr1_s = sb("offmr1_s", [16, 128])
    g1st = sb("g1st", [128, 16]); g2st = sb("g2st", [128, 16])
    v1st = sb("v1st", [128, 16]); v2st = sb("v2st", [128, 16])
    ybst = sb("ybst", [128, 32], I32)
    sct = sb("sct", [128, 32], I32)
    zer_s = sb("zer_s", [128, 258], I32)
    invt = sb("invt", [128, 32], I32)
    cnt_s = sb("cnt_s", [1, 16])

    pb = [ctx.enter_context(nc.psum_tensor(f"pb{i}", [128, 512], F32))
          for i in range(8)]

    LD = Sem(nc, ctx, "LD")
    SP = Sem(nc, ctx, "SP")
    VC = Sem(nc, ctx, "VC")
    SC = Sem(nc, ctx, "SC")
    TE = Sem(nc, ctx, "TE")
    CC = Sem(nc, ctx, "CC")
    WL = Sem(nc, ctx, "WL")

    block = ctx.enter_context(nc.Block())

    def SY(f): block.sync(f)
    def GP(f): block.gpsimd(f)
    def VE(f): block.vector(f)
    def SE(f): block.scalar(f)
    def TN(f): block.tensor(f)

    ctxA = ExitStack()
    def sba(name, shape, dt=F32):
        return ctxA.enter_context(nc.sbuf_tensor(name, shape, dt))

    wv_h = sba("wv_h", [128, 8 * D], BF16)
    wv_l = sba("wv_l", [128, 8 * D], BF16)
    ow_h = sba("ow_h", [128, 8 * D], BF16)
    ow_l = sba("ow_l", [128, 8 * D], BF16)
    Z_s = sba("Z_s", [16, T])
    r_s = sba("r_s", [16, T])
    dinv_s = sba("dinv_s", [16, T])
    r_s2 = sba("r_s2", [16, T])
    ss_t = sba("ss_t", [128, CH])
    vv_t = sba("vv_t", [128, CH])
    xt2 = [sba(f"xt{i}", [128, D]) for i in range(2)]
    xc2 = [sba(f"xc{i}", [128, D]) for i in range(2)]
    xn322 = [sba(f"xn32{i}", [128, D]) for i in range(2)]
    xnh2 = [sba(f"xnh{i}", [128, D], BF16) for i in range(2)]
    xnl2 = [sba(f"xnl{i}", [128, D], BF16) for i in range(2)]
    sm12 = [sba(f"sm1{i}", [128, 8]) for i in range(2)]

    ctxW = ExitStack()
    def sbw(name, shape, dt=F32):
        return ctxW.enter_context(nc.sbuf_tensor(name, shape, dt))

    wtmp2 = [sbw(f"wtmp{i}", [128, 2048]) for i in range(2)]
    wth2 = [sbw(f"wth{i}", [128, 2048], BF16) for i in range(2)]
    wtl2 = [sbw(f"wtl{i}", [128, 2048], BF16) for i in range(2)]
    wkt32 = sbw("wkt32", [128, 8, 48])
    wg32 = sbw("wg32", [128, 8, 8])
    cM_s = sbw("cM_s", [128, 128])
    nvec_s = sbw("nvec_s", [128, T])

    @SY
    def _(sy):
        for t, d_ in ((cM_s, cM_i), (tri_s, tri_i), (bsel_s, bsel_i),
                      (nvec_s, nvec_i), (seqm_s, seqm_i), (offm_s, offm_i),
                      (gtok_s, gtok_i), (evec_s, evec_i), (e512_s, e512_i),
                      (thr_s, thr_i), (ltm_s, ltm_i), (ones1_s, ones1_i),
                      (onesc_s, onesc_i), (b1m_s, b1m_i), (b2r_s, b2r_i),
                      (a1s, a1_i), (b1s, b1_i), (a2s, a2_i), (b2s, b2_i),
                      (einv_s, einv_i), (offmr0_s, offr0_i),
                      (offmr1_s, offr1_i)):
            LD.inc(sy.dma_start(out=t[:], in_=d_[:, :]), 16)
        LD.inc(sy.dma_start(
            out=wkt32[:], in_=wkt.rearrange("(f p) c -> p f c", p=128)), 16)
        LD.inc(sy.dma_start(
            out=wg32[:], in_=wgm.rearrange("(f p) c -> p f c", p=128)), 16)
    ld_consts = LD.n

    @VE
    def _(v):
        v.wait_ge(LD.h, ld_consts)
        for i in range(8):
            v.memset(scarL[i][:], 0.0)
        v.memset(Zcar[:], 0.0)
        v.memset(cnt_s[:], 0.0)
        v.memset(zer_s[:], 0)
        v.tensor_copy(wk_h[:], wkt32[:])
        v.tensor_copy(wg_h[:], wg32[:])
        v.tensor_tensor(wk_l[:], wkt32[:], wk_h[:], ALU.subtract)
        VC.inc(v.tensor_tensor(wg_l[:], wg32[:], wg_h[:], ALU.subtract))
    vc_consts = VC.n

    vfree = [VC.n, VC.n]
    for ji, (wsrc, dsth, dstl) in enumerate(
            [(wv2, wv_h, wv_l), (ow2, ow_h, ow_l)]):
        for fc in range(8):
            b = (ji * 8 + fc) % 2
            @SY
            def _(sy, fc=fc, b=b, wsrc=wsrc):
                sy.wait_ge(VC.h, vfree[b])
                LD.inc(sy.dma_start(out=wtmp2[b][:, 0:1024],
                                    in_=wsrc[fc * 128:(fc + 1) * 128, :]), 16)
            ldc = LD.n

            @VE
            def _(v, fc=fc, b=b, ldc=ldc, dsth=dsth, dstl=dstl):
                v.wait_ge(LD.h, ldc)
                v.tensor_copy(dsth[:, fc * D:(fc + 1) * D],
                              wtmp2[b][:, 0:1024])
                VC.inc(v.tensor_tensor(dstl[:, fc * D:(fc + 1) * D],
                                       wtmp2[b][:, 0:1024],
                                       dsth[:, fc * D:(fc + 1) * D],
                                       ALU.subtract))
                vfree[b] = VC.n



    def emit_w12_prep():
        vfree2 = [VC.n, VC.n]
        spfree = [SP.n, SP.n]
        lfree = [LD.n, LD.n]
        jobs = []
        for fc in range(8):
            for hh in range(2):
                jobs.append(("w1", fc, hh))
        for hc2 in range(16):
            jobs.append(("w2", hc2, 0))
        for ji, job in enumerate(jobs):
            b = ji % 2
            kind, i0, i1 = job

            @SE
            def _(sy, kind=kind, i0=i0, i1=i1, b=b, ji=ji):
                sy.wait_ge(VC.h, vfree2[b])
                if kind == "w1":
                    WL.inc(sy.dma_start(
                        out=wtmp2[b][:],
                        in_=w1m[i0 * 128:(i0 + 1) * 128,
                                i1 * 2048:(i1 + 1) * 2048]), 16)
                else:
                    WL.inc(sy.dma_start(
                        out=wtmp2[b][:],
                        in_=w2m.rearrange("(c p) m -> p c m", p=128)[
                            :, i0 * 2:(i0 + 1) * 2, :]), 16)
            ldc = WL.n

            @VE
            def _(v, b=b, ldc=ldc):
                v.wait_ge(WL.h, ldc)
                v.wait_ge(SP.h, spfree[b])
                v.wait_ge(LD.h, lfree[b])
                v.tensor_copy(wth2[b][:], wtmp2[b][:])
                v.drain()
                VC.inc(v.tensor_tensor(wtl2[b][:], wtmp2[b][:], wth2[b][:],
                                       ALU.subtract))
                vfree2[b] = VC.n
            vcc = VC.n

            @GP
            def _(g, kind=kind, i0=i0, i1=i1, b=b, vcc=vcc):
                g.wait_ge(VC.h, vcc)
                if kind == "w1":
                    SP.inc(g.dma_start(
                        out=w1h_d.rearrange("(H q) (f h) -> q H f h", q=128,
                                            h=128)[:, i1 * 16:(i1 + 1) * 16,
                                                   i0, :],
                        in_=wth2[b][:]), 16)
                else:
                    SP.inc(g.dma_start(
                        out=w2h_d.rearrange("(c p) m -> p c m", p=128)[
                            :, i0 * 2:(i0 + 1) * 2, :], in_=wth2[b][:]), 16)
                    SP.inc(g.dma_start(
                        out=w2l_d.rearrange("(c p) m -> p c m", p=128)[
                            :, i0 * 2:(i0 + 1) * 2, :], in_=wtl2[b][:]), 16)
                spfree[b] = SP.n

            if kind == "w1":
                @SY
                def _(sy, i0=i0, i1=i1, b=b, vcc=vcc):
                    sy.wait_ge(VC.h, vcc)
                    LD.inc(sy.dma_start(
                        out=w1l_d.rearrange("(H q) (f h) -> q H f h", q=128,
                                            h=128)[:, i1 * 16:(i1 + 1) * 16,
                                                   i0, :],
                        in_=wtl2[b][:]), 16)
                lfree[b] = LD.n


    def emit_p12():
        p12e = ctxW.enter_context(nc.sbuf_tensor("p12e", [16, CH], F32))
        sfree = [TE.n, TE.n]
        for k in range(2):
            aa, bb = (a1s, b1s) if k == 0 else (a2s, b2s)
            for nt in range(NT):
                t0 = nt * CH
                for sc8 in range(8):
                    b = sc8 % 2

                    @VE
                    def _(v, t0=t0, sc8=sc8, b=b, aa=aa, bb=bb):
                        v.wait_ge(LD.h, ld_consts)
                        v.wait_ge(TE.h, sfree[b])
                        VC.inc(v.tensor_scalar(
                            wtmp2[b][:, 0:CH], nvec_s[:, t0:t0 + CH],
                            aa[:, sc8:sc8 + 1], bb[:, sc8:sc8 + 1],
                            ALU.mult, ALU.add))
                    vca = VC.n

                    @SE
                    def _(s, b=b, vca=vca):
                        s.wait_ge(VC.h, vca)
                        SC.inc(s.activation(wtmp2[b][:, 0:CH],
                                            wtmp2[b][:, 0:CH], AF.Sin))
                    @TN
                    def _(t, sc8=sc8, b=b, scn=SC.n, k=k, nt=nt):
                        t.wait_ge(SC.h, scn)
                        if sc8 == 0:
                            t.wait_ge(VC.h, p12free[0])
                        TE.inc(t.matmul(pb[7][0:16, :],
                                        cM_s[:, sc8 * 16:(sc8 + 1) * 16],
                                        wtmp2[b][:, 0:CH], start=(sc8 == 0),
                                        stop=(sc8 == 7)))
                    sfree[b] = TE.n

                @VE
                def _(v, ten=TE.n):
                    v.wait_ge(TE.h, ten)
                    VC.inc(v.tensor_copy(p12e[:], pb[7][0:16, :]))
                p12free[0] = VC.n

                @GP
                def _(g, k=k, t0=t0, vcn=VC.n):
                    g.wait_ge(VC.h, vcn)
                    SP.inc(g.dma_start(
                        out=p12_d[:, k * T + t0:k * T + t0 + CH],
                        in_=p12e[:]), 16)

    p12free = [VC.n]
    emit_w12_prep()
    ld_w12 = LD.n
    emit_p12()
    sp_w12 = SP.n
    ctxW.close()

    vc_apply = [0] * 18
    sp_p1_tiles = [0] * 16
    for tt in range(16):
        b = tt % 2
        xt = xt2[b]; xc_ = xc2[b]; xn32 = xn322[b]
        xnh_t = xnh2[b]; xnl_t = xnl2[b]; sm = sm12[b]
        s1_ = sm[:, 0:1]; s2_ = sm[:, 1:2]; mu_ = sm[:, 2:3]
        var_ = sm[:, 3:4]; isd_ = sm[:, 4:5]; s2b_ = sm[:, 5:6]
        nmu_ = sm[:, 6:7]

        @GP
        def _(g, tt=tt, xt=xt):
            if tt > 1:
                g.wait_ge(VC.h, vc_apply[tt])
            SP.inc(g.dma_start(out=xt[:],
                               in_=x_in[tt * 128:(tt + 1) * 128, :]), 16)
        sp_x = SP.n

        @SE
        def _(s, sp_x=sp_x, xt=xt, xn32=xn32, s1_=s1_, s2_=s2_):
            s.wait_ge(SP.h, sp_x)
            s.activation(xn32[:], xt[:], AF.Square, accum_out=s2_)
            SC.inc(s.activation(xn32[:], xt[:], AF.Identity, accum_out=s1_))
        sc_sq = SC.n

        @VE
        def _(v, sc_sq=sc_sq, s1_=s1_, s2_=s2_, mu_=mu_, var_=var_,
              s2b_=s2b_, nmu_=nmu_):
            v.wait_ge(SC.h, sc_sq)
            v.tensor_scalar(mu_, s1_, 1.0 / D, None, ALU.mult)
            v.tensor_scalar(s2b_, s2_, 1.0 / D, 1e-5, ALU.mult, ALU.add)
            v.tensor_scalar(nmu_, mu_, -1.0, None, ALU.mult)
            v.tensor_tensor(var_, mu_, mu_, ALU.mult)
            v.drain()
            v.tensor_tensor(s1_, s2b_, var_, ALU.subtract)
            v.drain()
            VC.inc(v.reciprocal(var_, s1_))
        vc_st = VC.n

        @SE
        def _(s, vc_st=vc_st, isd_=isd_, var_=var_):
            s.wait_ge(VC.h, vc_st)
            SC.inc(s.activation(isd_, var_, AF.Sqrt))
        sc_ap = SC.n

        @VE
        def _(v, tt=tt, sc_ap=sc_ap, xt=xt, xc_=xc_, xn32=xn32,
              xnh_t=xnh_t, xnl_t=xnl_t, nmu_=nmu_, isd_=isd_):
            v.wait_ge(SC.h, sc_ap)
            if tt > 1:
                v.wait_ge(SP.h, sp_p1_tiles[tt - 2])
            v.tensor_scalar(xc_[:], xt[:], nmu_, None, ALU.add)
            v.tensor_scalar(xn32[:], xc_[:], isd_, None, ALU.mult)
            v.tensor_copy(xnh_t[:], xn32[:])
            VC.inc(v.tensor_tensor(xnl_t[:], xn32[:], xnh_t[:],
                                   ALU.subtract))
        vc_apply[tt + 2] = VC.n

        @GP
        def _(g, tt=tt, xnh_t=xnh_t, xnl_t=xnl_t, vcc=VC.n):
            g.wait_ge(VC.h, vcc)
            SP.inc(g.dma_start(out=xnh_d[tt * 128:(tt + 1) * 128, :],
                               in_=xnh_t[:]), 16)
            SP.inc(g.dma_start(out=xnl_d[tt * 128:(tt + 1) * 128, :],
                               in_=xnl_t[:]), 16)
        sp_p1_tiles[tt] = SP.n

    xt = xt2[0]; xc_ = xc2[0]; xn32 = xn322[0]
    xnh_t = xnh2[0]; xnl_t = xnl2[0]
    sm = sm12[0]
    s1_ = sm[:, 0:1]; s2_ = sm[:, 1:2]; mu_ = sm[:, 2:3]
    var_ = sm[:, 3:4]; isd_ = sm[:, 4:5]; s2b_ = sm[:, 5:6]
    nmu_ = sm[:, 6:7]

    ctxP2 = ExitStack()
    def sb2(name, shape, dt=F32):
        return ctxP2.enter_context(nc.sbuf_tensor(name, shape, dt))

    xfm_h = sb2("xfm_h", [128, 8 * CH], BF16)
    xfm_l = sb2("xfm_l", [128, 8 * CH], BF16)
    p1_t = sb2("p1_t", [16, CH]); p2raw = sb2("p2raw", [16, CH])
    c_t = sb2("c_t", [16, CH]); d_t = sb2("d_t", [16, CH])
    e_t = sb2("e_t", [16, CH]); p2_t = sb2("p2_t", [16, CH])
    rt1 = sb2("rt1", [16, CH])
    cx_t = sb2("cx_t", [128, CH])
    cv_t = sb2("cv_t", [128, CH])

    cx_evac = [VC.n]
    vbank_free = {4: VC.n, 5: VC.n, 6: VC.n, 7: VC.n}
    sc_scar = [SC.n]
    vc_p12used = [VC.n]

    for nt in range(NT):
        t0 = nt * CH

        @SY
        def _(sy, nt=nt, t0=t0):
            sy.wait_ge(SP.h, sp_p1_tiles[4 * nt + 3])
            if nt > 0:
                sy.wait_ge(TE.h, TE.n)
            for fc in range(8):
                LD.inc(sy.dma_start_transpose(
                    xfm_h[:, fc * CH:(fc + 1) * CH],
                    xnh_d[t0:t0 + CH, fc * 128:(fc + 1) * 128]), 16)
                LD.inc(sy.dma_start_transpose(
                    xfm_l[:, fc * CH:(fc + 1) * CH],
                    xnl_d[t0:t0 + CH, fc * 128:(fc + 1) * 128]), 16)
        ld_xfm = LD.n

        @GP
        def _(g, t0=t0, nt=nt):
            if nt > 0:
                g.wait_ge(VC.h, vc_p12used[0])
            SP.inc(g.dma_start(out=p1_t[:], in_=p12_d[:, t0:t0 + CH]), 16)
            SP.inc(g.dma_start(out=p2raw[:],
                               in_=p12_d[:, T + t0:T + t0 + CH]), 16)
        sp_p12 = SP.n

        @TN
        def _(t, nt=nt, ld_xfm=ld_xfm):
            t.wait_ge(LD.h, ld_xfm)
            t.wait_ge(VC.h, VC.n)
            for kind in range(3):
                for fc in range(8):
                    xh = xfm_h[:, fc * CH:(fc + 1) * CH]
                    xl = xfm_l[:, fc * CH:(fc + 1) * CH]
                    kh = wk_h[:, fc, kind * 16:(kind + 1) * 16]
                    kl = wk_l[:, fc, kind * 16:(kind + 1) * 16]
                    t.matmul(pb[kind][0:16, :], kh, xh,
                             start=(fc == 0), stop=False)
                    t.matmul(pb[kind][0:16, :], kh, xl, start=False,
                             stop=False)
                    TE.inc(t.matmul(pb[kind][0:16, :], kl, xh,
                                    start=False, stop=(fc == 7)))
        te_p2m = TE.n

        @VE
        def _(v, te_p2m=te_p2m, sp_p12=sp_p12):
            v.wait_ge(TE.h, te_p2m)
            v.wait_ge(SP.h, sp_p12)
            v.tensor_tensor(rt1[:], pb[0][0:16, :], p1_t[:], ALU.add)
            v.tensor_scalar(d_t[:], pb[1][0:16, :], 20.0, -20.0, ALU.min,
                            ALU.max)
            v.tensor_scalar(e_t[:], pb[2][0:16, :], 20.0, -20.0, ALU.min,
                            ALU.max)
            v.tensor_scalar(p2_t[:], p2raw[:], 20.0, -20.0, ALU.min,
                            ALU.max)
            VC.inc(v.tensor_scalar(c_t[:], rt1[:], 20.0, -20.0, ALU.min,
                                   ALU.max))
        vc_clip = VC.n
        vc_p12used[0] = VC.n

        @SE
        def _(s, vc_clip=vc_clip):
            s.wait_ge(VC.h, vc_clip)
            s.activation(c_t[:], c_t[:], AF.Exp)
            s.activation(d_t[:], d_t[:], AF.Exp)
            s.activation(e_t[:], e_t[:], AF.Exp)
            SC.inc(s.activation(p2_t[:], p2_t[:], AF.Exp))
        sc_exp = SC.n

        @VE
        def _(v, t0=t0, sc_exp=sc_exp):
            v.wait_ge(SC.h, sc_exp)
            v.reciprocal(rt1[:], d_t[:])
            v.tensor_tensor(ss_t[0:16, :], p2_t[:], e_t[:], ALU.mult)
            v.tensor_tensor(r_s[:, t0:t0 + CH], ss_t[0:16, :], rt1[:],
                            ALU.mult)
            VC.inc(v.tensor_tensor_scan(Z_s[:, t0:t0 + CH], onesc_s[0:16, 0:1].to_broadcast([16, CH]),
                                        c_t[:], Zcar[:, 0:1], ALU.mult,
                                        ALU.add))
        vc_zr = VC.n

        @SE
        def _(s, t0=t0, vc_zr=vc_zr):
            s.wait_ge(VC.h, vc_zr)
            SC.inc(s.copy(Zcar[:], Z_s[:, t0 + CH - 1:t0 + CH]))
        sc_zcar = SC.n

        for oc in range(8):
            bank = 4 + (oc % 4)

            @TN
            def _(t, nt=nt, oc=oc, bank=bank, vc_zr=vc_zr):
                t.wait_ge(VC.h, vc_zr)
                t.wait_ge(VC.h, cx_evac[0])
                t.matmul(pb[0][:], bsel_s[:, oc * 128:(oc + 1) * 128],
                         c_t[:], start=True, stop=True)
                t.wait_ge(VC.h, vbank_free[bank])
                for fc in range(8):
                    xh = xfm_h[:, fc * CH:(fc + 1) * CH]
                    xl = xfm_l[:, fc * CH:(fc + 1) * CH]
                    wvh = wv_h[:, fc * D + oc * 128:fc * D + (oc + 1) * 128]
                    wvl = wv_l[:, fc * D + oc * 128:fc * D + (oc + 1) * 128]
                    t.matmul(pb[bank][:], wvh, xh, start=(fc == 0),
                             stop=False)
                    t.matmul(pb[bank][:], wvh, xl, start=False, stop=False)
                    TE.inc(t.matmul(pb[bank][:], wvl, xh, start=False,
                                    stop=(fc == 7)))
            te_v = TE.n

            @VE
            def _(v, nt=nt, oc=oc, bank=bank, te_v=te_v):
                v.wait_ge(TE.h, te_v)
                v.wait_ge(SP.h, SP.n)
                v.wait_ge(SC.h, sc_scar[0])
                VC.inc(v.tensor_copy(cx_t[:], pb[0][:]))
                cx_evac[0] = VC.n
                v.tensor_tensor(cv_t[:], pb[bank][:], cx_t[:], ALU.mult)
                v.tensor_copy(vv_t[:], pb[bank][:])
                VC.inc(v.tensor_tensor_scan(ss_t[:], onesc_s[:, 0:1].to_broadcast([128, CH]), cv_t[:],
                                            scarL[oc][:, 0:1], ALU.mult,
                                            ALU.add))
                vbank_free[bank] = VC.n
            vc_soc = VC.n

            @SE
            def _(s, oc=oc, vc_soc=vc_soc):
                s.wait_ge(VC.h, vc_soc)
                SC.inc(s.copy(scarL[oc][:], ss_t[:, CH - 1:CH]))
            sc_scar[0] = SC.n

            @GP
            def _(g, nt=nt, oc=oc, t0=t0, vc_soc=vc_soc):
                g.wait_ge(VC.h, vc_soc)
                SP.inc(g.dma_start(out=s_d[oc, :, t0:t0 + CH], in_=ss_t[:]),
                       16)
                SP.inc(g.dma_start(out=v_d[oc, :, t0:t0 + CH], in_=vv_t[:]),
                       16)
    ctxP2.close()

    ctxP4 = ExitStack()
    def sb4(name, shape, dt=F32):
        return ctxP4.enter_context(nc.sbuf_tensor(name, shape, dt))

    o_h = sb4("o_h", [128, 8 * CH], BF16)
    o_l = sb4("o_l", [128, 8 * CH], BF16)
    rx_t = sb4("rx_t", [128, CH])
    dx_t = sb4("dx_t", [128, CH])
    o32 = sb4("o32", [128, CH])
    tp4 = sb4("tp4", [128, CH])
    x1t = sb4("x1t", [128, D])
    xrow = sb4("xrow", [128, D])
    agcar_s = sb4("agcar_s", [8, 1040])

    @GP
    def _(g):
        g.wait_ge(VC.h, VC.n)
        g.wait_ge(SC.h, SC.n)
        for oc in range(8):
            SP.inc(g.dma_start(out=agcar_i[0:1, oc * 128:(oc + 1) * 128],
                               in_=scarL[oc][:, 0:1],
                               allow_slow_non_contiguous=True), 16)
        SP.inc(g.dma_start(out=agcar_i[0:1, 1024:1040], in_=Zcar[:, 0:1],
                           allow_slow_non_contiguous=True), 16)
        g.wait_ge(SP.h, SP.n)
        CC.inc(g.collective_compute(
            "AllGather", ALU.bypass, replica_groups=[list(range(NC))],
            ins=[agcar_i[:, :]], outs=[agcar_o[:, :]]))
        g.wait_ge(CC.h, CC.n)
        SP.inc(g.dma_start(out=agcar_s[:], in_=agcar_o[:, :]), 16)
    sp_p3 = SP.n

    @TN
    def _(t):
        t.wait_ge(SP.h, sp_p3)
        for oc in range(8):
            t.matmul(pb[3][:, oc:oc + 1],
                     agcar_s[:, oc * 128:(oc + 1) * 128], seqm_s[:, :],
                     start=(oc == 0), stop=(oc == 7))
        TE.inc(t.matmul(pb[2][0:16, 0:1], agcar_s[:, 1024:1040],
                        seqm_s[:, :], start=True, stop=True))
    te_p3 = TE.n

    @VE
    def _(v, te_p3=te_p3):
        v.wait_ge(TE.h, te_p3)
        v.tensor_copy(car2[:], pb[3][:, 0:8])
        v.tensor_copy(zc2[:], pb[2][0:16, 0:1])
        v.drain()
        v.tensor_scalar(r_s2[:], Z_s[:], zc2[:, 0:1], None, ALU.add)
        v.tensor_tensor(dinv_s[:], r_s2[:], r_s[:], ALU.mult)
        v.tensor_scalar(r_s2[:], dinv_s[:], 1.0, None, ALU.add)
        VC.inc(v.reciprocal(dinv_s[:], r_s2[:]))
    vc_p3 = VC.n


    rx_evac = [VC.n]
    o_used = [TE.n]
    ybank_free = {4: VC.n, 5: VC.n, 6: VC.n, 7: VC.n}

    for nt in range(NT):
        t0 = nt * CH
        for oc in range(8):
            @GP
            def _(g, oc=oc, t0=t0):
                g.wait_ge(VC.h, VC.n)
                SP.inc(g.dma_start(out=ss_t[:], in_=s_d[oc, :, t0:t0 + CH]),
                       16)
                SP.inc(g.dma_start(out=vv_t[:], in_=v_d[oc, :, t0:t0 + CH]),
                       16)
            sp_sv = SP.n

            @TN
            def _(t, oc=oc, t0=t0):
                t.wait_ge(VC.h, max(vc_p3, rx_evac[0]))
                t.matmul(pb[0][:], bsel_s[:, oc * 128:(oc + 1) * 128],
                         r_s[:, t0:t0 + CH], start=True, stop=True)
                TE.inc(t.matmul(pb[1][:], bsel_s[:, oc * 128:(oc + 1) * 128],
                                dinv_s[:, t0:t0 + CH], start=True, stop=True))
            te_rd = TE.n

            @VE
            def _(v, oc=oc, te_rd=te_rd, sp_sv=sp_sv):
                v.wait_ge(TE.h, te_rd)
                v.wait_ge(SP.h, sp_sv)
                v.wait_ge(TE.h, o_used[0])
                v.tensor_copy(rx_t[:], pb[0][:])
                VC.inc(v.tensor_copy(dx_t[:], pb[1][:]))
                rx_evac[0] = VC.n
                v.tensor_scalar(tp4[:], ss_t[:], car2[:, oc:oc + 1], None,
                                ALU.add)
                v.tensor_tensor(o32[:], tp4[:], rx_t[:], ALU.mult)
                v.tensor_tensor(tp4[:], o32[:], vv_t[:], ALU.add)
                v.tensor_tensor(o32[:], tp4[:], dx_t[:], ALU.mult)
                v.tensor_copy(o_h[:, oc * CH:(oc + 1) * CH], o32[:])
                VC.inc(v.tensor_tensor(o_l[:, oc * CH:(oc + 1) * CH], o32[:],
                                       o_h[:, oc * CH:(oc + 1) * CH],
                                       ALU.subtract))
        vc_o = VC.n

        for q in range(4):
            tt = nt * 4 + q

            @GP
            def _(g, tt=tt):
                g.wait_ge(VC.h, VC.n)
                SP.inc(g.dma_start(out=xrow[:],
                                   in_=x_in[tt * 128:(tt + 1) * 128, :]), 16)
            sp_xr = SP.n

            @TN
            def _(t, q=q, vc_o=vc_o):
                t.wait_ge(VC.h, vc_o)
                for wc in range(2):
                    bank = 4 + (q * 2 + wc) % 4
                    t.wait_ge(VC.h, ybank_free[bank])
                    for oc in range(8):
                        oh = o_h[:, oc * CH + q * 128:oc * CH + (q + 1) * 128]
                        ol = o_l[:, oc * CH + q * 128:oc * CH + (q + 1) * 128]
                        owh = ow_h[:, oc * D + wc * 512:oc * D + (wc + 1) * 512]
                        owl = ow_l[:, oc * D + wc * 512:oc * D + (wc + 1) * 512]
                        t.matmul(pb[bank][:], oh, owh, start=(oc == 0),
                                 stop=False)
                        t.matmul(pb[bank][:], oh, owl, start=False,
                                 stop=False)
                        TE.inc(t.matmul(pb[bank][:], ol, owh, start=False,
                                        stop=(oc == 7)))
            te_y = TE.n

            @VE
            def _(v, q=q, te_y=te_y, sp_xr=sp_xr):
                v.wait_ge(TE.h, te_y)
                v.wait_ge(SP.h, SP.n)
                for wc in range(2):
                    bank = 4 + (q * 2 + wc) % 4
                    VC.inc(v.tensor_tensor(x1t[:, wc * 512:(wc + 1) * 512],
                                           pb[bank][:],
                                           xrow[:, wc * 512:(wc + 1) * 512],
                                           ALU.add))
                    ybank_free[bank] = VC.n
            vc_x1 = VC.n

            @SE
            def _(s, vc_x1=vc_x1):
                s.wait_ge(VC.h, vc_x1)
                s.activation(xn32[:], x1t[:], AF.Square,
                             accum_out=s2_)
                SC.inc(s.activation(xn32[:], x1t[:], AF.Identity,
                                    accum_out=s1_))
            sc_sq2 = SC.n

            @VE
            def _(v, sc_sq2=sc_sq2):
                v.wait_ge(SC.h, sc_sq2)
                v.tensor_scalar(mu_, s1_, 1.0 / D, None, ALU.mult)
                v.tensor_scalar(s2b_, s2_, 1.0 / D, 1e-5, ALU.mult,
                                ALU.add)
                v.tensor_scalar(nmu_, mu_, -1.0, None, ALU.mult)
                v.tensor_tensor(var_, mu_, mu_, ALU.mult)
                v.drain()
                v.tensor_tensor(s1_, s2b_, var_, ALU.subtract)
                v.drain()
                VC.inc(v.reciprocal(var_, s1_))
            vc_st2 = VC.n

            @SE
            def _(s, vc_st2=vc_st2):
                s.wait_ge(VC.h, vc_st2)
                SC.inc(s.activation(isd_, var_, AF.Sqrt))
            sc_ap2 = SC.n

            @VE
            def _(v, sc_ap2=sc_ap2):
                v.wait_ge(SC.h, sc_ap2)
                v.wait_ge(SP.h, SP.n)
                v.tensor_scalar(xc_[:], x1t[:], nmu_, None, ALU.add)
                v.tensor_scalar(xn32[:], xc_[:], isd_, None, ALU.mult)
                v.tensor_copy(xnh_t[:], xn32[:])
                VC.inc(v.tensor_tensor(xnl_t[:], xn32[:], xnh_t[:],
                                       ALU.subtract))
            vc_sp2 = VC.n

            @GP
            def _(g, tt=tt, vc_sp2=vc_sp2):
                g.wait_ge(VC.h, vc_sp2)
                SP.inc(g.dma_start(out=x1_d[tt * 128:(tt + 1) * 128, :],
                                   in_=x1t[:]), 16)
                SP.inc(g.dma_start(out=agx_i[tt * 128:(tt + 1) * 128, 0:D],
                                   in_=xnh_t[:]), 16)
                SP.inc(g.dma_start(out=agx_i[tt * 128:(tt + 1) * 128,
                                             D:2 * D], in_=xnl_t[:]), 16)
        o_used[0] = TE.n
    sp_p4 = SP.n

    @GP
    def _(g):
        g.wait_ge(SP.h, sp_p4)
        CC.inc(g.collective_compute(
            "AllGather", ALU.bypass, replica_groups=[list(range(NC))],
            ins=[agx_i[:, :]], outs=[agx_o[:, :]]))
    cc_agx = CC.n

    ctxP4.close()
    ctxA.close()

    ctxG = ExitStack()
    def sbg(name, shape, dt=F32):
        return ctxG.enter_context(nc.sbuf_tensor(name, shape, dt))

    xg_h = sbg("xg_h", [128, 8 * T], BF16)
    xg_l = sbg("xg_l", [128, 8 * T], BF16)
    ohst = sbg("ohst", [128, 2 * 16 * 8])
    plst = sbg("plst", [128, 32])
    lg_t = sbg("lg_t", [128, 128])
    es_t = sbg("es_t", [128, 128])
    m8_t = sbg("m8_t", [128, 16])
    ssum_t = sbg("ssum_t", [128, 16])
    rec_t = sbg("rec_t", [128, 16])
    sm_w = sbg("sm_w", [128, 128])
    gt_w = sbg("gt_w", [128, 1024])
    eq_w = sbg("eq_w", [128, 1024])
    eqm_w = sbg("eqm_w", [128, 1024])
    rks_w = sbg("rks_w", [128, 1024])
    rk_w = sbg("rk_w", [128, 128])
    a1t = sbg("a1t", [128, 8]); a2t = sbg("a2t", [128, 8])
    voff = sbg("voff", [128, 256])
    selo = sbg("selo", [128, 256]); sele = sbg("sele", [128, 256])
    sel5 = sbg("sel5", [128, 256]); cmpq = sbg("cmpq", [128, 256])
    pos_add = sbg("pos_add", [128, 32]); esum = sbg("esum", [128, 32])
    posga = sbg("posga", [128, 32]); valid_w = sbg("valid_w", [128, 32])
    e512s = sbg("e512s", [128, 32]); q_w = sbg("q_w", [128, 32])
    posc_w = sbg("posc_w", [128, 32]); cdif = sbg("cdif", [128, 32])
    w32a = sbg("w32a", [128, 32]); w32b = sbg("w32b", [128, 32])
    s12 = sbg("s12", [128, 16]); rec12 = sbg("rec12", [128, 16])
    g1a = sbg("g1a", [128, 16]); g2a = sbg("g2a", [128, 16])

    @SY
    def _(sy):
        sy.wait_ge(SP.h, sp_p4)
        for fc in range(8):
            LD.inc(sy.dma_start_transpose(
                xg_h[:, fc * T:(fc + 1) * T],
                agx_i[:, fc * 128:(fc + 1) * 128]), 16)
            LD.inc(sy.dma_start_transpose(
                xg_l[:, fc * T:(fc + 1) * T],
                agx_i[:, D + fc * 128:D + (fc + 1) * 128]), 16)
    ld_xg = LD.n

    lg_read = [VC.n]
    for tt in range(16):
        @TN
        def _(t, tt=tt, ld_xg=ld_xg):
            t.wait_ge(LD.h, ld_xg)
            t.wait_ge(VC.h, lg_read[0])
            for fc in range(8):
                xh = xg_h[:, fc * T + tt * 128:fc * T + (tt + 1) * 128]
                xl = xg_l[:, fc * T + tt * 128:fc * T + (tt + 1) * 128]
                t.matmul(pb[0][:, 0:8], xh, wg_h[:, fc, :], start=(fc == 0),
                         stop=False)
                t.matmul(pb[0][:, 0:8], xh, wg_l[:, fc, :], start=False,
                         stop=False)
                TE.inc(t.matmul(pb[0][:, 0:8], xl, wg_h[:, fc, :],
                                start=False, stop=(fc == 7)))
        te_lg = TE.n

        @VE
        def _(v, tt=tt, te_lg=te_lg):
            v.wait_ge(TE.h, te_lg)
            VC.inc(v.tensor_copy(lg_t[:, tt * 8:tt * 8 + 8], pb[0][:, 0:8]))
            lg_read[0] = VC.n

    lg3 = lg_t.ap().rearrange("p (t k) -> p t k", k=8)
    es3 = es_t.ap().rearrange("p (t k) -> p t k", k=8)
    sm3 = sm_w.ap().rearrange("p (t k) -> p t k", k=8)

    @VE
    def _(v):
        v.tensor_reduce(m8_t[:], lg3, AX.X, ALU.max)
        v.drain()
        VC.inc(v.tensor_tensor(
            es3, lg3, m8_t[:, :, None].to_broadcast([128, 16, 8]),
            ALU.subtract))
    vc_es = VC.n

    @SE
    def _(s, vc_es=vc_es):
        s.wait_ge(VC.h, vc_es)
        SC.inc(s.activation(es_t[:], es_t[:], AF.Exp))
    sc_ex = SC.n

    @VE
    def _(v, sc_ex=sc_ex):
        v.wait_ge(SC.h, sc_ex)
        v.tensor_reduce(ssum_t[:], es3, AX.X, ALU.add)
        v.drain()
        v.reciprocal(rec_t[:], ssum_t[:])
        v.drain()
        v.tensor_tensor(sm3, es3,
                        rec_t[:, :, None].to_broadcast([128, 16, 8]),
                        ALU.mult)
        v.drain()
        smA = sm_w.ap().rearrange("p (t k) -> p t k", k=8)[
            :, :, None, :].to_broadcast([128, 16, 8, 8])
        smB = sm_w.ap().rearrange("p (t j) -> p t j", j=8)[
            :, :, :, None].to_broadcast([128, 16, 8, 8])
        gt4 = gt_w.ap().rearrange("p (t j k) -> p t j k", j=8, k=8)
        eq4 = eq_w.ap().rearrange("p (t j k) -> p t j k", j=8, k=8)
        v.tensor_tensor(gt4, smA, smB, ALU.is_gt)
        v.tensor_tensor(eq4, smA, smB, ALU.is_equal)
        v.tensor_tensor(eqm_w[:], eq_w[:], ltm_s[:, :], ALU.mult)
        v.tensor_tensor(rks_w[:], gt_w[:], eqm_w[:], ALU.add)
        v.tensor_reduce(rk_w[:],
                        rks_w.ap().rearrange("p (t j k) -> p t j k", j=8,
                                             k=8).rearrange(
                            "p t j k -> p (t j) k"),
                        AX.X, ALU.add)
        v.tensor_scalar(ohst[:, 0:128], rk_w[:], 0.0, None, ALU.is_equal)
        v.tensor_scalar(ohst[:, 128:256], rk_w[:], 1.0, None, ALU.is_equal)
        v.tensor_tensor(lg_t[:], ohst[:, 0:128], sm_w[:], ALU.mult)
        v.tensor_tensor(es_t[:], ohst[:, 128:256], sm_w[:], ALU.mult)
        v.tensor_reduce(v1st[:],
                        lg_t.ap().rearrange("p (t k) -> p t k", k=8),
                        AX.X, ALU.add)
        VC.inc(v.tensor_reduce(v2st[:],
                               es_t.ap().rearrange("p (t k) -> p t k", k=8),
                               AX.X, ALU.add))
    vc_top = VC.n

    for tt in range(16):
        @TN
        def _(t, tt=tt, vc_top=vc_top):
            t.wait_ge(VC.h, max(vc_top, VC.n))
            oh1s = ohst[:, tt * 8:tt * 8 + 8]
            oh2s = ohst[:, 128 + tt * 8:128 + tt * 8 + 8]
            t.matmul(pb[4][:, 0:8], tri_s[:], oh1s, start=True, stop=False)
            t.matmul(pb[4][:, 0:8], ones1_s[:], cnt_s[0:1, 0:8], start=False,
                     stop=True)
            t.matmul(pb[5][:, 0:8], tri_s[:], oh2s, start=True, stop=False)
            t.matmul(pb[5][:, 0:8], ones1_s[:], cnt_s[0:1, 8:16], start=False,
                     stop=True)
            t.matmul(pb[6][0:1, 0:8], onesc_s[:], oh1s, start=(tt == 0),
                     stop=True, skip_group_check=True)
            TE.inc(t.matmul(pb[6][0:1, 8:16], onesc_s[:], oh2s,
                            start=False, stop=True, skip_group_check=True))
        te_pos = TE.n

        @VE
        def _(v, tt=tt, te_pos=te_pos):
            v.wait_ge(TE.h, te_pos)
            v.tensor_tensor(a1t[:], ohst[:, tt * 8:tt * 8 + 8],
                            pb[4][:, 0:8], ALU.mult)
            v.tensor_tensor(a2t[:], ohst[:, 128 + tt * 8:128 + tt * 8 + 8],
                            pb[5][:, 0:8], ALU.mult)
            v.drain()
            v.reduce_sum(plst[:, tt:tt + 1], a1t[:], axis=AX.X)
            v.reduce_sum(plst[:, 16 + tt:16 + tt + 1], a2t[:], axis=AX.X)
            VC.inc(v.tensor_copy(cnt_s[:], pb[6][0:1, 0:16]))
    vc_cnt = VC.n

    @GP
    def _(g, vc_cnt=vc_cnt):
        g.wait_ge(VC.h, vc_cnt)
        SP.inc(g.dma_start(out=agc_i[:, :],
                           in_=cnt_s.ap().rearrange("p (a b) -> p a b", a=2)),
               16)
        g.wait_ge(SP.h, SP.n)
        CC.inc(g.collective_compute(
            "AllGather", ALU.bypass, replica_groups=[list(range(NC))],
            ins=[agc_i[:, :]], outs=[agc_o[:, :]]))
        g.wait_ge(CC.h, CC.n)
        SP.inc(g.dma_start(out=agc_s[:], in_=agc_o[:, :]), 16)
    sp_agc = SP.n

    @TN
    def _(t):
        t.wait_ge(SP.h, sp_agc)
        t.matmul(pb[7][:, 0:8], offmr0_s[:], agc_s[:], start=True, stop=False)
        TE.inc(t.matmul(pb[7][:, 8:16], offmr1_s[:], agc_s[:], start=False,
                        stop=True))
    te_off = TE.n

    @VE
    def _(v, te_off=te_off):
        v.wait_ge(TE.h, te_off)
        v.tensor_copy(voff.ap().rearrange("p (k t e) -> p k t e", k=2, t=16,
                                          e=8)[:, 0:1, :, :],
                      pb[7][:, 0:8][:, None, :].to_broadcast([128, 16, 8])[
                          :, None, :, :].to_broadcast([128, 1, 16, 8]))
        VC.inc(v.tensor_copy(
            voff.ap().rearrange("p (k t e) -> p k t e", k=2, t=16,
                                e=8)[:, 1:2, :, :],
            pb[7][:, 8:16][:, None, :].to_broadcast([128, 16, 8])[
                :, None, :, :].to_broadcast([128, 1, 16, 8])))
    vc_voff = VC.n

    oh32 = ohst.ap().rearrange("p (c e) -> p c e", e=8)
    @VE
    def _(v, vc_voff=vc_voff):
        v.wait_ge(VC.h, vc_voff)
        v.drain()
        v.tensor_tensor(selo[:], ohst[:], voff[:], ALU.mult)
        v.tensor_tensor(sele[:], ohst[:], evec_s[:], ALU.mult)
        v.tensor_reduce(pos_add[:],
                        selo.ap().rearrange("p (c e) -> p c e", e=8),
                        AX.X, ALU.add)
        v.tensor_reduce(esum[:],
                        sele.ap().rearrange("p (c e) -> p c e", e=8),
                        AX.X, ALU.add)
        v.tensor_tensor(posga[:], pos_add[:], plst[:], ALU.add)
        v.tensor_tensor(sel5[:], ohst[:], e512_s[:], ALU.mult)
        v.tensor_scalar(valid_w[:], posga[:], 4096.0, None, ALU.is_lt)
        v.tensor_reduce(e512s[:],
                        sel5.ap().rearrange("p (c e) -> p c e", e=8),
                        AX.X, ALU.add)
        v.tensor_tensor(w32a[:], esum[:], posga[:], ALU.add)
        v.tensor_tensor(s12[:], v1st[:], v2st[:], ALU.add)
        v.tensor_scalar(w32b[:], w32a[:], float(TRASH), None, ALU.subtract)
        v.reciprocal(rec12[:], s12[:])
        v.tensor_tensor(w32a[:], w32b[:], valid_w[:], ALU.mult)
        v.tensor_tensor(g1a[:], v1st[:], rec12[:], ALU.mult)
        v.tensor_scalar(w32b[:], w32a[:], float(TRASH), None, ALU.add)
        v.tensor_tensor(g2a[:], v2st[:], rec12[:], ALU.mult)
        v.tensor_copy(sct[:], w32b[:])
        v.tensor_scalar(posc_w[:], posga[:], 4095.0, None, ALU.min)
        v.tensor_tensor(g1st[:], g1a[:], valid_w[:, 0:16], ALU.mult)
        v.tensor_tensor(g2st[:], g2a[:], valid_w[:, 16:32], ALU.mult)
        v.tensor_tensor(cmpq.ap().rearrange("p (c e) -> p c e", e=8),
                        posc_w[:, :, None].to_broadcast([128, 32, 8]),
                        thr_s.ap().rearrange("p (c e) -> p c e", e=8),
                        ALU.is_ge)
        v.drain()
        v.tensor_reduce(q_w[:],
                        cmpq.ap().rearrange("p (c e) -> p c e", e=8),
                        AX.X, ALU.add)
        v.drain()
        v.tensor_scalar(w32a[:], q_w[:], 512.0, None, ALU.mult)
        v.tensor_scalar(w32b[:], q_w[:], 4096.0, None, ALU.mult)
        v.tensor_tensor(cdif[:], posc_w[:], w32a[:], ALU.subtract)
        v.tensor_tensor(w32a[:], w32b[:], e512s[:], ALU.add)
        v.drain()
        v.tensor_tensor(w32b[:], w32a[:], cdif[:], ALU.add)
        v.drain()
        v.tensor_tensor(w32a[:], w32b[:], valid_w[:], ALU.mult)
        v.drain()
        VC.inc(v.tensor_copy(ybst[:], w32a[:]))
    vc_route = VC.n

    @GP
    def _(g, vc_route=vc_route):
        g.wait_ge(VC.h, vc_route)
        SP.inc(g.dma_start(
            out=invl_d.rearrange("(p c) one -> p (c one)", p=128),
            in_=zer_s[:]), 16)
        g.wait_ge(SP.h, SP.n)
        for col in range(32):
            SP.inc(g.indirect_dma_start(
                out=invl_d[:, :],
                out_offset=IndirectOffsetOnAxis(sct[:, col:col + 1], 0),
                in_=gtok_s[:, col % 16:col % 16 + 1], in_offset=None), 16)
        g.wait_ge(SP.h, SP.n)
        CC.inc(g.collective_compute(
            "AllReduce", ALU.add, replica_groups=[list(range(NC))],
            ins=[invl_d[:, :]], outs=[invg_d[:, :]]))
        g.wait_ge(CC.h, CC.n)
        for col in range(32):
            SP.inc(g.indirect_dma_start(
                out=invt[:, col:col + 1], out_offset=None,
                in_=invg_d[:, :],
                in_offset=IndirectOffsetOnAxis(einv_s[:, col:col + 1], 0)),
                16)
    sp_inv = SP.n

    ctxG.close()

    ctxF = ExitStack()
    def sbf(name, shape, dt=F32):
        return ctxF.enter_context(nc.sbuf_tensor(name, shape, dt))

    h_h = sbf("h_h", [128, 32 * CH], BF16)
    h_l = sbf("h_l", [128, 32 * CH], BF16)
    xf_h2 = [sbf(f"xf_h{i}", [128, 8 * CH], BF16) for i in range(2)]
    xf_l2 = [sbf(f"xf_l{i}", [128, 8 * CH], BF16) for i in range(2)]
    w1s_h = [sbf(f"w1s_h{i}", [128, 8, 128], BF16) for i in range(2)]
    w1s_l = [sbf(f"w1s_l{i}", [128, 8, 128], BF16) for i in range(2)]
    w2s_h = [sbf(f"w2s_h{i}", [128, D], BF16) for i in range(2)]
    w2s_l = [sbf(f"w2s_l{i}", [128, D], BF16) for i in range(2)]
    dsb2 = [sbf(f"dsb{i}", [128, 4 * 2 * D], BF16) for i in range(2)]
    h32a = sbf("h32a", [128, CH])
    h32b = sbf("h32b", [128, CH])
    yt_ = sbf("yt_", [128, CH])
    invf = sbf("invf", [128, 32])
    invg2 = sbf("invg2", [128, 32])

    @VE
    def _(v):
        v.wait_ge(SP.h, sp_inv)
        v.tensor_copy(invf[:], invt[:])
        v.drain()
        v.tensor_scalar(invg2[:], invf[:], -1.0, 0.0, ALU.add, ALU.max)
        v.drain()
        VC.inc(v.tensor_copy(invt[:], invg2[:]))
    vc_invt = VC.n

    xf_free = [TE.n, TE.n]
    w1_free = [TE.n, TE.n]
    w2_free = [TE.n, TE.n]
    h_free = [SC.n] * 4
    hsplit = [VC.n, VC.n]
    y2_free = [VC.n] * 8
    sp_disp_a = [0] * TC
    ld_xf_a = [0] * TC

    def emit_gath(tc):
        bb = tc % 2

        @GP
        def _(g, tc=tc, bb=bb):
            if tc == 0:
                g.wait_ge(VC.h, vc_invt)
                g.wait_ge(CC.h, cc_agx)
            if tc >= 2:
                g.wait_ge(LD.h, ld_xf_a[tc - 2])
            for q in range(4):
                SP.inc(g.indirect_dma_start(
                    out=dsb2[bb][:, q * 2 * D:(q + 1) * 2 * D],
                    out_offset=None,
                    in_=agx_o[:, :],
                    in_offset=IndirectOffsetOnAxis(
                        invt[:, tc * 4 + q:tc * 4 + q + 1], 0)), 16)
        sp_disp_a[tc] = SP.n

    def emit_transp(tc):
        bb = tc % 2

        @SY
        def _(sy, tc=tc, bb=bb):
            sy.wait_ge(SP.h, sp_disp_a[tc])
            sy.wait_ge(TE.h, xf_free[bb])
            for q in range(4):
                for fc in range(8):
                    LD.inc(sy.dma_start_transpose(
                        xf_h2[bb][:, fc * CH + q * 128:
                                  fc * CH + (q + 1) * 128],
                        dsb2[bb][:, q * 2 * D + fc * 128:
                                 q * 2 * D + (fc + 1) * 128]), 16)
                    LD.inc(sy.dma_start_transpose(
                        xf_l2[bb][:, fc * CH + q * 128:
                                  fc * CH + (q + 1) * 128],
                        dsb2[bb][:, q * 2 * D + D + fc * 128:
                                 q * 2 * D + D + (fc + 1) * 128]), 16)
        ld_xf_a[tc] = LD.n

    emit_gath(0)
    emit_transp(0)
    for tc in range(TC):
        s0 = tc * CH
        bb = tc % 2
        xf_h = xf_h2[bb]
        xf_l = xf_l2[bb]
        if tc + 1 < TC:
            emit_gath(tc + 1)
            emit_transp(tc + 1)
        ld_xf = ld_xf_a[tc]

        for hc in range(32):
            buf = hc % 2
            bank = hc % 4

            @SE
            def _(sy, tc=tc, hc=hc, buf=buf):
                if tc == 0 and hc == 0:
                    sy.wait_ge(SP.h, sp_w12)
                    sy.wait_ge(LD.h, ld_w12)
                sy.wait_ge(TE.h, w1_free[buf])
                WL.inc(sy.dma_start(
                    out=w1s_h[buf][:],
                    in_=w1h_d[hc * 128:(hc + 1) * 128, :]), 16)
                WL.inc(sy.dma_start(
                    out=w1s_l[buf][:],
                    in_=w1l_d[hc * 128:(hc + 1) * 128, :]), 16)
            ld_w1 = WL.n

            @TN
            def _(t, tc=tc, hc=hc, buf=buf, bank=bank, ld_w1=ld_w1,
                  ld_xf=ld_xf, bb=bb, xf_h=xf_h, xf_l=xf_l):
                t.wait_ge(WL.h, ld_w1)
                if hc == 0:
                    t.wait_ge(LD.h, ld_xf)
                    if tc == 0:
                        t.wait_ge(VC.h, vc_route)
                t.wait_ge(SC.h, h_free[bank])
                if tc > 0 and hc < 4:
                    t.wait_ge(VC.h, y2_free[bank])
                for fc in range(8):
                    xh = xf_h[:, fc * CH:(fc + 1) * CH]
                    xl = xf_l[:, fc * CH:(fc + 1) * CH]
                    t.matmul(pb[bank][:], w1s_h[buf][:, fc, :], xh,
                             start=(fc == 0), stop=False)
                    t.matmul(pb[bank][:], w1s_h[buf][:, fc, :], xl,
                             start=False, stop=False)
                    TE.inc(t.matmul(pb[bank][:], w1s_l[buf][:, fc, :], xh,
                                    start=False, stop=(fc == 7)))
                w1_free[buf] = TE.n
                if hc == 31:
                    xf_free[bb] = TE.n
            te_h1 = TE.n

            @SE
            def _(s, hc=hc, bank=bank, te_h1=te_h1):
                s.wait_ge(TE.h, te_h1)
                s.wait_ge(VC.h, hsplit[hc % 2])
                h32 = h32a if hc % 2 == 0 else h32b
                SC.inc(s.activation(h32[:], pb[bank][:], AF.Relu,
                                    bias=b1m_s[:, hc:hc + 1]))
                h_free[bank] = SC.n
            sc_h = SC.n

            @VE
            def _(v, hc=hc, sc_h=sc_h):
                v.wait_ge(SC.h, sc_h)
                h32 = h32a if hc % 2 == 0 else h32b
                v.tensor_copy(h_h[:, hc * CH:(hc + 1) * CH], h32[:])
                VC.inc(v.tensor_tensor(h_l[:, hc * CH:(hc + 1) * CH], h32[:],
                                       h_h[:, hc * CH:(hc + 1) * CH],
                                       ALU.subtract))
                hsplit[hc % 2] = VC.n
        vc_hdone = VC.n

        for hc in range(32):
            buf = hc % 2

            @SE
            def _(sy, hc=hc, buf=buf):
                sy.wait_ge(TE.h, w2_free[buf])
                WL.inc(sy.dma_start(
                    out=w2s_h[buf][:],
                    in_=w2h_d[hc * 128:(hc + 1) * 128, :]), 16)
                WL.inc(sy.dma_start(
                    out=w2s_l[buf][:],
                    in_=w2l_d[hc * 128:(hc + 1) * 128, :]), 16)
            ld_w2 = WL.n

            @TN
            def _(t, hc=hc, buf=buf, ld_w2=ld_w2, vc_hdone=vc_hdone):
                t.wait_ge(WL.h, ld_w2)
                if hc == 0:
                    t.wait_ge(VC.h, vc_hdone)
                    for b8 in range(8):
                        t.wait_ge(VC.h, y2_free[b8])
                for q in range(4):
                    hh = h_h[:, hc * CH + q * 128:hc * CH + (q + 1) * 128]
                    hl = h_l[:, hc * CH + q * 128:hc * CH + (q + 1) * 128]
                    for mc in range(2):
                        bank = q * 2 + mc
                        wh = w2s_h[buf][:, mc * 512:(mc + 1) * 512]
                        wl = w2s_l[buf][:, mc * 512:(mc + 1) * 512]
                        t.matmul(pb[bank][:], hh, wh, start=(hc == 0),
                                 stop=False, skip_group_check=True)
                        t.matmul(pb[bank][:], hh, wl, start=False,
                                 stop=False, skip_group_check=True)
                        TE.inc(t.matmul(pb[bank][:], hl, wh, start=False,
                                        stop=(hc == 31),
                                        skip_group_check=True))
                w2_free[buf] = TE.n
        te_y2 = TE.n

        for q in range(4):
            for mc in range(2):
                bank = q * 2 + mc

                @VE
                def _(v, bank=bank, mc=mc, te_y2=te_y2):
                    v.wait_ge(TE.h, te_y2)
                    v.wait_ge(SP.h, SP.n)
                    VC.inc(v.tensor_tensor(
                        yt_[:], pb[bank][:],
                        b2r_s[:, mc * 512:(mc + 1) * 512], ALU.add))
                    y2_free[bank] = VC.n
                vc_y = VC.n

                @GP
                def _(g, q=q, mc=mc, s0=s0, vc_y=vc_y):
                    g.wait_ge(VC.h, vc_y)
                    SP.inc(g.dma_start(
                        out=agy_i[s0 + q * 128:s0 + (q + 1) * 128,
                                  mc * 512:(mc + 1) * 512], in_=yt_[:]), 16)

        @GP
        def _(g, tc=tc, s0=s0):
            g.wait_ge(SP.h, SP.n)
            CC.inc(g.collective_compute(
                "AllGather", ALU.bypass, replica_groups=[list(range(NC))],
                ins=[agy_i[s0:s0 + CH, :]],
                outs=[agy_o[tc * CAP:(tc + 1) * CAP, :]]))
    cc_ally = CC.n

    y1t2 = [sbf(f"y1t{i}", [128, D]) for i in range(2)]
    y2t2 = [sbf(f"y2t{i}", [128, D]) for i in range(2)]
    x1r2 = [sbf(f"x1r{i}", [128, D]) for i in range(2)]
    fin2 = [sbf(f"fin{i}", [128, D]) for i in range(2)]
    fin0s = sbf("fin0s", [128, D])

    vc_fin2 = [0] * 18
    sp_st2 = [0] * 18
    for tt in range(16):
        b = tt % 2
        y1t = y1t2[b]; y2t = y2t2[b]; x1r = x1r2[b]
        fin = fin2[b]; fin0 = fin0s

        @GP
        def _(g, tt=tt, y1t=y1t, y2t=y2t, x1r=x1r):
            if tt == 0:
                g.wait_ge(CC.h, cc_ally)
            if tt > 1:
                g.wait_ge(VC.h, vc_fin2[tt])
            SP.inc(g.indirect_dma_start(
                out=y1t[:, :], out_offset=None, in_=agy_o[:, :],
                in_offset=IndirectOffsetOnAxis(ybst[:, tt:tt + 1], 0)), 16)
            SP.inc(g.indirect_dma_start(
                out=y2t[:, :], out_offset=None, in_=agy_o[:, :],
                in_offset=IndirectOffsetOnAxis(ybst[:, 16 + tt:16 + tt + 1],
                                               0)), 16)
            SP.inc(g.dma_start(out=x1r[:],
                               in_=x1_d[tt * 128:(tt + 1) * 128, :]), 16)
        sp_y = SP.n

        @VE
        def _(v, tt=tt, sp_y=sp_y, y1t=y1t, y2t=y2t, x1r=x1r, fin=fin,
              fin0=fin0):
            v.wait_ge(SP.h, sp_y)
            if tt > 1:
                v.wait_ge(SP.h, sp_st2[tt])
            v.scalar_tensor_tensor(fin0[:], y1t[:], g1st[:, tt:tt + 1],
                                   x1r[:], ALU.mult, ALU.add)
            VC.inc(v.scalar_tensor_tensor(fin[:], y2t[:], g2st[:, tt:tt + 1],
                                          fin0[:], ALU.mult, ALU.add))
        vc_fin2[tt + 2] = VC.n
        vc_fin = VC.n

        @GP
        def _(g, tt=tt, vc_fin=vc_fin, fin=fin):
            g.wait_ge(VC.h, vc_fin)
            SP.inc(g.dma_start(out=y_out[tt * 128:(tt + 1) * 128, :],
                               in_=fin[:]), 16)
        sp_st2[tt + 2] = SP.n

    @GP
    def _(g):
        g.wait_ge(SP.h, SP.n)

    ctxF.close()
    ctx.close()
    return nc


def prep_inputs(inputs):
    x = np.ascontiguousarray(np.asarray(inputs["L0_attn_x"], np.float32)
                             .reshape(NC * T, D))
    k1 = np.asarray(inputs["L0_attn_k1"], np.float32)
    k2 = np.asarray(inputs["L0_attn_k2"], np.float32)
    k3 = np.asarray(inputs["L0_attn_k3"], np.float32)
    a1 = np.asarray(inputs["L0_attn_a1"], np.float32).reshape(H, 64)
    a2 = np.asarray(inputs["L0_attn_a2"], np.float32).reshape(H, 64)
    b1 = np.asarray(inputs["L0_attn_b1"], np.float32).reshape(H, 64)
    b2 = np.asarray(inputs["L0_attn_b2"], np.float32).reshape(H, 64)
    c_ = np.asarray(inputs["L0_attn_c"], np.float32).reshape(H, 64)
    vw = np.asarray(inputs["L0_attn_value_weight"], np.float32)
    ow = np.asarray(inputs["L0_attn_output_weight"], np.float32)
    wg = np.asarray(inputs["L0_ffn_wg"], np.float32)
    w1 = np.asarray(inputs["L0_ffn_w1"], np.float32)
    fb1 = np.asarray(inputs["L0_ffn_b1"], np.float32)
    w2 = np.asarray(inputs["L0_ffn_w2"], np.float32)
    fb2 = np.asarray(inputs["L0_ffn_b2"], np.float32)

    wkt = np.ascontiguousarray(np.concatenate([k1, k2, k3], 0).T)
    wv2 = np.ascontiguousarray(vw.transpose(1, 0, 2).reshape(D, D))
    ow2 = np.ascontiguousarray(ow.transpose(0, 2, 1).reshape(D, D))

    cMm = np.zeros((128, 128), np.float32)
    for sc in range(8):
        for lh in range(2):
            h = sc * 2 + lh
            cMm[lh * 64:(lh + 1) * 64, sc * 16 + h] = c_[h]
    pc = {}
    for nm, arr in (("a1c", a1), ("b1c", b1), ("a2c", a2), ("b2c", b2)):
        pc[nm] = np.ascontiguousarray(arr.reshape(8, 128).T)

    trim = np.ascontiguousarray(np.triu(np.ones((128, 128), np.float32), 1))
    bselm = np.zeros((16, 1024), np.float32)
    for oc in range(8):
        for lh in range(2):
            h = oc * 2 + lh
            bselm[h, oc * 128 + lh * 64:oc * 128 + (lh + 1) * 64] = 1.0
    evec = np.ascontiguousarray(np.broadcast_to(
        np.tile(np.arange(8, dtype=np.float32) * 4096.0, 32), (128, 256)))
    e512 = np.ascontiguousarray(np.broadcast_to(
        np.tile(np.arange(8, dtype=np.float32) * 512.0, 32), (128, 256)))
    thr = np.ascontiguousarray(np.broadcast_to(
        np.tile(np.arange(1, 9, dtype=np.float32) * 512.0, 32), (128, 256)))
    ltm = np.zeros((8, 8), np.float32)
    for j in range(8):
        ltm[j, :j] = 1.0
    ltm = np.ascontiguousarray(np.broadcast_to(
        np.tile(ltm.reshape(64), 16), (128, 1024)))
    ones1 = np.ones((1, 128), np.float32)
    onesc = np.ones((128, 1), np.float32)

    in_maps = []
    for j in range(NC):
        b1mh = np.ascontiguousarray(fb1[j].reshape(32, 128).T)
        b2rh = np.ascontiguousarray(
            np.broadcast_to(fb2[j].reshape(1, D), (128, D)))
        b = j // 4
        nloc = np.ascontiguousarray(np.broadcast_to(
            ((np.arange(T) + (j % 4) * T).astype(np.float32) / 8191.0)
            .reshape(1, T), (128, T)))
        seqm = np.zeros((8, 1), np.float32)
        seqm[4 * b:j, 0] = 1.0
        offm = np.zeros((16, 2), np.float32)
        offm[0:2 * j:2, 0] = 1.0
        offm[0::2, 1] = 1.0
        offm[1:2 * j:2, 1] = 1.0
        offmr0 = np.ascontiguousarray(np.broadcast_to(offm[:, 0:1], (16, 128)))
        offmr1 = np.ascontiguousarray(np.broadcast_to(offm[:, 1:2], (16, 128)))
        gt = (np.arange(T, dtype=np.int64) + j * T + 1).astype(np.int32)
        gtok = np.ascontiguousarray(gt.reshape(16, 128).T)
        einv = np.ascontiguousarray(
            (j * 4096 + np.arange(32, dtype=np.int32)[None, :] * 128
             + np.arange(128, dtype=np.int32)[:, None]).astype(np.int32))
        in_maps.append({
            "x": x[j * T:(j + 1) * T],
            "wkt": wkt, "wv2": wv2, "ow2": ow2, "wgm": wg,
            "w1m": w1[j], "w2m": w2[j], "b1m": b1mh, "b2row": b2rh,
            "cM": cMm, "a1c": pc["a1c"], "b1c": pc["b1c"],
            "a2c": pc["a2c"], "b2c": pc["b2c"],
            "tri": trim, "bsel": bselm,
            "nvec": nloc, "seqm": seqm,
            "offm": offm, "offmr0": offmr0, "offmr1": offmr1,
            "gtok": gtok, "evec": evec, "e512": e512,
            "thr": thr, "ltm": ltm, "ones1": ones1, "onesc": onesc,
            "einv": einv,
        })
    return in_maps


def kernel(**inputs):
    nc = build_program()
    in_maps = prep_inputs(inputs)
    res = run_bass_kernel_spmd(nc, in_maps, core_ids=list(range(NC)))
    out = np.concatenate([res.results[j]["y_out"] for j in range(NC)], axis=0)
    return out.reshape(2, 8192, 1024)


# version 11
# speedup vs baseline: 1.3513x; 1.1830x over previous
from contextlib import ExitStack

import numpy as np

import concourse.bass as bass
import concourse.mybir as mybir
from concourse.bass_utils import run_bass_kernel_spmd
from concourse.bass_core_rust import IndirectOffsetOnAxis

F32 = mybir.dt.float32
BF16 = mybir.dt.bfloat16
I32 = mybir.dt.int32
ALU = mybir.AluOpType
AF = mybir.ActivationFunctionType
AX = mybir.AxisListType

NC = 8
T = 2048
D = 1024
H = 16
HD = 4096
CAP = 4096
NT = 4
CH = 512
TC = 8
TRASH = 32768


class Sem:
    def __init__(self, nc, ctx, name):
        self.h = ctx.enter_context(nc.semaphore(name))
        self.n = 0

    def inc(self, inst, k=1):
        inst.then_inc(self.h, k)
        self.n += k
        return self.n


def build_program():
    nc = bass.Bass("TRN2", target_bir_lowering=False,
                   detect_race_conditions=False)
    ctx = ExitStack()

    def din(name, shape, dt=F32):
        return nc.dram_tensor(name, shape, dt, kind="ExternalInput").ap()

    x_in = din("x", [T, D])
    wkt = din("wkt", [D, 48])
    wv2 = din("wv2", [D, D])
    ow2 = din("ow2", [D, D])
    wgm = din("wgm", [D, 8])
    w1m = din("w1m", [D, HD])
    w2m = din("w2m", [HD, D])
    b1m_i = din("b1m", [128, 32])
    b2r_i = din("b2row", [128, D])
    cM_i = din("cM", [128, 128])
    a1_i = din("a1c", [128, 8])
    b1_i = din("b1c", [128, 8])
    a2_i = din("a2c", [128, 8])
    b2_i = din("b2c", [128, 8])
    tri_i = din("tri", [128, 128])
    bsel_i = din("bsel", [16, 1024])
    nvec_i = din("nvec", [128, T])
    seqm_i = din("seqm", [8, 1])
    offm_i = din("offm", [16, 2])
    offr0_i = din("offmr0", [16, 128])
    offr1_i = din("offmr1", [16, 128])
    gtok_i = din("gtok", [128, 16], I32)
    evec_i = din("evec", [128, 256])
    e512_i = din("e512", [128, 256])
    thr_i = din("thr", [128, 256])
    ltm_i = din("ltm", [128, 1024])
    ones1_i = din("ones1", [1, 128])
    onesc_i = din("onesc", [128, 1])
    einv_i = din("einv", [128, 32], I32)

    y_out = nc.dram_tensor("y_out", [T, D], F32, kind="ExternalOutput").ap()

    def dint(name, shape, dt=F32):
        return nc.dram_tensor(name, shape, dt).ap()

    xnh_d = dint("xnh_d", [T, D], BF16)
    xnl_d = dint("xnl_d", [T, D], BF16)
    s_d = dint("s_d", [8, 128, T])
    v_d = dint("v_d", [8, 128, T])
    x1_d = dint("x1_d", [T, D])
    p12_d = dint("p12_d", [16, 2 * T])
    agcar_i = dint("agcar_i", [1, 1040])
    agcar_o = dint("agcar_o", [8, 1040])
    agx_i = dint("agx_i", [T, 2 * D], BF16)
    agx_o = dint("agx_o", [NC * T, 2 * D], BF16)
    agc_i = dint("agc_i", [2, 8])
    agc_o = dint("agc_o", [16, 8])
    invl_d = dint("invl_d", [33024, 1], I32)
    invg_d = dint("invg_d", [33024, 1], I32)
    agy_i = dint("agy_i", [CAP, D])
    agy_o = dint("agy_o", [NC * CAP, D])
    w1h_d = dint("w1h_d", [HD, D], BF16)
    w1l_d = dint("w1l_d", [HD, D], BF16)
    w2h_d = dint("w2h_d", [HD, D], BF16)
    w2l_d = dint("w2l_d", [HD, D], BF16)

    def sb(name, shape, dt=F32):
        return ctx.enter_context(nc.sbuf_tensor(name, shape, dt))

    wk_h = sb("wk_h", [128, 8, 48], BF16)
    wk_l = sb("wk_l", [128, 8, 48], BF16)
    wg_h = sb("wg_h", [128, 8, 8], BF16)
    wg_l = sb("wg_l", [128, 8, 8], BF16)
    a1s = sb("a1s", [128, 8]); b1s = sb("b1s", [128, 8])
    a2s = sb("a2s", [128, 8]); b2s = sb("b2s", [128, 8])
    tri_s = sb("tri_s", [128, 128])
    bsel_s = sb("bsel_s", [16, 1024])
    seqm_s = sb("seqm_s", [8, 1])
    offm_s = sb("offm_s", [16, 2])
    gtok_s = sb("gtok_s", [128, 16], I32)
    evec_s = sb("evec_s", [128, 256])
    e512_s = sb("e512_s", [128, 256])
    thr_s = sb("thr_s", [128, 256])
    ltm_s = sb("ltm_s", [128, 1024])
    ones1_s = sb("ones1_s", [1, 128])
    onesc_s = sb("onesc_s", [128, 1])
    b1m_s = sb("b1m_s", [128, 32])
    b2r_s = sb("b2r_s", [128, D])
    einv_s = sb("einv_s", [128, 32], I32)
    scarL = [sb(f"scar{i}", [128, 1]) for i in range(8)]
    Zcar = sb("Zcar", [16, 1])
    car2 = sb("car2", [128, 8])
    zc2 = sb("zc2", [16, 1])
    agc_s = sb("agc_s", [16, 8])
    offmr0_s = sb("offmr0_s", [16, 128])
    offmr1_s = sb("offmr1_s", [16, 128])
    g1st = sb("g1st", [128, 16]); g2st = sb("g2st", [128, 16])
    v1st = sb("v1st", [128, 16]); v2st = sb("v2st", [128, 16])
    ybst = sb("ybst", [128, 32], I32)
    sct = sb("sct", [128, 32], I32)
    zer_s = sb("zer_s", [128, 258], I32)
    invt = sb("invt", [128, 32], I32)
    cnt_s = sb("cnt_s", [1, 16])

    pb = [ctx.enter_context(nc.psum_tensor(f"pb{i}", [128, 512], F32))
          for i in range(8)]

    LD = Sem(nc, ctx, "LD")
    SP = Sem(nc, ctx, "SP")
    VC = Sem(nc, ctx, "VC")
    SC = Sem(nc, ctx, "SC")
    TE = Sem(nc, ctx, "TE")
    CC = Sem(nc, ctx, "CC")
    WL = Sem(nc, ctx, "WL")

    block = ctx.enter_context(nc.Block())

    def SY(f): block.sync(f)
    def GP(f): block.gpsimd(f)
    def VE(f): block.vector(f)
    def SE(f): block.scalar(f)
    def TN(f): block.tensor(f)

    ctxA = ExitStack()
    def sba(name, shape, dt=F32):
        return ctxA.enter_context(nc.sbuf_tensor(name, shape, dt))

    wv_h = sba("wv_h", [128, 8 * D], BF16)
    wv_l = sba("wv_l", [128, 8 * D], BF16)
    ow_h = sba("ow_h", [128, 8 * D], BF16)
    ow_l = sba("ow_l", [128, 8 * D], BF16)
    Z_s = sba("Z_s", [16, T])
    r_s = sba("r_s", [16, T])
    dinv_s = sba("dinv_s", [16, T])
    r_s2 = sba("r_s2", [16, T])
    ss_t = sba("ss_t", [128, CH])
    vv_t = sba("vv_t", [128, CH])
    xt2 = [sba(f"xt{i}", [128, D]) for i in range(2)]
    xc2 = [sba(f"xc{i}", [128, D]) for i in range(2)]
    xn322 = [sba(f"xn32{i}", [128, D]) for i in range(2)]
    xnh2 = [sba(f"xnh{i}", [128, D], BF16) for i in range(2)]
    xnl2 = [sba(f"xnl{i}", [128, D], BF16) for i in range(2)]
    sm12 = [sba(f"sm1{i}", [128, 8]) for i in range(2)]

    ctxW = ExitStack()
    def sbw(name, shape, dt=F32):
        return ctxW.enter_context(nc.sbuf_tensor(name, shape, dt))

    wtmp2 = [sbw(f"wtmp{i}", [128, 2048]) for i in range(2)]
    wth2 = [sbw(f"wth{i}", [128, 2048], BF16) for i in range(2)]
    wtl2 = [sbw(f"wtl{i}", [128, 2048], BF16) for i in range(2)]
    wkt32 = sbw("wkt32", [128, 8, 48])
    wg32 = sbw("wg32", [128, 8, 8])
    cM_s = sbw("cM_s", [128, 128])
    nvec_s = sbw("nvec_s", [128, T])

    @SY
    def _(sy):
        for t, d_ in ((cM_s, cM_i), (tri_s, tri_i), (bsel_s, bsel_i),
                      (nvec_s, nvec_i), (seqm_s, seqm_i), (offm_s, offm_i),
                      (gtok_s, gtok_i), (evec_s, evec_i), (e512_s, e512_i),
                      (thr_s, thr_i), (ltm_s, ltm_i), (ones1_s, ones1_i),
                      (onesc_s, onesc_i), (b1m_s, b1m_i), (b2r_s, b2r_i),
                      (a1s, a1_i), (b1s, b1_i), (a2s, a2_i), (b2s, b2_i),
                      (einv_s, einv_i), (offmr0_s, offr0_i),
                      (offmr1_s, offr1_i)):
            LD.inc(sy.dma_start(out=t[:], in_=d_[:, :]), 16)
        LD.inc(sy.dma_start(
            out=wkt32[:], in_=wkt.rearrange("(f p) c -> p f c", p=128)), 16)
        LD.inc(sy.dma_start(
            out=wg32[:], in_=wgm.rearrange("(f p) c -> p f c", p=128)), 16)
    ld_consts = LD.n

    @VE
    def _(v):
        v.wait_ge(LD.h, ld_consts)
        for i in range(8):
            v.memset(scarL[i][:], 0.0)
        v.memset(Zcar[:], 0.0)
        v.memset(cnt_s[:], 0.0)
        v.memset(zer_s[:], 0)
        v.tensor_copy(wk_h[:], wkt32[:])
        v.tensor_copy(wg_h[:], wg32[:])
        v.tensor_tensor(wk_l[:], wkt32[:], wk_h[:], ALU.subtract)
        VC.inc(v.tensor_tensor(wg_l[:], wg32[:], wg_h[:], ALU.subtract))
    vc_consts = VC.n

    vfree = [VC.n, VC.n]
    for ji, (wsrc, dsth, dstl) in enumerate(
            [(wv2, wv_h, wv_l), (ow2, ow_h, ow_l)]):
        for fc in range(8):
            b = (ji * 8 + fc) % 2
            @SY
            def _(sy, fc=fc, b=b, wsrc=wsrc):
                sy.wait_ge(VC.h, vfree[b])
                LD.inc(sy.dma_start(out=wtmp2[b][:, 0:1024],
                                    in_=wsrc[fc * 128:(fc + 1) * 128, :]), 16)
            ldc = LD.n

            @VE
            def _(v, fc=fc, b=b, ldc=ldc, dsth=dsth, dstl=dstl):
                v.wait_ge(LD.h, ldc)
                v.tensor_copy(dsth[:, fc * D:(fc + 1) * D],
                              wtmp2[b][:, 0:1024])
                VC.inc(v.tensor_tensor(dstl[:, fc * D:(fc + 1) * D],
                                       wtmp2[b][:, 0:1024],
                                       dsth[:, fc * D:(fc + 1) * D],
                                       ALU.subtract))
                vfree[b] = VC.n



    def emit_w12_prep():
        vfree2 = [VC.n, VC.n]
        spfree = [SP.n, SP.n]
        lfree = [LD.n, LD.n]
        jobs = []
        for fc in range(8):
            for hh in range(2):
                jobs.append(("w1", fc, hh))
        for hc2 in range(16):
            jobs.append(("w2", hc2, 0))
        for ji, job in enumerate(jobs):
            b = ji % 2
            kind, i0, i1 = job

            @SE
            def _(sy, kind=kind, i0=i0, i1=i1, b=b, ji=ji):
                sy.wait_ge(VC.h, vfree2[b])
                if kind == "w1":
                    WL.inc(sy.dma_start(
                        out=wtmp2[b][:],
                        in_=w1m[i0 * 128:(i0 + 1) * 128,
                                i1 * 2048:(i1 + 1) * 2048]), 16)
                else:
                    WL.inc(sy.dma_start(
                        out=wtmp2[b][:],
                        in_=w2m.rearrange("(c p) m -> p c m", p=128)[
                            :, i0 * 2:(i0 + 1) * 2, :]), 16)
            ldc = WL.n

            @VE
            def _(v, b=b, ldc=ldc):
                v.wait_ge(WL.h, ldc)
                v.wait_ge(SP.h, spfree[b])
                v.wait_ge(LD.h, lfree[b])
                v.tensor_copy(wth2[b][:], wtmp2[b][:])
                v.drain()
                VC.inc(v.tensor_tensor(wtl2[b][:], wtmp2[b][:], wth2[b][:],
                                       ALU.subtract))
                vfree2[b] = VC.n
            vcc = VC.n

            @GP
            def _(g, kind=kind, i0=i0, i1=i1, b=b, vcc=vcc):
                g.wait_ge(VC.h, vcc)
                if kind == "w1":
                    SP.inc(g.dma_start(
                        out=w1h_d.rearrange("(H q) (f h) -> q H f h", q=128,
                                            h=128)[:, i1 * 16:(i1 + 1) * 16,
                                                   i0, :],
                        in_=wth2[b][:]), 16)
                else:
                    SP.inc(g.dma_start(
                        out=w2h_d.rearrange("(c p) m -> p c m", p=128)[
                            :, i0 * 2:(i0 + 1) * 2, :], in_=wth2[b][:]), 16)
                    SP.inc(g.dma_start(
                        out=w2l_d.rearrange("(c p) m -> p c m", p=128)[
                            :, i0 * 2:(i0 + 1) * 2, :], in_=wtl2[b][:]), 16)
                spfree[b] = SP.n

            if kind == "w1":
                @SY
                def _(sy, i0=i0, i1=i1, b=b, vcc=vcc):
                    sy.wait_ge(VC.h, vcc)
                    LD.inc(sy.dma_start(
                        out=w1l_d.rearrange("(H q) (f h) -> q H f h", q=128,
                                            h=128)[:, i1 * 16:(i1 + 1) * 16,
                                                   i0, :],
                        in_=wtl2[b][:]), 16)
                lfree[b] = LD.n


    def emit_p12():
        p12e = ctxW.enter_context(nc.sbuf_tensor("p12e", [16, CH], F32))
        sfree = [TE.n, TE.n]
        for k in range(2):
            aa, bb = (a1s, b1s) if k == 0 else (a2s, b2s)
            for nt in range(NT):
                t0 = nt * CH
                for sc8 in range(8):
                    b = sc8 % 2

                    @VE
                    def _(v, t0=t0, sc8=sc8, b=b, aa=aa, bb=bb):
                        v.wait_ge(LD.h, ld_consts)
                        v.wait_ge(TE.h, sfree[b])
                        VC.inc(v.tensor_scalar(
                            wtmp2[b][:, 0:CH], nvec_s[:, t0:t0 + CH],
                            aa[:, sc8:sc8 + 1], bb[:, sc8:sc8 + 1],
                            ALU.mult, ALU.add))
                    vca = VC.n

                    @SE
                    def _(s, b=b, vca=vca):
                        s.wait_ge(VC.h, vca)
                        SC.inc(s.activation(wtmp2[b][:, 0:CH],
                                            wtmp2[b][:, 0:CH], AF.Sin))
                    @TN
                    def _(t, sc8=sc8, b=b, scn=SC.n, k=k, nt=nt):
                        t.wait_ge(SC.h, scn)
                        if sc8 == 0:
                            t.wait_ge(VC.h, p12free[0])
                        TE.inc(t.matmul(pb[7][0:16, :],
                                        cM_s[:, sc8 * 16:(sc8 + 1) * 16],
                                        wtmp2[b][:, 0:CH], start=(sc8 == 0),
                                        stop=(sc8 == 7)))
                    sfree[b] = TE.n

                @VE
                def _(v, ten=TE.n):
                    v.wait_ge(TE.h, ten)
                    VC.inc(v.tensor_copy(p12e[:], pb[7][0:16, :]))
                p12free[0] = VC.n

                @GP
                def _(g, k=k, t0=t0, vcn=VC.n):
                    g.wait_ge(VC.h, vcn)
                    SP.inc(g.dma_start(
                        out=p12_d[:, k * T + t0:k * T + t0 + CH],
                        in_=p12e[:]), 16)

    p12free = [VC.n]
    emit_w12_prep()
    ld_w12 = LD.n
    emit_p12()
    sp_w12 = SP.n
    ctxW.close()

    vc_apply = [0] * 18
    sp_p1_tiles = [0] * 16
    for tt in range(16):
        b = tt % 2
        xt = xt2[b]; xc_ = xc2[b]; xn32 = xn322[b]
        xnh_t = xnh2[b]; xnl_t = xnl2[b]; sm = sm12[b]
        s1_ = sm[:, 0:1]; s2_ = sm[:, 1:2]; mu_ = sm[:, 2:3]
        var_ = sm[:, 3:4]; isd_ = sm[:, 4:5]; s2b_ = sm[:, 5:6]
        nmu_ = sm[:, 6:7]

        @GP
        def _(g, tt=tt, xt=xt):
            if tt > 1:
                g.wait_ge(VC.h, vc_apply[tt])
            SP.inc(g.dma_start(out=xt[:],
                               in_=x_in[tt * 128:(tt + 1) * 128, :]), 16)
        sp_x = SP.n

        @SE
        def _(s, sp_x=sp_x, xt=xt, xn32=xn32, s1_=s1_, s2_=s2_):
            s.wait_ge(SP.h, sp_x)
            s.activation(xn32[:], xt[:], AF.Square, accum_out=s2_)
            SC.inc(s.activation(xn32[:], xt[:], AF.Identity, accum_out=s1_))
        sc_sq = SC.n

        @VE
        def _(v, sc_sq=sc_sq, s1_=s1_, s2_=s2_, mu_=mu_, var_=var_,
              s2b_=s2b_, nmu_=nmu_):
            v.wait_ge(SC.h, sc_sq)
            v.tensor_scalar(mu_, s1_, 1.0 / D, None, ALU.mult)
            v.tensor_scalar(s2b_, s2_, 1.0 / D, 1e-5, ALU.mult, ALU.add)
            v.tensor_scalar(nmu_, mu_, -1.0, None, ALU.mult)
            v.tensor_tensor(var_, mu_, mu_, ALU.mult)
            v.drain()
            v.tensor_tensor(s1_, s2b_, var_, ALU.subtract)
            v.drain()
            VC.inc(v.reciprocal(var_, s1_))
        vc_st = VC.n

        @SE
        def _(s, vc_st=vc_st, isd_=isd_, var_=var_):
            s.wait_ge(VC.h, vc_st)
            SC.inc(s.activation(isd_, var_, AF.Sqrt))
        sc_ap = SC.n

        @VE
        def _(v, tt=tt, sc_ap=sc_ap, xt=xt, xc_=xc_, xn32=xn32,
              xnh_t=xnh_t, xnl_t=xnl_t, nmu_=nmu_, isd_=isd_):
            v.wait_ge(SC.h, sc_ap)
            if tt > 1:
                v.wait_ge(SP.h, sp_p1_tiles[tt - 2])
            v.tensor_scalar(xc_[:], xt[:], nmu_, None, ALU.add)
            v.tensor_scalar(xn32[:], xc_[:], isd_, None, ALU.mult)
            v.tensor_copy(xnh_t[:], xn32[:])
            VC.inc(v.tensor_tensor(xnl_t[:], xn32[:], xnh_t[:],
                                   ALU.subtract))
        vc_apply[tt + 2] = VC.n

        @GP
        def _(g, tt=tt, xnh_t=xnh_t, xnl_t=xnl_t, vcc=VC.n):
            g.wait_ge(VC.h, vcc)
            SP.inc(g.dma_start(out=xnh_d[tt * 128:(tt + 1) * 128, :],
                               in_=xnh_t[:]), 16)
            SP.inc(g.dma_start(out=xnl_d[tt * 128:(tt + 1) * 128, :],
                               in_=xnl_t[:]), 16)
        sp_p1_tiles[tt] = SP.n

    xt = xt2[0]; xc_ = xc2[0]; xn32 = xn322[0]
    xnh_t = xnh2[0]; xnl_t = xnl2[0]
    sm = sm12[0]
    s1_ = sm[:, 0:1]; s2_ = sm[:, 1:2]; mu_ = sm[:, 2:3]
    var_ = sm[:, 3:4]; isd_ = sm[:, 4:5]; s2b_ = sm[:, 5:6]
    nmu_ = sm[:, 6:7]

    ctxP2 = ExitStack()
    def sb2(name, shape, dt=F32):
        return ctxP2.enter_context(nc.sbuf_tensor(name, shape, dt))

    xfm_h = sb2("xfm_h", [128, 8 * CH], BF16)
    xfm_l = sb2("xfm_l", [128, 8 * CH], BF16)
    p1_t = sb2("p1_t", [16, CH]); p2raw = sb2("p2raw", [16, CH])
    c_t = sb2("c_t", [16, CH]); d_t = sb2("d_t", [16, CH])
    e_t = sb2("e_t", [16, CH]); p2_t = sb2("p2_t", [16, CH])
    rt1 = sb2("rt1", [16, CH])
    cx_t = sb2("cx_t", [128, CH])
    cv_t = sb2("cv_t", [128, CH])

    cx_evac = [VC.n]
    vbank_free = {4: VC.n, 5: VC.n, 6: VC.n, 7: VC.n}
    sc_scar = [SC.n]
    vc_p12used = [VC.n]

    for nt in range(NT):
        t0 = nt * CH

        @SY
        def _(sy, nt=nt, t0=t0):
            sy.wait_ge(SP.h, sp_p1_tiles[4 * nt + 3])
            if nt > 0:
                sy.wait_ge(TE.h, TE.n)
            for fc in range(8):
                LD.inc(sy.dma_start_transpose(
                    xfm_h[:, fc * CH:(fc + 1) * CH],
                    xnh_d[t0:t0 + CH, fc * 128:(fc + 1) * 128]), 16)
                LD.inc(sy.dma_start_transpose(
                    xfm_l[:, fc * CH:(fc + 1) * CH],
                    xnl_d[t0:t0 + CH, fc * 128:(fc + 1) * 128]), 16)
        ld_xfm = LD.n

        @GP
        def _(g, t0=t0, nt=nt):
            if nt > 0:
                g.wait_ge(VC.h, vc_p12used[0])
            SP.inc(g.dma_start(out=p1_t[:], in_=p12_d[:, t0:t0 + CH]), 16)
            SP.inc(g.dma_start(out=p2raw[:],
                               in_=p12_d[:, T + t0:T + t0 + CH]), 16)
        sp_p12 = SP.n

        @TN
        def _(t, nt=nt, ld_xfm=ld_xfm):
            t.wait_ge(LD.h, ld_xfm)
            t.wait_ge(VC.h, VC.n)
            for kind in range(3):
                for fc in range(8):
                    xh = xfm_h[:, fc * CH:(fc + 1) * CH]
                    xl = xfm_l[:, fc * CH:(fc + 1) * CH]
                    kh = wk_h[:, fc, kind * 16:(kind + 1) * 16]
                    kl = wk_l[:, fc, kind * 16:(kind + 1) * 16]
                    t.matmul(pb[kind][0:16, :], kh, xh,
                             start=(fc == 0), stop=False)
                    t.matmul(pb[kind][0:16, :], kh, xl, start=False,
                             stop=False)
                    TE.inc(t.matmul(pb[kind][0:16, :], kl, xh,
                                    start=False, stop=(fc == 7)))
        te_p2m = TE.n

        @VE
        def _(v, te_p2m=te_p2m, sp_p12=sp_p12):
            v.wait_ge(TE.h, te_p2m)
            v.wait_ge(SP.h, sp_p12)
            v.tensor_tensor(rt1[:], pb[0][0:16, :], p1_t[:], ALU.add)
            v.tensor_scalar(d_t[:], pb[1][0:16, :], 20.0, -20.0, ALU.min,
                            ALU.max)
            v.tensor_scalar(e_t[:], pb[2][0:16, :], 20.0, -20.0, ALU.min,
                            ALU.max)
            v.tensor_scalar(p2_t[:], p2raw[:], 20.0, -20.0, ALU.min,
                            ALU.max)
            VC.inc(v.tensor_scalar(c_t[:], rt1[:], 20.0, -20.0, ALU.min,
                                   ALU.max))
        vc_clip = VC.n
        vc_p12used[0] = VC.n

        @SE
        def _(s, vc_clip=vc_clip):
            s.wait_ge(VC.h, vc_clip)
            s.activation(c_t[:], c_t[:], AF.Exp)
            s.activation(d_t[:], d_t[:], AF.Exp)
            s.activation(e_t[:], e_t[:], AF.Exp)
            SC.inc(s.activation(p2_t[:], p2_t[:], AF.Exp))
        sc_exp = SC.n

        @VE
        def _(v, t0=t0, sc_exp=sc_exp):
            v.wait_ge(SC.h, sc_exp)
            v.reciprocal(rt1[:], d_t[:])
            v.tensor_tensor(ss_t[0:16, :], p2_t[:], e_t[:], ALU.mult)
            v.tensor_tensor(r_s[:, t0:t0 + CH], ss_t[0:16, :], rt1[:],
                            ALU.mult)
            VC.inc(v.tensor_tensor_scan(Z_s[:, t0:t0 + CH], onesc_s[0:16, 0:1].to_broadcast([16, CH]),
                                        c_t[:], Zcar[:, 0:1], ALU.mult,
                                        ALU.add))
        vc_zr = VC.n

        @SE
        def _(s, t0=t0, vc_zr=vc_zr):
            s.wait_ge(VC.h, vc_zr)
            SC.inc(s.copy(Zcar[:], Z_s[:, t0 + CH - 1:t0 + CH]))
        sc_zcar = SC.n

        for oc in range(8):
            bank = 4 + (oc % 4)

            @TN
            def _(t, nt=nt, oc=oc, bank=bank, vc_zr=vc_zr):
                t.wait_ge(VC.h, vc_zr)
                t.wait_ge(VC.h, cx_evac[0])
                t.matmul(pb[0][:], bsel_s[:, oc * 128:(oc + 1) * 128],
                         c_t[:], start=True, stop=True)
                t.wait_ge(VC.h, vbank_free[bank])
                for fc in range(8):
                    xh = xfm_h[:, fc * CH:(fc + 1) * CH]
                    xl = xfm_l[:, fc * CH:(fc + 1) * CH]
                    wvh = wv_h[:, fc * D + oc * 128:fc * D + (oc + 1) * 128]
                    wvl = wv_l[:, fc * D + oc * 128:fc * D + (oc + 1) * 128]
                    t.matmul(pb[bank][:], wvh, xh, start=(fc == 0),
                             stop=False)
                    t.matmul(pb[bank][:], wvh, xl, start=False, stop=False)
                    TE.inc(t.matmul(pb[bank][:], wvl, xh, start=False,
                                    stop=(fc == 7)))
            te_v = TE.n

            @VE
            def _(v, nt=nt, oc=oc, bank=bank, te_v=te_v):
                v.wait_ge(TE.h, te_v)
                v.wait_ge(SP.h, SP.n)
                v.wait_ge(SC.h, sc_scar[0])
                VC.inc(v.tensor_copy(cx_t[:], pb[0][:]))
                cx_evac[0] = VC.n
                v.tensor_tensor(cv_t[:], pb[bank][:], cx_t[:], ALU.mult)
                v.tensor_copy(vv_t[:], pb[bank][:])
                VC.inc(v.tensor_tensor_scan(ss_t[:], onesc_s[:, 0:1].to_broadcast([128, CH]), cv_t[:],
                                            scarL[oc][:, 0:1], ALU.mult,
                                            ALU.add))
                vbank_free[bank] = VC.n
            vc_soc = VC.n

            @SE
            def _(s, oc=oc, vc_soc=vc_soc):
                s.wait_ge(VC.h, vc_soc)
                SC.inc(s.copy(scarL[oc][:], ss_t[:, CH - 1:CH]))
            sc_scar[0] = SC.n

            @GP
            def _(g, nt=nt, oc=oc, t0=t0, vc_soc=vc_soc):
                g.wait_ge(VC.h, vc_soc)
                SP.inc(g.dma_start(out=s_d[oc, :, t0:t0 + CH], in_=ss_t[:]),
                       16)
                SP.inc(g.dma_start(out=v_d[oc, :, t0:t0 + CH], in_=vv_t[:]),
                       16)
    ctxP2.close()

    ctxP4 = ExitStack()
    def sb4(name, shape, dt=F32):
        return ctxP4.enter_context(nc.sbuf_tensor(name, shape, dt))

    o_h = sb4("o_h", [128, 8 * CH], BF16)
    o_l = sb4("o_l", [128, 8 * CH], BF16)
    rx_t = sb4("rx_t", [128, CH])
    dx_t = sb4("dx_t", [128, CH])
    o32 = sb4("o32", [128, CH])
    tp4 = sb4("tp4", [128, CH])
    x1t = sb4("x1t", [128, D])
    xrow = sb4("xrow", [128, D])
    agcar_s = sb4("agcar_s", [8, 1040])

    @GP
    def _(g):
        g.wait_ge(VC.h, VC.n)
        g.wait_ge(SC.h, SC.n)
        for oc in range(8):
            SP.inc(g.dma_start(out=agcar_i[0:1, oc * 128:(oc + 1) * 128],
                               in_=scarL[oc][:, 0:1],
                               allow_slow_non_contiguous=True), 16)
        SP.inc(g.dma_start(out=agcar_i[0:1, 1024:1040], in_=Zcar[:, 0:1],
                           allow_slow_non_contiguous=True), 16)
        g.wait_ge(SP.h, SP.n)
        CC.inc(g.collective_compute(
            "AllGather", ALU.bypass, replica_groups=[list(range(NC))],
            ins=[agcar_i[:, :]], outs=[agcar_o[:, :]]))
        g.wait_ge(CC.h, CC.n)
        SP.inc(g.dma_start(out=agcar_s[:], in_=agcar_o[:, :]), 16)
    sp_p3 = SP.n

    @TN
    def _(t):
        t.wait_ge(SP.h, sp_p3)
        for oc in range(8):
            t.matmul(pb[3][:, oc:oc + 1],
                     agcar_s[:, oc * 128:(oc + 1) * 128], seqm_s[:, :],
                     start=(oc == 0), stop=(oc == 7))
        TE.inc(t.matmul(pb[2][0:16, 0:1], agcar_s[:, 1024:1040],
                        seqm_s[:, :], start=True, stop=True))
    te_p3 = TE.n

    @VE
    def _(v, te_p3=te_p3):
        v.wait_ge(TE.h, te_p3)
        v.tensor_copy(car2[:], pb[3][:, 0:8])
        v.tensor_copy(zc2[:], pb[2][0:16, 0:1])
        v.drain()
        v.tensor_scalar(r_s2[:], Z_s[:], zc2[:, 0:1], None, ALU.add)
        v.tensor_tensor(dinv_s[:], r_s2[:], r_s[:], ALU.mult)
        v.tensor_scalar(r_s2[:], dinv_s[:], 1.0, None, ALU.add)
        VC.inc(v.reciprocal(dinv_s[:], r_s2[:]))
    vc_p3 = VC.n


    rx_evac = [VC.n]
    o_used = [TE.n]
    ybank_free = {4: VC.n, 5: VC.n, 6: VC.n, 7: VC.n}

    for nt in range(NT):
        t0 = nt * CH
        for oc in range(8):
            @GP
            def _(g, oc=oc, t0=t0):
                g.wait_ge(VC.h, VC.n)
                SP.inc(g.dma_start(out=ss_t[:], in_=s_d[oc, :, t0:t0 + CH]),
                       16)
                SP.inc(g.dma_start(out=vv_t[:], in_=v_d[oc, :, t0:t0 + CH]),
                       16)
            sp_sv = SP.n

            @TN
            def _(t, oc=oc, t0=t0):
                t.wait_ge(VC.h, max(vc_p3, rx_evac[0]))
                t.matmul(pb[0][:], bsel_s[:, oc * 128:(oc + 1) * 128],
                         r_s[:, t0:t0 + CH], start=True, stop=True)
                TE.inc(t.matmul(pb[1][:], bsel_s[:, oc * 128:(oc + 1) * 128],
                                dinv_s[:, t0:t0 + CH], start=True, stop=True))
            te_rd = TE.n

            @VE
            def _(v, oc=oc, te_rd=te_rd, sp_sv=sp_sv):
                v.wait_ge(TE.h, te_rd)
                v.wait_ge(SP.h, sp_sv)
                v.wait_ge(TE.h, o_used[0])
                v.tensor_copy(rx_t[:], pb[0][:])
                VC.inc(v.tensor_copy(dx_t[:], pb[1][:]))
                rx_evac[0] = VC.n
                v.tensor_scalar(tp4[:], ss_t[:], car2[:, oc:oc + 1], None,
                                ALU.add)
                v.tensor_tensor(o32[:], tp4[:], rx_t[:], ALU.mult)
                v.tensor_tensor(tp4[:], o32[:], vv_t[:], ALU.add)
                v.tensor_tensor(o32[:], tp4[:], dx_t[:], ALU.mult)
                v.tensor_copy(o_h[:, oc * CH:(oc + 1) * CH], o32[:])
                VC.inc(v.tensor_tensor(o_l[:, oc * CH:(oc + 1) * CH], o32[:],
                                       o_h[:, oc * CH:(oc + 1) * CH],
                                       ALU.subtract))
        vc_o = VC.n

        for q in range(4):
            tt = nt * 4 + q

            @GP
            def _(g, tt=tt):
                g.wait_ge(VC.h, VC.n)
                SP.inc(g.dma_start(out=xrow[:],
                                   in_=x_in[tt * 128:(tt + 1) * 128, :]), 16)
            sp_xr = SP.n

            @TN
            def _(t, q=q, vc_o=vc_o):
                t.wait_ge(VC.h, vc_o)
                for wc in range(2):
                    bank = 4 + (q * 2 + wc) % 4
                    t.wait_ge(VC.h, ybank_free[bank])
                    for oc in range(8):
                        oh = o_h[:, oc * CH + q * 128:oc * CH + (q + 1) * 128]
                        ol = o_l[:, oc * CH + q * 128:oc * CH + (q + 1) * 128]
                        owh = ow_h[:, oc * D + wc * 512:oc * D + (wc + 1) * 512]
                        owl = ow_l[:, oc * D + wc * 512:oc * D + (wc + 1) * 512]
                        t.matmul(pb[bank][:], oh, owh, start=(oc == 0),
                                 stop=False)
                        t.matmul(pb[bank][:], oh, owl, start=False,
                                 stop=False)
                        TE.inc(t.matmul(pb[bank][:], ol, owh, start=False,
                                        stop=(oc == 7)))
            te_y = TE.n

            @VE
            def _(v, q=q, te_y=te_y, sp_xr=sp_xr):
                v.wait_ge(TE.h, te_y)
                v.wait_ge(SP.h, SP.n)
                for wc in range(2):
                    bank = 4 + (q * 2 + wc) % 4
                    VC.inc(v.tensor_tensor(x1t[:, wc * 512:(wc + 1) * 512],
                                           pb[bank][:],
                                           xrow[:, wc * 512:(wc + 1) * 512],
                                           ALU.add))
                    ybank_free[bank] = VC.n
            vc_x1 = VC.n

            @SE
            def _(s, vc_x1=vc_x1):
                s.wait_ge(VC.h, vc_x1)
                s.activation(xn32[:], x1t[:], AF.Square,
                             accum_out=s2_)
                SC.inc(s.activation(xn32[:], x1t[:], AF.Identity,
                                    accum_out=s1_))
            sc_sq2 = SC.n

            @VE
            def _(v, sc_sq2=sc_sq2):
                v.wait_ge(SC.h, sc_sq2)
                v.tensor_scalar(mu_, s1_, 1.0 / D, None, ALU.mult)
                v.tensor_scalar(s2b_, s2_, 1.0 / D, 1e-5, ALU.mult,
                                ALU.add)
                v.tensor_scalar(nmu_, mu_, -1.0, None, ALU.mult)
                v.tensor_tensor(var_, mu_, mu_, ALU.mult)
                v.drain()
                v.tensor_tensor(s1_, s2b_, var_, ALU.subtract)
                v.drain()
                VC.inc(v.reciprocal(var_, s1_))
            vc_st2 = VC.n

            @SE
            def _(s, vc_st2=vc_st2):
                s.wait_ge(VC.h, vc_st2)
                SC.inc(s.activation(isd_, var_, AF.Sqrt))
            sc_ap2 = SC.n

            @VE
            def _(v, sc_ap2=sc_ap2):
                v.wait_ge(SC.h, sc_ap2)
                v.wait_ge(SP.h, SP.n)
                v.tensor_scalar(xc_[:], x1t[:], nmu_, None, ALU.add)
                v.tensor_scalar(xn32[:], xc_[:], isd_, None, ALU.mult)
                v.tensor_copy(xnh_t[:], xn32[:])
                VC.inc(v.tensor_tensor(xnl_t[:], xn32[:], xnh_t[:],
                                       ALU.subtract))
            vc_sp2 = VC.n

            @GP
            def _(g, tt=tt, vc_sp2=vc_sp2):
                g.wait_ge(VC.h, vc_sp2)
                SP.inc(g.dma_start(out=x1_d[tt * 128:(tt + 1) * 128, :],
                                   in_=x1t[:]), 16)
                SP.inc(g.dma_start(out=agx_i[tt * 128:(tt + 1) * 128, 0:D],
                                   in_=xnh_t[:]), 16)
                SP.inc(g.dma_start(out=agx_i[tt * 128:(tt + 1) * 128,
                                             D:2 * D], in_=xnl_t[:]), 16)
        o_used[0] = TE.n
    sp_p4 = SP.n

    @GP
    def _(g):
        g.wait_ge(SP.h, sp_p4)
        CC.inc(g.collective_compute(
            "AllGather", ALU.bypass, replica_groups=[list(range(NC))],
            ins=[agx_i[:, :]], outs=[agx_o[:, :]]))
    cc_agx = CC.n

    ctxP4.close()
    ctxA.close()

    ctxG = ExitStack()
    def sbg(name, shape, dt=F32):
        return ctxG.enter_context(nc.sbuf_tensor(name, shape, dt))

    xg_h = sbg("xg_h", [128, 8 * T], BF16)
    xg_l = sbg("xg_l", [128, 8 * T], BF16)
    ohst = sbg("ohst", [128, 2 * 16 * 8])
    plst = sbg("plst", [128, 32])
    lg_t = sbg("lg_t", [128, 128])
    es_t = sbg("es_t", [128, 128])
    m8_t = sbg("m8_t", [128, 16])
    ssum_t = sbg("ssum_t", [128, 16])
    rec_t = sbg("rec_t", [128, 16])
    sm_w = sbg("sm_w", [128, 128])
    gt_w = sbg("gt_w", [128, 1024])
    eq_w = sbg("eq_w", [128, 1024])
    eqm_w = sbg("eqm_w", [128, 1024])
    rks_w = sbg("rks_w", [128, 1024])
    rk_w = sbg("rk_w", [128, 128])
    a1t = sbg("a1t", [128, 8]); a2t = sbg("a2t", [128, 8])
    voff = sbg("voff", [128, 256])
    selo = sbg("selo", [128, 256]); sele = sbg("sele", [128, 256])
    sel5 = sbg("sel5", [128, 256]); cmpq = sbg("cmpq", [128, 256])
    pos_add = sbg("pos_add", [128, 32]); esum = sbg("esum", [128, 32])
    posga = sbg("posga", [128, 32]); valid_w = sbg("valid_w", [128, 32])
    e512s = sbg("e512s", [128, 32]); q_w = sbg("q_w", [128, 32])
    posc_w = sbg("posc_w", [128, 32]); cdif = sbg("cdif", [128, 32])
    w32a = sbg("w32a", [128, 32]); w32b = sbg("w32b", [128, 32])
    s12 = sbg("s12", [128, 16]); rec12 = sbg("rec12", [128, 16])
    g1a = sbg("g1a", [128, 16]); g2a = sbg("g2a", [128, 16])

    @SY
    def _(sy):
        sy.wait_ge(SP.h, sp_p4)
        for fc in range(8):
            LD.inc(sy.dma_start_transpose(
                xg_h[:, fc * T:(fc + 1) * T],
                agx_i[:, fc * 128:(fc + 1) * 128]), 16)
            LD.inc(sy.dma_start_transpose(
                xg_l[:, fc * T:(fc + 1) * T],
                agx_i[:, D + fc * 128:D + (fc + 1) * 128]), 16)
    ld_xg = LD.n

    lg_read = [VC.n]
    for tt in range(16):
        @TN
        def _(t, tt=tt, ld_xg=ld_xg):
            t.wait_ge(LD.h, ld_xg)
            t.wait_ge(VC.h, lg_read[0])
            for fc in range(8):
                xh = xg_h[:, fc * T + tt * 128:fc * T + (tt + 1) * 128]
                xl = xg_l[:, fc * T + tt * 128:fc * T + (tt + 1) * 128]
                t.matmul(pb[0][:, 0:8], xh, wg_h[:, fc, :], start=(fc == 0),
                         stop=False)
                t.matmul(pb[0][:, 0:8], xh, wg_l[:, fc, :], start=False,
                         stop=False)
                TE.inc(t.matmul(pb[0][:, 0:8], xl, wg_h[:, fc, :],
                                start=False, stop=(fc == 7)))
        te_lg = TE.n

        @VE
        def _(v, tt=tt, te_lg=te_lg):
            v.wait_ge(TE.h, te_lg)
            VC.inc(v.tensor_copy(lg_t[:, tt * 8:tt * 8 + 8], pb[0][:, 0:8]))
            lg_read[0] = VC.n

    lg3 = lg_t.ap().rearrange("p (t k) -> p t k", k=8)
    es3 = es_t.ap().rearrange("p (t k) -> p t k", k=8)
    sm3 = sm_w.ap().rearrange("p (t k) -> p t k", k=8)

    @VE
    def _(v):
        v.tensor_reduce(m8_t[:], lg3, AX.X, ALU.max)
        v.drain()
        VC.inc(v.tensor_tensor(
            es3, lg3, m8_t[:, :, None].to_broadcast([128, 16, 8]),
            ALU.subtract))
    vc_es = VC.n

    @SE
    def _(s, vc_es=vc_es):
        s.wait_ge(VC.h, vc_es)
        SC.inc(s.activation(es_t[:], es_t[:], AF.Exp))
    sc_ex = SC.n

    @VE
    def _(v, sc_ex=sc_ex):
        v.wait_ge(SC.h, sc_ex)
        v.tensor_reduce(ssum_t[:], es3, AX.X, ALU.add)
        v.drain()
        v.reciprocal(rec_t[:], ssum_t[:])
        v.drain()
        v.tensor_tensor(sm3, es3,
                        rec_t[:, :, None].to_broadcast([128, 16, 8]),
                        ALU.mult)
        v.drain()
        smA = sm_w.ap().rearrange("p (t k) -> p t k", k=8)[
            :, :, None, :].to_broadcast([128, 16, 8, 8])
        smB = sm_w.ap().rearrange("p (t j) -> p t j", j=8)[
            :, :, :, None].to_broadcast([128, 16, 8, 8])
        gt4 = gt_w.ap().rearrange("p (t j k) -> p t j k", j=8, k=8)
        eq4 = eq_w.ap().rearrange("p (t j k) -> p t j k", j=8, k=8)
        v.tensor_tensor(gt4, smA, smB, ALU.is_gt)
        v.tensor_tensor(eq4, smA, smB, ALU.is_equal)
        v.tensor_tensor(eqm_w[:], eq_w[:], ltm_s[:, :], ALU.mult)
        v.tensor_tensor(rks_w[:], gt_w[:], eqm_w[:], ALU.add)
        v.tensor_reduce(rk_w[:],
                        rks_w.ap().rearrange("p (t j k) -> p t j k", j=8,
                                             k=8).rearrange(
                            "p t j k -> p (t j) k"),
                        AX.X, ALU.add)
        v.tensor_scalar(ohst[:, 0:128], rk_w[:], 0.0, None, ALU.is_equal)
        v.tensor_scalar(ohst[:, 128:256], rk_w[:], 1.0, None, ALU.is_equal)
        v.tensor_tensor(lg_t[:], ohst[:, 0:128], sm_w[:], ALU.mult)
        v.tensor_tensor(es_t[:], ohst[:, 128:256], sm_w[:], ALU.mult)
        v.tensor_reduce(v1st[:],
                        lg_t.ap().rearrange("p (t k) -> p t k", k=8),
                        AX.X, ALU.add)
        VC.inc(v.tensor_reduce(v2st[:],
                               es_t.ap().rearrange("p (t k) -> p t k", k=8),
                               AX.X, ALU.add))
    vc_top = VC.n

    for tt in range(16):
        @TN
        def _(t, tt=tt, vc_top=vc_top):
            t.wait_ge(VC.h, max(vc_top, VC.n))
            oh1s = ohst[:, tt * 8:tt * 8 + 8]
            oh2s = ohst[:, 128 + tt * 8:128 + tt * 8 + 8]
            t.matmul(pb[4][:, 0:8], tri_s[:], oh1s, start=True, stop=False)
            t.matmul(pb[4][:, 0:8], ones1_s[:], cnt_s[0:1, 0:8], start=False,
                     stop=True)
            t.matmul(pb[5][:, 0:8], tri_s[:], oh2s, start=True, stop=False)
            t.matmul(pb[5][:, 0:8], ones1_s[:], cnt_s[0:1, 8:16], start=False,
                     stop=True)
            t.matmul(pb[6][0:1, 0:8], onesc_s[:], oh1s, start=(tt == 0),
                     stop=True, skip_group_check=True)
            TE.inc(t.matmul(pb[6][0:1, 8:16], onesc_s[:], oh2s,
                            start=False, stop=True, skip_group_check=True))
        te_pos = TE.n

        @VE
        def _(v, tt=tt, te_pos=te_pos):
            v.wait_ge(TE.h, te_pos)
            v.tensor_tensor(a1t[:], ohst[:, tt * 8:tt * 8 + 8],
                            pb[4][:, 0:8], ALU.mult)
            v.tensor_tensor(a2t[:], ohst[:, 128 + tt * 8:128 + tt * 8 + 8],
                            pb[5][:, 0:8], ALU.mult)
            v.drain()
            v.reduce_sum(plst[:, tt:tt + 1], a1t[:], axis=AX.X)
            v.reduce_sum(plst[:, 16 + tt:16 + tt + 1], a2t[:], axis=AX.X)
            VC.inc(v.tensor_copy(cnt_s[:], pb[6][0:1, 0:16]))
    vc_cnt = VC.n

    @GP
    def _(g, vc_cnt=vc_cnt):
        g.wait_ge(VC.h, vc_cnt)
        SP.inc(g.dma_start(out=agc_i[:, :],
                           in_=cnt_s.ap().rearrange("p (a b) -> p a b", a=2)),
               16)
        g.wait_ge(SP.h, SP.n)
        CC.inc(g.collective_compute(
            "AllGather", ALU.bypass, replica_groups=[list(range(NC))],
            ins=[agc_i[:, :]], outs=[agc_o[:, :]]))
        g.wait_ge(CC.h, CC.n)
        SP.inc(g.dma_start(out=agc_s[:], in_=agc_o[:, :]), 16)
    sp_agc = SP.n

    @TN
    def _(t):
        t.wait_ge(SP.h, sp_agc)
        t.matmul(pb[7][:, 0:8], offmr0_s[:], agc_s[:], start=True, stop=False)
        TE.inc(t.matmul(pb[7][:, 8:16], offmr1_s[:], agc_s[:], start=False,
                        stop=True))
    te_off = TE.n

    @VE
    def _(v, te_off=te_off):
        v.wait_ge(TE.h, te_off)
        v.tensor_copy(voff.ap().rearrange("p (k t e) -> p k t e", k=2, t=16,
                                          e=8)[:, 0:1, :, :],
                      pb[7][:, 0:8][:, None, :].to_broadcast([128, 16, 8])[
                          :, None, :, :].to_broadcast([128, 1, 16, 8]))
        VC.inc(v.tensor_copy(
            voff.ap().rearrange("p (k t e) -> p k t e", k=2, t=16,
                                e=8)[:, 1:2, :, :],
            pb[7][:, 8:16][:, None, :].to_broadcast([128, 16, 8])[
                :, None, :, :].to_broadcast([128, 1, 16, 8])))
    vc_voff = VC.n

    oh32 = ohst.ap().rearrange("p (c e) -> p c e", e=8)
    @VE
    def _(v, vc_voff=vc_voff):
        v.wait_ge(VC.h, vc_voff)
        v.drain()
        v.tensor_tensor(selo[:], ohst[:], voff[:], ALU.mult)
        v.tensor_tensor(sele[:], ohst[:], evec_s[:], ALU.mult)
        v.tensor_reduce(pos_add[:],
                        selo.ap().rearrange("p (c e) -> p c e", e=8),
                        AX.X, ALU.add)
        v.tensor_reduce(esum[:],
                        sele.ap().rearrange("p (c e) -> p c e", e=8),
                        AX.X, ALU.add)
        v.tensor_tensor(posga[:], pos_add[:], plst[:], ALU.add)
        v.tensor_tensor(sel5[:], ohst[:], e512_s[:], ALU.mult)
        v.tensor_scalar(valid_w[:], posga[:], 4096.0, None, ALU.is_lt)
        v.tensor_reduce(e512s[:],
                        sel5.ap().rearrange("p (c e) -> p c e", e=8),
                        AX.X, ALU.add)
        v.tensor_tensor(w32a[:], esum[:], posga[:], ALU.add)
        v.tensor_tensor(s12[:], v1st[:], v2st[:], ALU.add)
        v.tensor_scalar(w32b[:], w32a[:], float(TRASH), None, ALU.subtract)
        v.reciprocal(rec12[:], s12[:])
        v.tensor_tensor(w32a[:], w32b[:], valid_w[:], ALU.mult)
        v.tensor_tensor(g1a[:], v1st[:], rec12[:], ALU.mult)
        v.tensor_scalar(w32b[:], w32a[:], float(TRASH), None, ALU.add)
        v.tensor_tensor(g2a[:], v2st[:], rec12[:], ALU.mult)
        v.tensor_copy(sct[:], w32b[:])
        v.tensor_scalar(posc_w[:], posga[:], 4095.0, None, ALU.min)
        v.tensor_tensor(g1st[:], g1a[:], valid_w[:, 0:16], ALU.mult)
        v.tensor_tensor(g2st[:], g2a[:], valid_w[:, 16:32], ALU.mult)
        v.tensor_tensor(cmpq.ap().rearrange("p (c e) -> p c e", e=8),
                        posc_w[:, :, None].to_broadcast([128, 32, 8]),
                        thr_s.ap().rearrange("p (c e) -> p c e", e=8),
                        ALU.is_ge)
        v.drain()
        v.tensor_reduce(q_w[:],
                        cmpq.ap().rearrange("p (c e) -> p c e", e=8),
                        AX.X, ALU.add)
        v.drain()
        v.tensor_scalar(w32a[:], q_w[:], 512.0, None, ALU.mult)
        v.tensor_scalar(w32b[:], q_w[:], 4096.0, None, ALU.mult)
        v.tensor_tensor(cdif[:], posc_w[:], w32a[:], ALU.subtract)
        v.tensor_tensor(w32a[:], w32b[:], e512s[:], ALU.add)
        v.drain()
        v.tensor_tensor(w32b[:], w32a[:], cdif[:], ALU.add)
        v.drain()
        v.tensor_tensor(w32a[:], w32b[:], valid_w[:], ALU.mult)
        v.drain()
        VC.inc(v.tensor_copy(ybst[:], w32a[:]))
    vc_route = VC.n

    @GP
    def _(g, vc_route=vc_route):
        g.wait_ge(VC.h, vc_route)
        SP.inc(g.dma_start(
            out=invl_d.rearrange("(p c) one -> p (c one)", p=128),
            in_=zer_s[:]), 16)
        g.wait_ge(SP.h, SP.n)
        for col in range(32):
            SP.inc(g.indirect_dma_start(
                out=invl_d[:, :],
                out_offset=IndirectOffsetOnAxis(sct[:, col:col + 1], 0),
                in_=gtok_s[:, col % 16:col % 16 + 1], in_offset=None), 16)
        g.wait_ge(SP.h, SP.n)
        CC.inc(g.collective_compute(
            "AllReduce", ALU.add, replica_groups=[list(range(NC))],
            ins=[invl_d[:, :]], outs=[invg_d[:, :]]))
        g.wait_ge(CC.h, CC.n)
        for col in range(32):
            SP.inc(g.indirect_dma_start(
                out=invt[:, col:col + 1], out_offset=None,
                in_=invg_d[:, :],
                in_offset=IndirectOffsetOnAxis(einv_s[:, col:col + 1], 0)),
                16)
    sp_inv = SP.n

    ctxG.close()

    ctxF = ExitStack()
    def sbf(name, shape, dt=F32):
        return ctxF.enter_context(nc.sbuf_tensor(name, shape, dt))

    h_h = sbf("h_h", [128, 32 * CH], BF16)
    h_l = sbf("h_l", [128, 32 * CH], BF16)
    xf_h2 = [sbf(f"xf_h{i}", [128, 8 * CH], BF16) for i in range(2)]
    xf_l2 = [sbf(f"xf_l{i}", [128, 8 * CH], BF16) for i in range(2)]
    w1s_h = [sbf(f"w1s_h{i}", [128, 8, 128], BF16) for i in range(2)]
    w1s_l = [sbf(f"w1s_l{i}", [128, 8, 128], BF16) for i in range(2)]
    w2s_h = [sbf(f"w2s_h{i}", [128, D], BF16) for i in range(2)]
    w2s_l = [sbf(f"w2s_l{i}", [128, D], BF16) for i in range(2)]
    dsb2 = [sbf(f"dsb{i}", [128, 4 * 2 * D], BF16) for i in range(2)]
    h32a = sbf("h32a", [128, CH])
    h32b = sbf("h32b", [128, CH])
    yt_ = sbf("yt_", [128, CH])
    invf = sbf("invf", [128, 32])
    invg2 = sbf("invg2", [128, 32])

    @VE
    def _(v):
        v.wait_ge(SP.h, sp_inv)
        v.tensor_copy(invf[:], invt[:])
        v.drain()
        v.tensor_scalar(invg2[:], invf[:], -1.0, 0.0, ALU.add, ALU.max)
        v.drain()
        VC.inc(v.tensor_copy(invt[:], invg2[:]))
    vc_invt = VC.n

    xf_free = [TE.n, TE.n]
    w1_free = [TE.n, TE.n]
    w2_free = [TE.n, TE.n]
    h_free = [VC.n] * 4
    y2_free = [VC.n] * 8
    sp_disp_a = [0] * TC
    ld_xf_a = [0] * TC

    def emit_gath(tc):
        bb = tc % 2

        @GP
        def _(g, tc=tc, bb=bb):
            if tc == 0:
                g.wait_ge(VC.h, vc_invt)
                g.wait_ge(CC.h, cc_agx)
            if tc >= 2:
                g.wait_ge(LD.h, ld_xf_a[tc - 2])
            for q in range(4):
                SP.inc(g.indirect_dma_start(
                    out=dsb2[bb][:, q * 2 * D:(q + 1) * 2 * D],
                    out_offset=None,
                    in_=agx_o[:, :],
                    in_offset=IndirectOffsetOnAxis(
                        invt[:, tc * 4 + q:tc * 4 + q + 1], 0)), 16)
        sp_disp_a[tc] = SP.n

    def emit_transp(tc):
        bb = tc % 2

        @SY
        def _(sy, tc=tc, bb=bb):
            sy.wait_ge(SP.h, sp_disp_a[tc])
            sy.wait_ge(TE.h, xf_free[bb])
            for q in range(4):
                for fc in range(8):
                    LD.inc(sy.dma_start_transpose(
                        xf_h2[bb][:, fc * CH + q * 128:
                                  fc * CH + (q + 1) * 128],
                        dsb2[bb][:, q * 2 * D + fc * 128:
                                 q * 2 * D + (fc + 1) * 128]), 16)
                    LD.inc(sy.dma_start_transpose(
                        xf_l2[bb][:, fc * CH + q * 128:
                                  fc * CH + (q + 1) * 128],
                        dsb2[bb][:, q * 2 * D + D + fc * 128:
                                 q * 2 * D + D + (fc + 1) * 128]), 16)
        ld_xf_a[tc] = LD.n

    emit_gath(0)
    emit_transp(0)
    for tc in range(TC):
        s0 = tc * CH
        bb = tc % 2
        xf_h = xf_h2[bb]
        xf_l = xf_l2[bb]
        if tc + 1 < TC:
            emit_gath(tc + 1)
            emit_transp(tc + 1)
        ld_xf = ld_xf_a[tc]

        for hc in range(32):
            buf = hc % 2
            bank = hc % 4

            @SE
            def _(sy, tc=tc, hc=hc, buf=buf):
                if tc == 0 and hc == 0:
                    sy.wait_ge(SP.h, sp_w12)
                    sy.wait_ge(LD.h, ld_w12)
                sy.wait_ge(TE.h, w1_free[buf])
                WL.inc(sy.dma_start(
                    out=w1s_h[buf][:],
                    in_=w1h_d[hc * 128:(hc + 1) * 128, :]), 16)
                WL.inc(sy.dma_start(
                    out=w1s_l[buf][:],
                    in_=w1l_d[hc * 128:(hc + 1) * 128, :]), 16)
            ld_w1 = WL.n

            @TN
            def _(t, tc=tc, hc=hc, buf=buf, bank=bank, ld_w1=ld_w1,
                  ld_xf=ld_xf, bb=bb, xf_h=xf_h, xf_l=xf_l):
                t.wait_ge(WL.h, ld_w1)
                if hc == 0:
                    t.wait_ge(LD.h, ld_xf)
                    if tc == 0:
                        t.wait_ge(VC.h, vc_route)
                t.wait_ge(VC.h, h_free[bank])
                if tc > 0 and hc < 4:
                    t.wait_ge(VC.h, y2_free[bank])
                for fc in range(8):
                    xh = xf_h[:, fc * CH:(fc + 1) * CH]
                    xl = xf_l[:, fc * CH:(fc + 1) * CH]
                    t.matmul(pb[bank][:], w1s_h[buf][:, fc, :], xh,
                             start=(fc == 0), stop=False)
                    t.matmul(pb[bank][:], w1s_h[buf][:, fc, :], xl,
                             start=False, stop=False)
                    TE.inc(t.matmul(pb[bank][:], w1s_l[buf][:, fc, :], xh,
                                    start=False, stop=(fc == 7)))
                w1_free[buf] = TE.n
                if hc == 31:
                    xf_free[bb] = TE.n
            te_h1 = TE.n

            @VE
            def _(v, hc=hc, bank=bank, te_h1=te_h1):
                v.wait_ge(TE.h, te_h1)
                h32 = h32a if hc % 2 == 0 else h32b
                v.tensor_scalar(h32[:], pb[bank][:], b1m_s[:, hc:hc + 1],
                                0.0, ALU.add, ALU.max)
                v.tensor_copy(h_h[:, hc * CH:(hc + 1) * CH], h32[:])
                VC.inc(v.tensor_tensor(h_l[:, hc * CH:(hc + 1) * CH], h32[:],
                                       h_h[:, hc * CH:(hc + 1) * CH],
                                       ALU.subtract))
                h_free[bank] = VC.n
        vc_hdone = VC.n

        for hc in range(32):
            buf = hc % 2

            @SE
            def _(sy, hc=hc, buf=buf):
                sy.wait_ge(TE.h, w2_free[buf])
                WL.inc(sy.dma_start(
                    out=w2s_h[buf][:],
                    in_=w2h_d[hc * 128:(hc + 1) * 128, :]), 16)
                WL.inc(sy.dma_start(
                    out=w2s_l[buf][:],
                    in_=w2l_d[hc * 128:(hc + 1) * 128, :]), 16)
            ld_w2 = WL.n

            @TN
            def _(t, hc=hc, buf=buf, ld_w2=ld_w2, vc_hdone=vc_hdone):
                t.wait_ge(WL.h, ld_w2)
                if hc == 0:
                    t.wait_ge(VC.h, vc_hdone)
                    for b8 in range(8):
                        t.wait_ge(VC.h, y2_free[b8])
                for q in range(4):
                    hh = h_h[:, hc * CH + q * 128:hc * CH + (q + 1) * 128]
                    hl = h_l[:, hc * CH + q * 128:hc * CH + (q + 1) * 128]
                    for mc in range(2):
                        bank = q * 2 + mc
                        wh = w2s_h[buf][:, mc * 512:(mc + 1) * 512]
                        wl = w2s_l[buf][:, mc * 512:(mc + 1) * 512]
                        t.matmul(pb[bank][:], hh, wh, start=(hc == 0),
                                 stop=False, skip_group_check=True)
                        t.matmul(pb[bank][:], hh, wl, start=False,
                                 stop=False, skip_group_check=True)
                        TE.inc(t.matmul(pb[bank][:], hl, wh, start=False,
                                        stop=(hc == 31),
                                        skip_group_check=True))
                w2_free[buf] = TE.n
        te_y2 = TE.n

        for q in range(4):
            for mc in range(2):
                bank = q * 2 + mc

                @VE
                def _(v, bank=bank, mc=mc, te_y2=te_y2):
                    v.wait_ge(TE.h, te_y2)
                    v.wait_ge(SP.h, SP.n)
                    VC.inc(v.tensor_tensor(
                        yt_[:], pb[bank][:],
                        b2r_s[:, mc * 512:(mc + 1) * 512], ALU.add))
                    y2_free[bank] = VC.n
                vc_y = VC.n

                @GP
                def _(g, q=q, mc=mc, s0=s0, vc_y=vc_y):
                    g.wait_ge(VC.h, vc_y)
                    SP.inc(g.dma_start(
                        out=agy_i[s0 + q * 128:s0 + (q + 1) * 128,
                                  mc * 512:(mc + 1) * 512], in_=yt_[:]), 16)

        @GP
        def _(g, tc=tc, s0=s0):
            g.wait_ge(SP.h, SP.n)
            CC.inc(g.collective_compute(
                "AllGather", ALU.bypass, replica_groups=[list(range(NC))],
                ins=[agy_i[s0:s0 + CH, :]],
                outs=[agy_o[tc * CAP:(tc + 1) * CAP, :]]))
    cc_ally = CC.n

    y1t2 = [sbf(f"y1t{i}", [128, D]) for i in range(2)]
    y2t2 = [sbf(f"y2t{i}", [128, D]) for i in range(2)]
    x1r2 = [sbf(f"x1r{i}", [128, D]) for i in range(2)]
    fin2 = [sbf(f"fin{i}", [128, D]) for i in range(2)]
    fin0s = sbf("fin0s", [128, D])

    vc_fin2 = [0] * 18
    sp_st2 = [0] * 18
    for tt in range(16):
        b = tt % 2
        y1t = y1t2[b]; y2t = y2t2[b]; x1r = x1r2[b]
        fin = fin2[b]; fin0 = fin0s

        @GP
        def _(g, tt=tt, y1t=y1t, y2t=y2t, x1r=x1r):
            if tt == 0:
                g.wait_ge(CC.h, cc_ally)
            if tt > 1:
                g.wait_ge(VC.h, vc_fin2[tt])
            SP.inc(g.indirect_dma_start(
                out=y1t[:, :], out_offset=None, in_=agy_o[:, :],
                in_offset=IndirectOffsetOnAxis(ybst[:, tt:tt + 1], 0)), 16)
            SP.inc(g.indirect_dma_start(
                out=y2t[:, :], out_offset=None, in_=agy_o[:, :],
                in_offset=IndirectOffsetOnAxis(ybst[:, 16 + tt:16 + tt + 1],
                                               0)), 16)
            SP.inc(g.dma_start(out=x1r[:],
                               in_=x1_d[tt * 128:(tt + 1) * 128, :]), 16)
        sp_y = SP.n

        @VE
        def _(v, tt=tt, sp_y=sp_y, y1t=y1t, y2t=y2t, x1r=x1r, fin=fin,
              fin0=fin0):
            v.wait_ge(SP.h, sp_y)
            if tt > 1:
                v.wait_ge(SP.h, sp_st2[tt])
            v.scalar_tensor_tensor(fin0[:], y1t[:], g1st[:, tt:tt + 1],
                                   x1r[:], ALU.mult, ALU.add)
            VC.inc(v.scalar_tensor_tensor(fin[:], y2t[:], g2st[:, tt:tt + 1],
                                          fin0[:], ALU.mult, ALU.add))
        vc_fin2[tt + 2] = VC.n
        vc_fin = VC.n

        @GP
        def _(g, tt=tt, vc_fin=vc_fin, fin=fin):
            g.wait_ge(VC.h, vc_fin)
            SP.inc(g.dma_start(out=y_out[tt * 128:(tt + 1) * 128, :],
                               in_=fin[:]), 16)
        sp_st2[tt + 2] = SP.n

    @GP
    def _(g):
        g.wait_ge(SP.h, SP.n)

    ctxF.close()
    ctx.close()
    return nc


def prep_inputs(inputs):
    x = np.ascontiguousarray(np.asarray(inputs["L0_attn_x"], np.float32)
                             .reshape(NC * T, D))
    k1 = np.asarray(inputs["L0_attn_k1"], np.float32)
    k2 = np.asarray(inputs["L0_attn_k2"], np.float32)
    k3 = np.asarray(inputs["L0_attn_k3"], np.float32)
    a1 = np.asarray(inputs["L0_attn_a1"], np.float32).reshape(H, 64)
    a2 = np.asarray(inputs["L0_attn_a2"], np.float32).reshape(H, 64)
    b1 = np.asarray(inputs["L0_attn_b1"], np.float32).reshape(H, 64)
    b2 = np.asarray(inputs["L0_attn_b2"], np.float32).reshape(H, 64)
    c_ = np.asarray(inputs["L0_attn_c"], np.float32).reshape(H, 64)
    vw = np.asarray(inputs["L0_attn_value_weight"], np.float32)
    ow = np.asarray(inputs["L0_attn_output_weight"], np.float32)
    wg = np.asarray(inputs["L0_ffn_wg"], np.float32)
    w1 = np.asarray(inputs["L0_ffn_w1"], np.float32)
    fb1 = np.asarray(inputs["L0_ffn_b1"], np.float32)
    w2 = np.asarray(inputs["L0_ffn_w2"], np.float32)
    fb2 = np.asarray(inputs["L0_ffn_b2"], np.float32)

    wkt = np.ascontiguousarray(np.concatenate([k1, k2, k3], 0).T)
    wv2 = np.ascontiguousarray(vw.transpose(1, 0, 2).reshape(D, D))
    ow2 = np.ascontiguousarray(ow.transpose(0, 2, 1).reshape(D, D))

    cMm = np.zeros((128, 128), np.float32)
    for sc in range(8):
        for lh in range(2):
            h = sc * 2 + lh
            cMm[lh * 64:(lh + 1) * 64, sc * 16 + h] = c_[h]
    pc = {}
    for nm, arr in (("a1c", a1), ("b1c", b1), ("a2c", a2), ("b2c", b2)):
        pc[nm] = np.ascontiguousarray(arr.reshape(8, 128).T)

    trim = np.ascontiguousarray(np.triu(np.ones((128, 128), np.float32), 1))
    bselm = np.zeros((16, 1024), np.float32)
    for oc in range(8):
        for lh in range(2):
            h = oc * 2 + lh
            bselm[h, oc * 128 + lh * 64:oc * 128 + (lh + 1) * 64] = 1.0
    evec = np.ascontiguousarray(np.broadcast_to(
        np.tile(np.arange(8, dtype=np.float32) * 4096.0, 32), (128, 256)))
    e512 = np.ascontiguousarray(np.broadcast_to(
        np.tile(np.arange(8, dtype=np.float32) * 512.0, 32), (128, 256)))
    thr = np.ascontiguousarray(np.broadcast_to(
        np.tile(np.arange(1, 9, dtype=np.float32) * 512.0, 32), (128, 256)))
    ltm = np.zeros((8, 8), np.float32)
    for j in range(8):
        ltm[j, :j] = 1.0
    ltm = np.ascontiguousarray(np.broadcast_to(
        np.tile(ltm.reshape(64), 16), (128, 1024)))
    ones1 = np.ones((1, 128), np.float32)
    onesc = np.ones((128, 1), np.float32)

    in_maps = []
    for j in range(NC):
        b1mh = np.ascontiguousarray(fb1[j].reshape(32, 128).T)
        b2rh = np.ascontiguousarray(
            np.broadcast_to(fb2[j].reshape(1, D), (128, D)))
        b = j // 4
        nloc = np.ascontiguousarray(np.broadcast_to(
            ((np.arange(T) + (j % 4) * T).astype(np.float32) / 8191.0)
            .reshape(1, T), (128, T)))
        seqm = np.zeros((8, 1), np.float32)
        seqm[4 * b:j, 0] = 1.0
        offm = np.zeros((16, 2), np.float32)
        offm[0:2 * j:2, 0] = 1.0
        offm[0::2, 1] = 1.0
        offm[1:2 * j:2, 1] = 1.0
        offmr0 = np.ascontiguousarray(np.broadcast_to(offm[:, 0:1], (16, 128)))
        offmr1 = np.ascontiguousarray(np.broadcast_to(offm[:, 1:2], (16, 128)))
        gt = (np.arange(T, dtype=np.int64) + j * T + 1).astype(np.int32)
        gtok = np.ascontiguousarray(gt.reshape(16, 128).T)
        einv = np.ascontiguousarray(
            (j * 4096 + np.arange(32, dtype=np.int32)[None, :] * 128
             + np.arange(128, dtype=np.int32)[:, None]).astype(np.int32))
        in_maps.append({
            "x": x[j * T:(j + 1) * T],
            "wkt": wkt, "wv2": wv2, "ow2": ow2, "wgm": wg,
            "w1m": w1[j], "w2m": w2[j], "b1m": b1mh, "b2row": b2rh,
            "cM": cMm, "a1c": pc["a1c"], "b1c": pc["b1c"],
            "a2c": pc["a2c"], "b2c": pc["b2c"],
            "tri": trim, "bsel": bselm,
            "nvec": nloc, "seqm": seqm,
            "offm": offm, "offmr0": offmr0, "offmr1": offmr1,
            "gtok": gtok, "evec": evec, "e512": e512,
            "thr": thr, "ltm": ltm, "ones1": ones1, "onesc": onesc,
            "einv": einv,
        })
    return in_maps


def kernel(**inputs):
    nc = build_program()
    in_maps = prep_inputs(inputs)
    res = run_bass_kernel_spmd(nc, in_maps, core_ids=list(range(NC)))
    out = np.concatenate([res.results[j]["y_out"] for j in range(NC)], axis=0)
    return out.reshape(2, 8192, 1024)


# version 12
# speedup vs baseline: 1.3593x; 1.0059x over previous
from contextlib import ExitStack

import numpy as np

import concourse.bass as bass
import concourse.mybir as mybir
from concourse.bass_utils import run_bass_kernel_spmd
from concourse.bass_core_rust import IndirectOffsetOnAxis

F32 = mybir.dt.float32
BF16 = mybir.dt.bfloat16
I32 = mybir.dt.int32
ALU = mybir.AluOpType
AF = mybir.ActivationFunctionType
AX = mybir.AxisListType

NC = 8
T = 2048
D = 1024
H = 16
HD = 4096
CAP = 4096
NT = 4
CH = 512
TC = 8
TRASH = 32768


class Sem:
    def __init__(self, nc, ctx, name):
        self.h = ctx.enter_context(nc.semaphore(name))
        self.n = 0

    def inc(self, inst, k=1):
        inst.then_inc(self.h, k)
        self.n += k
        return self.n


def build_program():
    nc = bass.Bass("TRN2", target_bir_lowering=False,
                   detect_race_conditions=False)
    ctx = ExitStack()

    def din(name, shape, dt=F32):
        return nc.dram_tensor(name, shape, dt, kind="ExternalInput").ap()

    x_in = din("x", [T, D])
    wkt = din("wkt", [D, 48])
    wv2 = din("wv2", [D, D])
    ow2 = din("ow2", [D, D])
    wgm = din("wgm", [D, 8])
    w1m = din("w1m", [D, HD])
    w2m = din("w2m", [HD, D])
    b1m_i = din("b1m", [128, 32])
    b2r_i = din("b2row", [128, D])
    cM_i = din("cM", [128, 128])
    a1_i = din("a1c", [128, 8])
    b1_i = din("b1c", [128, 8])
    a2_i = din("a2c", [128, 8])
    b2_i = din("b2c", [128, 8])
    tri_i = din("tri", [128, 128])
    bsel_i = din("bsel", [16, 1024])
    nvec_i = din("nvec", [128, T])
    seqm_i = din("seqm", [8, 1])
    offm_i = din("offm", [16, 2])
    offr0_i = din("offmr0", [16, 128])
    offr1_i = din("offmr1", [16, 128])
    gtok_i = din("gtok", [128, 16], I32)
    evec_i = din("evec", [128, 256])
    e512_i = din("e512", [128, 256])
    thr_i = din("thr", [128, 256])
    ltm_i = din("ltm", [128, 1024])
    ones1_i = din("ones1", [1, 128])
    onesc_i = din("onesc", [128, 1])
    einv_i = din("einv", [128, 32], I32)

    y_out = nc.dram_tensor("y_out", [T, D], F32, kind="ExternalOutput").ap()

    def dint(name, shape, dt=F32):
        return nc.dram_tensor(name, shape, dt).ap()

    xnh_d = dint("xnh_d", [T, D], BF16)
    xnl_d = dint("xnl_d", [T, D], BF16)
    s_d = dint("s_d", [8, 128, T])
    v_d = dint("v_d", [8, 128, T])
    x1_d = dint("x1_d", [T, D])
    p12_d = dint("p12_d", [16, 2 * T])
    agcar_i = dint("agcar_i", [1, 1040])
    agcar_o = dint("agcar_o", [8, 1040])
    agx_i = dint("agx_i", [T, 2 * D], BF16)
    agx_o = dint("agx_o", [NC * T, 2 * D], BF16)
    agc_i = dint("agc_i", [2, 8])
    agc_o = dint("agc_o", [16, 8])
    invl_d = dint("invl_d", [33024, 1], I32)
    invg_d = dint("invg_d", [33024, 1], I32)
    agy_i = dint("agy_i", [CAP, D])
    agy_o = dint("agy_o", [NC * CAP, D])
    w1h_d = dint("w1h_d", [HD, D], BF16)
    w1l_d = dint("w1l_d", [HD, D], BF16)
    w2h_d = dint("w2h_d", [HD, D], BF16)
    w2l_d = dint("w2l_d", [HD, D], BF16)

    def sb(name, shape, dt=F32):
        return ctx.enter_context(nc.sbuf_tensor(name, shape, dt))

    wk_h = sb("wk_h", [128, 8, 48], BF16)
    wk_l = sb("wk_l", [128, 8, 48], BF16)
    wg_h = sb("wg_h", [128, 8, 8], BF16)
    wg_l = sb("wg_l", [128, 8, 8], BF16)
    a1s = sb("a1s", [128, 8]); b1s = sb("b1s", [128, 8])
    a2s = sb("a2s", [128, 8]); b2s = sb("b2s", [128, 8])
    tri_s = sb("tri_s", [128, 128])
    bsel_s = sb("bsel_s", [16, 1024])
    seqm_s = sb("seqm_s", [8, 1])
    offm_s = sb("offm_s", [16, 2])
    gtok_s = sb("gtok_s", [128, 16], I32)
    evec_s = sb("evec_s", [128, 256])
    e512_s = sb("e512_s", [128, 256])
    thr_s = sb("thr_s", [128, 256])
    ltm_s = sb("ltm_s", [128, 1024])
    ones1_s = sb("ones1_s", [1, 128])
    onesc_s = sb("onesc_s", [128, 1])
    b1m_s = sb("b1m_s", [128, 32])
    b2r_s = sb("b2r_s", [128, D])
    einv_s = sb("einv_s", [128, 32], I32)
    scarL = [sb(f"scar{i}", [128, 1]) for i in range(8)]
    Zcar = sb("Zcar", [16, 1])
    car2 = sb("car2", [128, 8])
    zc2 = sb("zc2", [16, 1])
    agc_s = sb("agc_s", [16, 8])
    offmr0_s = sb("offmr0_s", [16, 128])
    offmr1_s = sb("offmr1_s", [16, 128])
    g1st = sb("g1st", [128, 16]); g2st = sb("g2st", [128, 16])
    v1st = sb("v1st", [128, 16]); v2st = sb("v2st", [128, 16])
    ybst = sb("ybst", [128, 32], I32)
    sct = sb("sct", [128, 32], I32)
    zer_s = sb("zer_s", [128, 258], I32)
    invt = sb("invt", [128, 32], I32)
    cnt_s = sb("cnt_s", [1, 16])

    pb = [ctx.enter_context(nc.psum_tensor(f"pb{i}", [128, 512], F32))
          for i in range(8)]

    LD = Sem(nc, ctx, "LD")
    SP = Sem(nc, ctx, "SP")
    VC = Sem(nc, ctx, "VC")
    SC = Sem(nc, ctx, "SC")
    TE = Sem(nc, ctx, "TE")
    CC = Sem(nc, ctx, "CC")
    WL = Sem(nc, ctx, "WL")

    block = ctx.enter_context(nc.Block())

    def SY(f): block.sync(f)
    def GP(f): block.gpsimd(f)
    def VE(f): block.vector(f)
    def SE(f): block.scalar(f)
    def TN(f): block.tensor(f)

    ctxA = ExitStack()
    def sba(name, shape, dt=F32):
        return ctxA.enter_context(nc.sbuf_tensor(name, shape, dt))

    wv_h = sba("wv_h", [128, 8 * D], BF16)
    wv_l = sba("wv_l", [128, 8 * D], BF16)
    ow_h = sba("ow_h", [128, 8 * D], BF16)
    ow_l = sba("ow_l", [128, 8 * D], BF16)
    Z_s = sba("Z_s", [16, T])
    r_s = sba("r_s", [16, T])
    dinv_s = sba("dinv_s", [16, T])
    r_s2 = sba("r_s2", [16, T])
    ss_t = sba("ss_t", [128, CH])
    vv_t = sba("vv_t", [128, CH])
    xt2 = [sba(f"xt{i}", [128, D]) for i in range(2)]
    xc2 = [sba(f"xc{i}", [128, D]) for i in range(2)]
    xn322 = [sba(f"xn32{i}", [128, D]) for i in range(2)]
    xnh2 = [sba(f"xnh{i}", [128, D], BF16) for i in range(2)]
    xnl2 = [sba(f"xnl{i}", [128, D], BF16) for i in range(2)]
    sm12 = [sba(f"sm1{i}", [128, 8]) for i in range(2)]

    ctxW = ExitStack()
    def sbw(name, shape, dt=F32):
        return ctxW.enter_context(nc.sbuf_tensor(name, shape, dt))

    wtmp2 = [sbw(f"wtmp{i}", [128, 2048]) for i in range(2)]
    wth2 = [sbw(f"wth{i}", [128, 2048], BF16) for i in range(2)]
    wtl2 = [sbw(f"wtl{i}", [128, 2048], BF16) for i in range(2)]
    wkt32 = sbw("wkt32", [128, 8, 48])
    wg32 = sbw("wg32", [128, 8, 8])
    cM_s = sbw("cM_s", [128, 128])
    nvec_s = sbw("nvec_s", [128, T])

    @SY
    def _(sy):
        for t, d_ in ((cM_s, cM_i), (tri_s, tri_i), (bsel_s, bsel_i),
                      (nvec_s, nvec_i), (seqm_s, seqm_i), (offm_s, offm_i),
                      (gtok_s, gtok_i), (evec_s, evec_i), (e512_s, e512_i),
                      (thr_s, thr_i), (ltm_s, ltm_i), (ones1_s, ones1_i),
                      (onesc_s, onesc_i), (b1m_s, b1m_i), (b2r_s, b2r_i),
                      (a1s, a1_i), (b1s, b1_i), (a2s, a2_i), (b2s, b2_i),
                      (einv_s, einv_i), (offmr0_s, offr0_i),
                      (offmr1_s, offr1_i)):
            LD.inc(sy.dma_start(out=t[:], in_=d_[:, :]), 16)
        LD.inc(sy.dma_start(
            out=wkt32[:], in_=wkt.rearrange("(f p) c -> p f c", p=128)), 16)
        LD.inc(sy.dma_start(
            out=wg32[:], in_=wgm.rearrange("(f p) c -> p f c", p=128)), 16)
    ld_consts = LD.n

    @VE
    def _(v):
        v.wait_ge(LD.h, ld_consts)
        for i in range(8):
            v.memset(scarL[i][:], 0.0)
        v.memset(Zcar[:], 0.0)
        v.memset(cnt_s[:], 0.0)
        v.memset(zer_s[:], 0)
        v.tensor_copy(wk_h[:], wkt32[:])
        v.tensor_copy(wg_h[:], wg32[:])
        v.tensor_tensor(wk_l[:], wkt32[:], wk_h[:], ALU.subtract)
        VC.inc(v.tensor_tensor(wg_l[:], wg32[:], wg_h[:], ALU.subtract))
    vc_consts = VC.n

    vfree = [VC.n, VC.n]
    for ji, (wsrc, dsth, dstl) in enumerate(
            [(wv2, wv_h, wv_l), (ow2, ow_h, ow_l)]):
        for fc in range(8):
            b = (ji * 8 + fc) % 2
            @SY
            def _(sy, fc=fc, b=b, wsrc=wsrc):
                sy.wait_ge(VC.h, vfree[b])
                LD.inc(sy.dma_start(out=wtmp2[b][:, 0:1024],
                                    in_=wsrc[fc * 128:(fc + 1) * 128, :]), 16)
            ldc = LD.n

            @VE
            def _(v, fc=fc, b=b, ldc=ldc, dsth=dsth, dstl=dstl):
                v.wait_ge(LD.h, ldc)
                v.tensor_copy(dsth[:, fc * D:(fc + 1) * D],
                              wtmp2[b][:, 0:1024])
                VC.inc(v.tensor_tensor(dstl[:, fc * D:(fc + 1) * D],
                                       wtmp2[b][:, 0:1024],
                                       dsth[:, fc * D:(fc + 1) * D],
                                       ALU.subtract))
                vfree[b] = VC.n



    def emit_w12_prep():
        vfree2 = [VC.n, VC.n]
        spfree = [SP.n, SP.n]
        lfree = [LD.n, LD.n]
        jobs = []
        for fc in range(8):
            for hh in range(2):
                jobs.append(("w1", fc, hh))
        for hc2 in range(16):
            jobs.append(("w2", hc2, 0))
        for ji, job in enumerate(jobs):
            b = ji % 2
            kind, i0, i1 = job

            @SE
            def _(sy, kind=kind, i0=i0, i1=i1, b=b, ji=ji):
                sy.wait_ge(VC.h, vfree2[b])
                if kind == "w1":
                    WL.inc(sy.dma_start(
                        out=wtmp2[b][:],
                        in_=w1m[i0 * 128:(i0 + 1) * 128,
                                i1 * 2048:(i1 + 1) * 2048]), 16)
                else:
                    WL.inc(sy.dma_start(
                        out=wtmp2[b][:],
                        in_=w2m.rearrange("(c p) m -> p c m", p=128)[
                            :, i0 * 2:(i0 + 1) * 2, :]), 16)
            ldc = WL.n

            @VE
            def _(v, b=b, ldc=ldc):
                v.wait_ge(WL.h, ldc)
                v.wait_ge(SP.h, spfree[b])
                v.wait_ge(LD.h, lfree[b])
                v.tensor_copy(wth2[b][:], wtmp2[b][:])
                v.drain()
                VC.inc(v.tensor_tensor(wtl2[b][:], wtmp2[b][:], wth2[b][:],
                                       ALU.subtract))
                vfree2[b] = VC.n
            vcc = VC.n

            @GP
            def _(g, kind=kind, i0=i0, i1=i1, b=b, vcc=vcc):
                g.wait_ge(VC.h, vcc)
                if kind == "w1":
                    SP.inc(g.dma_start(
                        out=w1h_d.rearrange("(H q) (f h) -> q H f h", q=128,
                                            h=128)[:, i1 * 16:(i1 + 1) * 16,
                                                   i0, :],
                        in_=wth2[b][:]), 16)
                else:
                    SP.inc(g.dma_start(
                        out=w2h_d.rearrange("(c p) m -> p c m", p=128)[
                            :, i0 * 2:(i0 + 1) * 2, :], in_=wth2[b][:]), 16)
                    SP.inc(g.dma_start(
                        out=w2l_d.rearrange("(c p) m -> p c m", p=128)[
                            :, i0 * 2:(i0 + 1) * 2, :], in_=wtl2[b][:]), 16)
                spfree[b] = SP.n

            if kind == "w1":
                @SY
                def _(sy, i0=i0, i1=i1, b=b, vcc=vcc):
                    sy.wait_ge(VC.h, vcc)
                    LD.inc(sy.dma_start(
                        out=w1l_d.rearrange("(H q) (f h) -> q H f h", q=128,
                                            h=128)[:, i1 * 16:(i1 + 1) * 16,
                                                   i0, :],
                        in_=wtl2[b][:]), 16)
                lfree[b] = LD.n


    def emit_p12():
        p12e = ctxW.enter_context(nc.sbuf_tensor("p12e", [16, CH], F32))
        sfree = [TE.n, TE.n]
        for k in range(2):
            aa, bb = (a1s, b1s) if k == 0 else (a2s, b2s)
            for nt in range(NT):
                t0 = nt * CH
                for sc8 in range(8):
                    b = sc8 % 2

                    @VE
                    def _(v, t0=t0, sc8=sc8, b=b, aa=aa, bb=bb):
                        v.wait_ge(LD.h, ld_consts)
                        v.wait_ge(TE.h, sfree[b])
                        VC.inc(v.tensor_scalar(
                            wtmp2[b][:, 0:CH], nvec_s[:, t0:t0 + CH],
                            aa[:, sc8:sc8 + 1], bb[:, sc8:sc8 + 1],
                            ALU.mult, ALU.add))
                    vca = VC.n

                    @SE
                    def _(s, b=b, vca=vca):
                        s.wait_ge(VC.h, vca)
                        SC.inc(s.activation(wtmp2[b][:, 0:CH],
                                            wtmp2[b][:, 0:CH], AF.Sin))
                    @TN
                    def _(t, sc8=sc8, b=b, scn=SC.n, k=k, nt=nt):
                        t.wait_ge(SC.h, scn)
                        if sc8 == 0:
                            t.wait_ge(VC.h, p12free[0])
                        TE.inc(t.matmul(pb[7][0:16, :],
                                        cM_s[:, sc8 * 16:(sc8 + 1) * 16],
                                        wtmp2[b][:, 0:CH], start=(sc8 == 0),
                                        stop=(sc8 == 7)))
                    sfree[b] = TE.n

                @VE
                def _(v, ten=TE.n):
                    v.wait_ge(TE.h, ten)
                    VC.inc(v.tensor_copy(p12e[:], pb[7][0:16, :]))
                p12free[0] = VC.n

                @GP
                def _(g, k=k, t0=t0, vcn=VC.n):
                    g.wait_ge(VC.h, vcn)
                    SP.inc(g.dma_start(
                        out=p12_d[:, k * T + t0:k * T + t0 + CH],
                        in_=p12e[:]), 16)

    p12free = [VC.n]
    emit_w12_prep()
    ld_w12 = LD.n
    emit_p12()
    sp_w12 = SP.n
    ctxW.close()

    vc_apply = [0] * 18
    sp_p1_tiles = [0] * 16
    for tt in range(16):
        b = tt % 2
        xt = xt2[b]; xc_ = xc2[b]; xn32 = xn322[b]
        xnh_t = xnh2[b]; xnl_t = xnl2[b]; sm = sm12[b]
        s1_ = sm[:, 0:1]; s2_ = sm[:, 1:2]; mu_ = sm[:, 2:3]
        var_ = sm[:, 3:4]; isd_ = sm[:, 4:5]; s2b_ = sm[:, 5:6]
        nmu_ = sm[:, 6:7]

        @GP
        def _(g, tt=tt, xt=xt):
            if tt > 1:
                g.wait_ge(VC.h, vc_apply[tt])
            SP.inc(g.dma_start(out=xt[:],
                               in_=x_in[tt * 128:(tt + 1) * 128, :]), 16)
        sp_x = SP.n

        @SE
        def _(s, sp_x=sp_x, xt=xt, xn32=xn32, s1_=s1_, s2_=s2_):
            s.wait_ge(SP.h, sp_x)
            s.activation(xn32[:], xt[:], AF.Square, accum_out=s2_)
            SC.inc(s.activation(xn32[:], xt[:], AF.Identity, accum_out=s1_))
        sc_sq = SC.n

        @VE
        def _(v, sc_sq=sc_sq, s1_=s1_, s2_=s2_, mu_=mu_, var_=var_,
              s2b_=s2b_, nmu_=nmu_):
            v.wait_ge(SC.h, sc_sq)
            v.tensor_scalar(mu_, s1_, 1.0 / D, None, ALU.mult)
            v.tensor_scalar(s2b_, s2_, 1.0 / D, 1e-5, ALU.mult, ALU.add)
            v.tensor_scalar(nmu_, mu_, -1.0, None, ALU.mult)
            v.tensor_tensor(var_, mu_, mu_, ALU.mult)
            v.drain()
            v.tensor_tensor(s1_, s2b_, var_, ALU.subtract)
            v.drain()
            VC.inc(v.reciprocal(var_, s1_))
        vc_st = VC.n

        @SE
        def _(s, vc_st=vc_st, isd_=isd_, var_=var_):
            s.wait_ge(VC.h, vc_st)
            SC.inc(s.activation(isd_, var_, AF.Sqrt))
        sc_ap = SC.n

        @VE
        def _(v, tt=tt, sc_ap=sc_ap, xt=xt, xc_=xc_, xn32=xn32,
              xnh_t=xnh_t, xnl_t=xnl_t, nmu_=nmu_, isd_=isd_):
            v.wait_ge(SC.h, sc_ap)
            if tt > 1:
                v.wait_ge(SP.h, sp_p1_tiles[tt - 2])
            v.tensor_scalar(xc_[:], xt[:], nmu_, None, ALU.add)
            v.tensor_scalar(xn32[:], xc_[:], isd_, None, ALU.mult)
            v.tensor_copy(xnh_t[:], xn32[:])
            VC.inc(v.tensor_tensor(xnl_t[:], xn32[:], xnh_t[:],
                                   ALU.subtract))
        vc_apply[tt + 2] = VC.n

        @GP
        def _(g, tt=tt, xnh_t=xnh_t, xnl_t=xnl_t, vcc=VC.n):
            g.wait_ge(VC.h, vcc)
            SP.inc(g.dma_start(out=xnh_d[tt * 128:(tt + 1) * 128, :],
                               in_=xnh_t[:]), 16)
            SP.inc(g.dma_start(out=xnl_d[tt * 128:(tt + 1) * 128, :],
                               in_=xnl_t[:]), 16)
        sp_p1_tiles[tt] = SP.n

    xt = xt2[0]; xc_ = xc2[0]; xn32 = xn322[0]
    xnh_t = xnh2[0]; xnl_t = xnl2[0]
    sm = sm12[0]
    s1_ = sm[:, 0:1]; s2_ = sm[:, 1:2]; mu_ = sm[:, 2:3]
    var_ = sm[:, 3:4]; isd_ = sm[:, 4:5]; s2b_ = sm[:, 5:6]
    nmu_ = sm[:, 6:7]

    ctxP2 = ExitStack()
    def sb2(name, shape, dt=F32):
        return ctxP2.enter_context(nc.sbuf_tensor(name, shape, dt))

    xfm_h = sb2("xfm_h", [128, 8 * CH], BF16)
    xfm_l = sb2("xfm_l", [128, 8 * CH], BF16)
    p1_t = sb2("p1_t", [16, CH]); p2raw = sb2("p2raw", [16, CH])
    c_t = sb2("c_t", [16, CH]); d_t = sb2("d_t", [16, CH])
    e_t = sb2("e_t", [16, CH]); p2_t = sb2("p2_t", [16, CH])
    rt1 = sb2("rt1", [16, CH])
    cx_t = sb2("cx_t", [128, CH])
    cv_t = sb2("cv_t", [128, CH])

    cx_evac = [VC.n]
    vbank_free = {4: VC.n, 5: VC.n, 6: VC.n, 7: VC.n}
    sc_scar = [SC.n]
    vc_p12used = [VC.n]

    for nt in range(NT):
        t0 = nt * CH

        @SY
        def _(sy, nt=nt, t0=t0):
            sy.wait_ge(SP.h, sp_p1_tiles[4 * nt + 3])
            if nt > 0:
                sy.wait_ge(TE.h, TE.n)
            for fc in range(8):
                LD.inc(sy.dma_start_transpose(
                    xfm_h[:, fc * CH:(fc + 1) * CH],
                    xnh_d[t0:t0 + CH, fc * 128:(fc + 1) * 128]), 16)
                LD.inc(sy.dma_start_transpose(
                    xfm_l[:, fc * CH:(fc + 1) * CH],
                    xnl_d[t0:t0 + CH, fc * 128:(fc + 1) * 128]), 16)
        ld_xfm = LD.n

        @GP
        def _(g, t0=t0, nt=nt):
            if nt > 0:
                g.wait_ge(VC.h, vc_p12used[0])
            SP.inc(g.dma_start(out=p1_t[:], in_=p12_d[:, t0:t0 + CH]), 16)
            SP.inc(g.dma_start(out=p2raw[:],
                               in_=p12_d[:, T + t0:T + t0 + CH]), 16)
        sp_p12 = SP.n

        @TN
        def _(t, nt=nt, ld_xfm=ld_xfm):
            t.wait_ge(LD.h, ld_xfm)
            t.wait_ge(VC.h, VC.n)
            for kind in range(3):
                for fc in range(8):
                    xh = xfm_h[:, fc * CH:(fc + 1) * CH]
                    xl = xfm_l[:, fc * CH:(fc + 1) * CH]
                    kh = wk_h[:, fc, kind * 16:(kind + 1) * 16]
                    kl = wk_l[:, fc, kind * 16:(kind + 1) * 16]
                    t.matmul(pb[kind][0:16, :], kh, xh,
                             start=(fc == 0), stop=False)
                    t.matmul(pb[kind][0:16, :], kh, xl, start=False,
                             stop=False)
                    TE.inc(t.matmul(pb[kind][0:16, :], kl, xh,
                                    start=False, stop=(fc == 7)))
        te_p2m = TE.n

        @VE
        def _(v, te_p2m=te_p2m, sp_p12=sp_p12):
            v.wait_ge(TE.h, te_p2m)
            v.wait_ge(SP.h, sp_p12)
            v.tensor_tensor(rt1[:], pb[0][0:16, :], p1_t[:], ALU.add)
            v.tensor_scalar(d_t[:], pb[1][0:16, :], 20.0, -20.0, ALU.min,
                            ALU.max)
            v.tensor_scalar(e_t[:], pb[2][0:16, :], 20.0, -20.0, ALU.min,
                            ALU.max)
            v.tensor_scalar(p2_t[:], p2raw[:], 20.0, -20.0, ALU.min,
                            ALU.max)
            VC.inc(v.tensor_scalar(c_t[:], rt1[:], 20.0, -20.0, ALU.min,
                                   ALU.max))
        vc_clip = VC.n
        vc_p12used[0] = VC.n

        @SE
        def _(s, vc_clip=vc_clip):
            s.wait_ge(VC.h, vc_clip)
            s.activation(c_t[:], c_t[:], AF.Exp)
            s.activation(d_t[:], d_t[:], AF.Exp)
            s.activation(e_t[:], e_t[:], AF.Exp)
            SC.inc(s.activation(p2_t[:], p2_t[:], AF.Exp))
        sc_exp = SC.n

        @VE
        def _(v, t0=t0, sc_exp=sc_exp):
            v.wait_ge(SC.h, sc_exp)
            v.reciprocal(rt1[:], d_t[:])
            v.tensor_tensor(ss_t[0:16, :], p2_t[:], e_t[:], ALU.mult)
            v.tensor_tensor(r_s[:, t0:t0 + CH], ss_t[0:16, :], rt1[:],
                            ALU.mult)
            VC.inc(v.tensor_tensor_scan(Z_s[:, t0:t0 + CH], onesc_s[0:16, 0:1].to_broadcast([16, CH]),
                                        c_t[:], Zcar[:, 0:1], ALU.mult,
                                        ALU.add))
        vc_zr = VC.n

        @SE
        def _(s, t0=t0, vc_zr=vc_zr):
            s.wait_ge(VC.h, vc_zr)
            SC.inc(s.copy(Zcar[:], Z_s[:, t0 + CH - 1:t0 + CH]))
        sc_zcar = SC.n

        for oc in range(8):
            bank = 4 + (oc % 4)

            @TN
            def _(t, nt=nt, oc=oc, bank=bank, vc_zr=vc_zr):
                t.wait_ge(VC.h, vc_zr)
                t.wait_ge(VC.h, cx_evac[0])
                t.matmul(pb[0][:], bsel_s[:, oc * 128:(oc + 1) * 128],
                         c_t[:], start=True, stop=True)
                t.wait_ge(VC.h, vbank_free[bank])
                for fc in range(8):
                    xh = xfm_h[:, fc * CH:(fc + 1) * CH]
                    xl = xfm_l[:, fc * CH:(fc + 1) * CH]
                    wvh = wv_h[:, fc * D + oc * 128:fc * D + (oc + 1) * 128]
                    wvl = wv_l[:, fc * D + oc * 128:fc * D + (oc + 1) * 128]
                    t.matmul(pb[bank][:], wvh, xh, start=(fc == 0),
                             stop=False)
                    t.matmul(pb[bank][:], wvh, xl, start=False, stop=False)
                    TE.inc(t.matmul(pb[bank][:], wvl, xh, start=False,
                                    stop=(fc == 7)))
            te_v = TE.n

            @VE
            def _(v, nt=nt, oc=oc, bank=bank, te_v=te_v):
                v.wait_ge(TE.h, te_v)
                v.wait_ge(SP.h, SP.n)
                v.wait_ge(SC.h, sc_scar[0])
                VC.inc(v.tensor_copy(cx_t[:], pb[0][:]))
                cx_evac[0] = VC.n
                v.tensor_tensor(cv_t[:], pb[bank][:], cx_t[:], ALU.mult)
                v.tensor_copy(vv_t[:], pb[bank][:])
                VC.inc(v.tensor_tensor_scan(ss_t[:], onesc_s[:, 0:1].to_broadcast([128, CH]), cv_t[:],
                                            scarL[oc][:, 0:1], ALU.mult,
                                            ALU.add))
                vbank_free[bank] = VC.n
            vc_soc = VC.n

            @SE
            def _(s, oc=oc, vc_soc=vc_soc):
                s.wait_ge(VC.h, vc_soc)
                SC.inc(s.copy(scarL[oc][:], ss_t[:, CH - 1:CH]))
            sc_scar[0] = SC.n

            @GP
            def _(g, nt=nt, oc=oc, t0=t0, vc_soc=vc_soc):
                g.wait_ge(VC.h, vc_soc)
                SP.inc(g.dma_start(out=s_d[oc, :, t0:t0 + CH], in_=ss_t[:]),
                       16)
                SP.inc(g.dma_start(out=v_d[oc, :, t0:t0 + CH], in_=vv_t[:]),
                       16)
    ctxP2.close()

    ctxP4 = ExitStack()
    def sb4(name, shape, dt=F32):
        return ctxP4.enter_context(nc.sbuf_tensor(name, shape, dt))

    o_h = sb4("o_h", [128, 8 * CH], BF16)
    o_l = sb4("o_l", [128, 8 * CH], BF16)
    rx_t = sb4("rx_t", [128, CH])
    dx_t = sb4("dx_t", [128, CH])
    o32 = sb4("o32", [128, CH])
    tp4 = sb4("tp4", [128, CH])
    x1t = sb4("x1t", [128, D])
    xrow = sb4("xrow", [128, D])
    agcar_s = sb4("agcar_s", [8, 1040])

    @GP
    def _(g):
        g.wait_ge(VC.h, VC.n)
        g.wait_ge(SC.h, SC.n)
        for oc in range(8):
            SP.inc(g.dma_start(out=agcar_i[0:1, oc * 128:(oc + 1) * 128],
                               in_=scarL[oc][:, 0:1],
                               allow_slow_non_contiguous=True), 16)
        SP.inc(g.dma_start(out=agcar_i[0:1, 1024:1040], in_=Zcar[:, 0:1],
                           allow_slow_non_contiguous=True), 16)
        g.wait_ge(SP.h, SP.n)
        CC.inc(g.collective_compute(
            "AllGather", ALU.bypass, replica_groups=[list(range(NC))],
            ins=[agcar_i[:, :]], outs=[agcar_o[:, :]]))
        g.wait_ge(CC.h, CC.n)
        SP.inc(g.dma_start(out=agcar_s[:], in_=agcar_o[:, :]), 16)
    sp_p3 = SP.n

    @TN
    def _(t):
        t.wait_ge(SP.h, sp_p3)
        for oc in range(8):
            t.matmul(pb[3][:, oc:oc + 1],
                     agcar_s[:, oc * 128:(oc + 1) * 128], seqm_s[:, :],
                     start=(oc == 0), stop=(oc == 7))
        TE.inc(t.matmul(pb[2][0:16, 0:1], agcar_s[:, 1024:1040],
                        seqm_s[:, :], start=True, stop=True))
    te_p3 = TE.n

    @VE
    def _(v, te_p3=te_p3):
        v.wait_ge(TE.h, te_p3)
        v.tensor_copy(car2[:], pb[3][:, 0:8])
        v.tensor_copy(zc2[:], pb[2][0:16, 0:1])
        v.drain()
        v.tensor_scalar(r_s2[:], Z_s[:], zc2[:, 0:1], None, ALU.add)
        v.tensor_tensor(dinv_s[:], r_s2[:], r_s[:], ALU.mult)
        v.tensor_scalar(r_s2[:], dinv_s[:], 1.0, None, ALU.add)
        VC.inc(v.reciprocal(dinv_s[:], r_s2[:]))
    vc_p3 = VC.n


    rx_evac = [VC.n]
    o_used = [TE.n]
    ybank_free = {4: VC.n, 5: VC.n, 6: VC.n, 7: VC.n}

    for nt in range(NT):
        t0 = nt * CH
        for oc in range(8):
            @GP
            def _(g, oc=oc, t0=t0):
                g.wait_ge(VC.h, VC.n)
                SP.inc(g.dma_start(out=ss_t[:], in_=s_d[oc, :, t0:t0 + CH]),
                       16)
                SP.inc(g.dma_start(out=vv_t[:], in_=v_d[oc, :, t0:t0 + CH]),
                       16)
            sp_sv = SP.n

            @TN
            def _(t, oc=oc, t0=t0):
                t.wait_ge(VC.h, max(vc_p3, rx_evac[0]))
                t.matmul(pb[0][:], bsel_s[:, oc * 128:(oc + 1) * 128],
                         r_s[:, t0:t0 + CH], start=True, stop=True)
                TE.inc(t.matmul(pb[1][:], bsel_s[:, oc * 128:(oc + 1) * 128],
                                dinv_s[:, t0:t0 + CH], start=True, stop=True))
            te_rd = TE.n

            @VE
            def _(v, oc=oc, te_rd=te_rd, sp_sv=sp_sv):
                v.wait_ge(TE.h, te_rd)
                v.wait_ge(SP.h, sp_sv)
                v.wait_ge(TE.h, o_used[0])
                v.tensor_copy(rx_t[:], pb[0][:])
                VC.inc(v.tensor_copy(dx_t[:], pb[1][:]))
                rx_evac[0] = VC.n
                v.tensor_scalar(tp4[:], ss_t[:], car2[:, oc:oc + 1], None,
                                ALU.add)
                v.tensor_tensor(o32[:], tp4[:], rx_t[:], ALU.mult)
                v.tensor_tensor(tp4[:], o32[:], vv_t[:], ALU.add)
                v.tensor_tensor(o32[:], tp4[:], dx_t[:], ALU.mult)
                v.tensor_copy(o_h[:, oc * CH:(oc + 1) * CH], o32[:])
                VC.inc(v.tensor_tensor(o_l[:, oc * CH:(oc + 1) * CH], o32[:],
                                       o_h[:, oc * CH:(oc + 1) * CH],
                                       ALU.subtract))
        vc_o = VC.n

        for q in range(4):
            tt = nt * 4 + q

            @GP
            def _(g, tt=tt):
                g.wait_ge(VC.h, VC.n)
                SP.inc(g.dma_start(out=xrow[:],
                                   in_=x_in[tt * 128:(tt + 1) * 128, :]), 16)
            sp_xr = SP.n

            @TN
            def _(t, q=q, vc_o=vc_o):
                t.wait_ge(VC.h, vc_o)
                for wc in range(2):
                    bank = 4 + (q * 2 + wc) % 4
                    t.wait_ge(VC.h, ybank_free[bank])
                    for oc in range(8):
                        oh = o_h[:, oc * CH + q * 128:oc * CH + (q + 1) * 128]
                        ol = o_l[:, oc * CH + q * 128:oc * CH + (q + 1) * 128]
                        owh = ow_h[:, oc * D + wc * 512:oc * D + (wc + 1) * 512]
                        owl = ow_l[:, oc * D + wc * 512:oc * D + (wc + 1) * 512]
                        t.matmul(pb[bank][:], oh, owh, start=(oc == 0),
                                 stop=False)
                        t.matmul(pb[bank][:], oh, owl, start=False,
                                 stop=False)
                        TE.inc(t.matmul(pb[bank][:], ol, owh, start=False,
                                        stop=(oc == 7)))
            te_y = TE.n

            @VE
            def _(v, q=q, te_y=te_y, sp_xr=sp_xr):
                v.wait_ge(TE.h, te_y)
                v.wait_ge(SP.h, SP.n)
                for wc in range(2):
                    bank = 4 + (q * 2 + wc) % 4
                    VC.inc(v.tensor_tensor(x1t[:, wc * 512:(wc + 1) * 512],
                                           pb[bank][:],
                                           xrow[:, wc * 512:(wc + 1) * 512],
                                           ALU.add))
                    ybank_free[bank] = VC.n
            vc_x1 = VC.n

            @SE
            def _(s, vc_x1=vc_x1):
                s.wait_ge(VC.h, vc_x1)
                s.activation(xn32[:], x1t[:], AF.Square,
                             accum_out=s2_)
                SC.inc(s.activation(xn32[:], x1t[:], AF.Identity,
                                    accum_out=s1_))
            sc_sq2 = SC.n

            @VE
            def _(v, sc_sq2=sc_sq2):
                v.wait_ge(SC.h, sc_sq2)
                v.tensor_scalar(mu_, s1_, 1.0 / D, None, ALU.mult)
                v.tensor_scalar(s2b_, s2_, 1.0 / D, 1e-5, ALU.mult,
                                ALU.add)
                v.tensor_scalar(nmu_, mu_, -1.0, None, ALU.mult)
                v.tensor_tensor(var_, mu_, mu_, ALU.mult)
                v.drain()
                v.tensor_tensor(s1_, s2b_, var_, ALU.subtract)
                v.drain()
                VC.inc(v.reciprocal(var_, s1_))
            vc_st2 = VC.n

            @SE
            def _(s, vc_st2=vc_st2):
                s.wait_ge(VC.h, vc_st2)
                SC.inc(s.activation(isd_, var_, AF.Sqrt))
            sc_ap2 = SC.n

            @VE
            def _(v, sc_ap2=sc_ap2):
                v.wait_ge(SC.h, sc_ap2)
                v.wait_ge(SP.h, SP.n)
                v.tensor_scalar(xc_[:], x1t[:], nmu_, None, ALU.add)
                v.tensor_scalar(xn32[:], xc_[:], isd_, None, ALU.mult)
                v.tensor_copy(xnh_t[:], xn32[:])
                VC.inc(v.tensor_tensor(xnl_t[:], xn32[:], xnh_t[:],
                                       ALU.subtract))
            vc_sp2 = VC.n

            @GP
            def _(g, tt=tt, vc_sp2=vc_sp2):
                g.wait_ge(VC.h, vc_sp2)
                SP.inc(g.dma_start(out=x1_d[tt * 128:(tt + 1) * 128, :],
                                   in_=x1t[:]), 16)
                SP.inc(g.dma_start(out=agx_i[tt * 128:(tt + 1) * 128, 0:D],
                                   in_=xnh_t[:]), 16)
                SP.inc(g.dma_start(out=agx_i[tt * 128:(tt + 1) * 128,
                                             D:2 * D], in_=xnl_t[:]), 16)
        o_used[0] = TE.n
    sp_p4 = SP.n

    @GP
    def _(g):
        g.wait_ge(SP.h, sp_p4)
        CC.inc(g.collective_compute(
            "AllGather", ALU.bypass, replica_groups=[list(range(NC))],
            ins=[agx_i[:, :]], outs=[agx_o[:, :]]))
    cc_agx = CC.n

    ctxP4.close()
    ctxA.close()

    ctxG = ExitStack()
    def sbg(name, shape, dt=F32):
        return ctxG.enter_context(nc.sbuf_tensor(name, shape, dt))

    xg_h = sbg("xg_h", [128, 8 * T], BF16)
    xg_l = sbg("xg_l", [128, 8 * T], BF16)
    ohst = sbg("ohst", [128, 2 * 16 * 8])
    plst = sbg("plst", [128, 32])
    lg_t = sbg("lg_t", [128, 128])
    es_t = sbg("es_t", [128, 128])
    m8_t = sbg("m8_t", [128, 16])
    ssum_t = sbg("ssum_t", [128, 16])
    rec_t = sbg("rec_t", [128, 16])
    sm_w = sbg("sm_w", [128, 128])
    gt_w = sbg("gt_w", [128, 1024])
    eq_w = sbg("eq_w", [128, 1024])
    eqm_w = sbg("eqm_w", [128, 1024])
    rks_w = sbg("rks_w", [128, 1024])
    rk_w = sbg("rk_w", [128, 128])
    a1t = sbg("a1t", [128, 8]); a2t = sbg("a2t", [128, 8])
    voff = sbg("voff", [128, 256])
    selo = sbg("selo", [128, 256]); sele = sbg("sele", [128, 256])
    sel5 = sbg("sel5", [128, 256]); cmpq = sbg("cmpq", [128, 256])
    pos_add = sbg("pos_add", [128, 32]); esum = sbg("esum", [128, 32])
    posga = sbg("posga", [128, 32]); valid_w = sbg("valid_w", [128, 32])
    e512s = sbg("e512s", [128, 32]); q_w = sbg("q_w", [128, 32])
    posc_w = sbg("posc_w", [128, 32]); cdif = sbg("cdif", [128, 32])
    w32a = sbg("w32a", [128, 32]); w32b = sbg("w32b", [128, 32])
    s12 = sbg("s12", [128, 16]); rec12 = sbg("rec12", [128, 16])
    g1a = sbg("g1a", [128, 16]); g2a = sbg("g2a", [128, 16])

    @SY
    def _(sy):
        sy.wait_ge(SP.h, sp_p4)
        for fc in range(8):
            LD.inc(sy.dma_start_transpose(
                xg_h[:, fc * T:(fc + 1) * T],
                agx_i[:, fc * 128:(fc + 1) * 128]), 16)
            LD.inc(sy.dma_start_transpose(
                xg_l[:, fc * T:(fc + 1) * T],
                agx_i[:, D + fc * 128:D + (fc + 1) * 128]), 16)
    ld_xg = LD.n

    lg_read = [VC.n]
    for tt in range(16):
        @TN
        def _(t, tt=tt, ld_xg=ld_xg):
            t.wait_ge(LD.h, ld_xg)
            t.wait_ge(VC.h, lg_read[0])
            for fc in range(8):
                xh = xg_h[:, fc * T + tt * 128:fc * T + (tt + 1) * 128]
                xl = xg_l[:, fc * T + tt * 128:fc * T + (tt + 1) * 128]
                t.matmul(pb[0][:, 0:8], xh, wg_h[:, fc, :], start=(fc == 0),
                         stop=False)
                t.matmul(pb[0][:, 0:8], xh, wg_l[:, fc, :], start=False,
                         stop=False)
                TE.inc(t.matmul(pb[0][:, 0:8], xl, wg_h[:, fc, :],
                                start=False, stop=(fc == 7)))
        te_lg = TE.n

        @VE
        def _(v, tt=tt, te_lg=te_lg):
            v.wait_ge(TE.h, te_lg)
            VC.inc(v.tensor_copy(lg_t[:, tt * 8:tt * 8 + 8], pb[0][:, 0:8]))
            lg_read[0] = VC.n

    lg3 = lg_t.ap().rearrange("p (t k) -> p t k", k=8)
    es3 = es_t.ap().rearrange("p (t k) -> p t k", k=8)
    sm3 = sm_w.ap().rearrange("p (t k) -> p t k", k=8)

    @VE
    def _(v):
        v.tensor_reduce(m8_t[:], lg3, AX.X, ALU.max)
        v.drain()
        VC.inc(v.tensor_tensor(
            es3, lg3, m8_t[:, :, None].to_broadcast([128, 16, 8]),
            ALU.subtract))
    vc_es = VC.n

    @SE
    def _(s, vc_es=vc_es):
        s.wait_ge(VC.h, vc_es)
        SC.inc(s.activation(es_t[:], es_t[:], AF.Exp))
    sc_ex = SC.n

    @VE
    def _(v, sc_ex=sc_ex):
        v.wait_ge(SC.h, sc_ex)
        v.tensor_reduce(ssum_t[:], es3, AX.X, ALU.add)
        v.drain()
        v.reciprocal(rec_t[:], ssum_t[:])
        v.drain()
        v.tensor_tensor(sm3, es3,
                        rec_t[:, :, None].to_broadcast([128, 16, 8]),
                        ALU.mult)
        v.drain()
        smA = sm_w.ap().rearrange("p (t k) -> p t k", k=8)[
            :, :, None, :].to_broadcast([128, 16, 8, 8])
        smB = sm_w.ap().rearrange("p (t j) -> p t j", j=8)[
            :, :, :, None].to_broadcast([128, 16, 8, 8])
        gt4 = gt_w.ap().rearrange("p (t j k) -> p t j k", j=8, k=8)
        eq4 = eq_w.ap().rearrange("p (t j k) -> p t j k", j=8, k=8)
        v.tensor_tensor(gt4, smA, smB, ALU.is_gt)
        v.tensor_tensor(eq4, smA, smB, ALU.is_equal)
        v.tensor_tensor(eqm_w[:], eq_w[:], ltm_s[:, :], ALU.mult)
        v.tensor_tensor(rks_w[:], gt_w[:], eqm_w[:], ALU.add)
        v.tensor_reduce(rk_w[:],
                        rks_w.ap().rearrange("p (t j k) -> p t j k", j=8,
                                             k=8).rearrange(
                            "p t j k -> p (t j) k"),
                        AX.X, ALU.add)
        v.tensor_scalar(ohst[:, 0:128], rk_w[:], 0.0, None, ALU.is_equal)
        v.tensor_scalar(ohst[:, 128:256], rk_w[:], 1.0, None, ALU.is_equal)
        v.tensor_tensor(lg_t[:], ohst[:, 0:128], sm_w[:], ALU.mult)
        v.tensor_tensor(es_t[:], ohst[:, 128:256], sm_w[:], ALU.mult)
        v.tensor_reduce(v1st[:],
                        lg_t.ap().rearrange("p (t k) -> p t k", k=8),
                        AX.X, ALU.add)
        VC.inc(v.tensor_reduce(v2st[:],
                               es_t.ap().rearrange("p (t k) -> p t k", k=8),
                               AX.X, ALU.add))
    vc_top = VC.n

    for tt in range(16):
        @TN
        def _(t, tt=tt, vc_top=vc_top):
            t.wait_ge(VC.h, max(vc_top, VC.n))
            oh1s = ohst[:, tt * 8:tt * 8 + 8]
            oh2s = ohst[:, 128 + tt * 8:128 + tt * 8 + 8]
            t.matmul(pb[4][:, 0:8], tri_s[:], oh1s, start=True, stop=False)
            t.matmul(pb[4][:, 0:8], ones1_s[:], cnt_s[0:1, 0:8], start=False,
                     stop=True)
            t.matmul(pb[5][:, 0:8], tri_s[:], oh2s, start=True, stop=False)
            t.matmul(pb[5][:, 0:8], ones1_s[:], cnt_s[0:1, 8:16], start=False,
                     stop=True)
            t.matmul(pb[6][0:1, 0:8], onesc_s[:], oh1s, start=(tt == 0),
                     stop=True, skip_group_check=True)
            TE.inc(t.matmul(pb[6][0:1, 8:16], onesc_s[:], oh2s,
                            start=False, stop=True, skip_group_check=True))
        te_pos = TE.n

        @VE
        def _(v, tt=tt, te_pos=te_pos):
            v.wait_ge(TE.h, te_pos)
            v.tensor_tensor(a1t[:], ohst[:, tt * 8:tt * 8 + 8],
                            pb[4][:, 0:8], ALU.mult)
            v.tensor_tensor(a2t[:], ohst[:, 128 + tt * 8:128 + tt * 8 + 8],
                            pb[5][:, 0:8], ALU.mult)
            v.drain()
            v.reduce_sum(plst[:, tt:tt + 1], a1t[:], axis=AX.X)
            v.reduce_sum(plst[:, 16 + tt:16 + tt + 1], a2t[:], axis=AX.X)
            VC.inc(v.tensor_copy(cnt_s[:], pb[6][0:1, 0:16]))
    vc_cnt = VC.n

    @GP
    def _(g, vc_cnt=vc_cnt):
        g.wait_ge(VC.h, vc_cnt)
        SP.inc(g.dma_start(out=agc_i[:, :],
                           in_=cnt_s.ap().rearrange("p (a b) -> p a b", a=2)),
               16)
        g.wait_ge(SP.h, SP.n)
        CC.inc(g.collective_compute(
            "AllGather", ALU.bypass, replica_groups=[list(range(NC))],
            ins=[agc_i[:, :]], outs=[agc_o[:, :]]))
        g.wait_ge(CC.h, CC.n)
        SP.inc(g.dma_start(out=agc_s[:], in_=agc_o[:, :]), 16)
    sp_agc = SP.n

    @TN
    def _(t):
        t.wait_ge(SP.h, sp_agc)
        t.matmul(pb[7][:, 0:8], offmr0_s[:], agc_s[:], start=True, stop=False)
        TE.inc(t.matmul(pb[7][:, 8:16], offmr1_s[:], agc_s[:], start=False,
                        stop=True))
    te_off = TE.n

    @VE
    def _(v, te_off=te_off):
        v.wait_ge(TE.h, te_off)
        v.tensor_copy(voff.ap().rearrange("p (k t e) -> p k t e", k=2, t=16,
                                          e=8)[:, 0:1, :, :],
                      pb[7][:, 0:8][:, None, :].to_broadcast([128, 16, 8])[
                          :, None, :, :].to_broadcast([128, 1, 16, 8]))
        VC.inc(v.tensor_copy(
            voff.ap().rearrange("p (k t e) -> p k t e", k=2, t=16,
                                e=8)[:, 1:2, :, :],
            pb[7][:, 8:16][:, None, :].to_broadcast([128, 16, 8])[
                :, None, :, :].to_broadcast([128, 1, 16, 8])))
    vc_voff = VC.n

    oh32 = ohst.ap().rearrange("p (c e) -> p c e", e=8)
    @VE
    def _(v, vc_voff=vc_voff):
        v.wait_ge(VC.h, vc_voff)
        v.drain()
        v.tensor_tensor(selo[:], ohst[:], voff[:], ALU.mult)
        v.tensor_tensor(sele[:], ohst[:], evec_s[:], ALU.mult)
        v.tensor_reduce(pos_add[:],
                        selo.ap().rearrange("p (c e) -> p c e", e=8),
                        AX.X, ALU.add)
        v.tensor_reduce(esum[:],
                        sele.ap().rearrange("p (c e) -> p c e", e=8),
                        AX.X, ALU.add)
        v.tensor_tensor(posga[:], pos_add[:], plst[:], ALU.add)
        v.tensor_tensor(sel5[:], ohst[:], e512_s[:], ALU.mult)
        v.tensor_scalar(valid_w[:], posga[:], 4096.0, None, ALU.is_lt)
        v.tensor_reduce(e512s[:],
                        sel5.ap().rearrange("p (c e) -> p c e", e=8),
                        AX.X, ALU.add)
        v.tensor_tensor(w32a[:], esum[:], posga[:], ALU.add)
        v.tensor_tensor(s12[:], v1st[:], v2st[:], ALU.add)
        v.tensor_scalar(w32b[:], w32a[:], float(TRASH), None, ALU.subtract)
        v.reciprocal(rec12[:], s12[:])
        v.tensor_tensor(w32a[:], w32b[:], valid_w[:], ALU.mult)
        v.tensor_tensor(g1a[:], v1st[:], rec12[:], ALU.mult)
        v.tensor_scalar(w32b[:], w32a[:], float(TRASH), None, ALU.add)
        v.tensor_tensor(g2a[:], v2st[:], rec12[:], ALU.mult)
        v.tensor_copy(sct[:], w32b[:])
        v.tensor_scalar(posc_w[:], posga[:], 4095.0, None, ALU.min)
        v.tensor_tensor(g1st[:], g1a[:], valid_w[:, 0:16], ALU.mult)
        v.tensor_tensor(g2st[:], g2a[:], valid_w[:, 16:32], ALU.mult)
        v.tensor_tensor(cmpq.ap().rearrange("p (c e) -> p c e", e=8),
                        posc_w[:, :, None].to_broadcast([128, 32, 8]),
                        thr_s.ap().rearrange("p (c e) -> p c e", e=8),
                        ALU.is_ge)
        v.drain()
        v.tensor_reduce(q_w[:],
                        cmpq.ap().rearrange("p (c e) -> p c e", e=8),
                        AX.X, ALU.add)
        v.drain()
        v.tensor_scalar(w32a[:], q_w[:], 512.0, None, ALU.mult)
        v.tensor_scalar(w32b[:], q_w[:], 4096.0, None, ALU.mult)
        v.tensor_tensor(cdif[:], posc_w[:], w32a[:], ALU.subtract)
        v.tensor_tensor(w32a[:], w32b[:], e512s[:], ALU.add)
        v.drain()
        v.tensor_tensor(w32b[:], w32a[:], cdif[:], ALU.add)
        v.drain()
        v.tensor_tensor(w32a[:], w32b[:], valid_w[:], ALU.mult)
        v.drain()
        VC.inc(v.tensor_copy(ybst[:], w32a[:]))
    vc_route = VC.n

    @GP
    def _(g, vc_route=vc_route):
        g.wait_ge(VC.h, vc_route)
        SP.inc(g.dma_start(
            out=invl_d.rearrange("(p c) one -> p (c one)", p=128),
            in_=zer_s[:]), 16)
        g.wait_ge(SP.h, SP.n)
        for col in range(32):
            SP.inc(g.indirect_dma_start(
                out=invl_d[:, :],
                out_offset=IndirectOffsetOnAxis(sct[:, col:col + 1], 0),
                in_=gtok_s[:, col % 16:col % 16 + 1], in_offset=None), 16)
        g.wait_ge(SP.h, SP.n)
        CC.inc(g.collective_compute(
            "AllReduce", ALU.add, replica_groups=[list(range(NC))],
            ins=[invl_d[:, :]], outs=[invg_d[:, :]]))
        g.wait_ge(CC.h, CC.n)
        for col in range(32):
            SP.inc(g.indirect_dma_start(
                out=invt[:, col:col + 1], out_offset=None,
                in_=invg_d[:, :],
                in_offset=IndirectOffsetOnAxis(einv_s[:, col:col + 1], 0)),
                16)
    sp_inv = SP.n

    ctxG.close()

    ctxF = ExitStack()
    def sbf(name, shape, dt=F32):
        return ctxF.enter_context(nc.sbuf_tensor(name, shape, dt))

    h_h = sbf("h_h", [128, 32 * CH], BF16)
    h_l = sbf("h_l", [128, 32 * CH], BF16)
    xf_h2 = [sbf(f"xf_h{i}", [128, 8 * CH], BF16) for i in range(2)]
    xf_l2 = [sbf(f"xf_l{i}", [128, 8 * CH], BF16) for i in range(2)]
    w1s_h = [sbf(f"w1s_h{i}", [128, 8, 128], BF16) for i in range(2)]
    w1s_l = [sbf(f"w1s_l{i}", [128, 8, 128], BF16) for i in range(2)]
    w2s_h = [sbf(f"w2s_h{i}", [128, D], BF16) for i in range(2)]
    w2s_l = [sbf(f"w2s_l{i}", [128, D], BF16) for i in range(2)]
    dsb2 = [sbf(f"dsb{i}", [128, 4 * 2 * D], BF16) for i in range(2)]
    h32a = sbf("h32a", [128, CH])
    h32b = sbf("h32b", [128, CH])
    yt_ = sbf("yt_", [128, CH])
    invf = sbf("invf", [128, 32])
    invg2 = sbf("invg2", [128, 32])

    @VE
    def _(v):
        v.wait_ge(SP.h, sp_inv)
        v.tensor_copy(invf[:], invt[:])
        v.drain()
        v.tensor_scalar(invg2[:], invf[:], -1.0, 0.0, ALU.add, ALU.max)
        v.drain()
        VC.inc(v.tensor_copy(invt[:], invg2[:]))
    vc_invt = VC.n

    xf_free = [TE.n, TE.n]
    w1_free = [TE.n, TE.n]
    w2_free = [TE.n, TE.n]
    h_free = [VC.n] * 4
    y2_free = [VC.n] * 8
    sp_disp_a = [0] * TC
    ld_xf_a = [0] * TC

    def emit_gath(tc):
        bb = tc % 2

        @GP
        def _(g, tc=tc, bb=bb):
            if tc == 0:
                g.wait_ge(VC.h, vc_invt)
                g.wait_ge(CC.h, cc_agx)
            if tc >= 2:
                g.wait_ge(LD.h, ld_xf_a[tc - 2])
            for q in range(4):
                SP.inc(g.indirect_dma_start(
                    out=dsb2[bb][:, q * 2 * D:(q + 1) * 2 * D],
                    out_offset=None,
                    in_=agx_o[:, :],
                    in_offset=IndirectOffsetOnAxis(
                        invt[:, tc * 4 + q:tc * 4 + q + 1], 0)), 16)
        sp_disp_a[tc] = SP.n

    def emit_transp(tc):
        bb = tc % 2

        @SY
        def _(sy, tc=tc, bb=bb):
            sy.wait_ge(SP.h, sp_disp_a[tc])
            sy.wait_ge(TE.h, xf_free[bb])
            for q in range(4):
                for fc in range(8):
                    LD.inc(sy.dma_start_transpose(
                        xf_h2[bb][:, fc * CH + q * 128:
                                  fc * CH + (q + 1) * 128],
                        dsb2[bb][:, q * 2 * D + fc * 128:
                                 q * 2 * D + (fc + 1) * 128]), 16)
                    LD.inc(sy.dma_start_transpose(
                        xf_l2[bb][:, fc * CH + q * 128:
                                  fc * CH + (q + 1) * 128],
                        dsb2[bb][:, q * 2 * D + D + fc * 128:
                                 q * 2 * D + D + (fc + 1) * 128]), 16)
        ld_xf_a[tc] = LD.n

    emit_gath(0)
    emit_transp(0)
    for tc in range(TC):
        s0 = tc * CH
        bb = tc % 2
        xf_h = xf_h2[bb]
        xf_l = xf_l2[bb]
        if tc + 1 < TC:
            emit_gath(tc + 1)
            emit_transp(tc + 1)
        ld_xf = ld_xf_a[tc]

        for hc in range(32):
            buf = hc % 2
            bank = hc % 4

            @SE
            def _(sy, tc=tc, hc=hc, buf=buf):
                if tc == 0 and hc == 0:
                    sy.wait_ge(SP.h, sp_w12)
                    sy.wait_ge(LD.h, ld_w12)
                sy.wait_ge(TE.h, w1_free[buf])
                WL.inc(sy.dma_start(
                    out=w1s_h[buf][:],
                    in_=w1h_d[hc * 128:(hc + 1) * 128, :]), 16)
                WL.inc(sy.dma_start(
                    out=w1s_l[buf][:],
                    in_=w1l_d[hc * 128:(hc + 1) * 128, :]), 16)
            ld_w1 = WL.n

            @TN
            def _(t, tc=tc, hc=hc, buf=buf, bank=bank, ld_w1=ld_w1,
                  ld_xf=ld_xf, bb=bb, xf_h=xf_h, xf_l=xf_l):
                t.wait_ge(WL.h, ld_w1)
                if hc == 0:
                    t.wait_ge(LD.h, ld_xf)
                    if tc == 0:
                        t.wait_ge(VC.h, vc_route)
                t.wait_ge(VC.h, h_free[bank])
                if tc > 0 and hc < 4:
                    t.wait_ge(VC.h, y2_free[bank])
                for fc in range(8):
                    xh = xf_h[:, fc * CH:(fc + 1) * CH]
                    xl = xf_l[:, fc * CH:(fc + 1) * CH]
                    t.matmul(pb[bank][:], w1s_h[buf][:, fc, :], xh,
                             start=(fc == 0), stop=False)
                    t.matmul(pb[bank][:], w1s_h[buf][:, fc, :], xl,
                             start=False, stop=False)
                    TE.inc(t.matmul(pb[bank][:], w1s_l[buf][:, fc, :], xh,
                                    start=False, stop=(fc == 7)))
                w1_free[buf] = TE.n
                if hc == 31:
                    xf_free[bb] = TE.n
            te_h1 = TE.n

            @VE
            def _(v, hc=hc, bank=bank, te_h1=te_h1):
                v.wait_ge(TE.h, te_h1)
                h32 = h32a if hc % 2 == 0 else h32b
                v.tensor_scalar(h32[:], pb[bank][:], b1m_s[:, hc:hc + 1],
                                0.0, ALU.add, ALU.max)
                v.drain()
                v.tensor_copy(h_h[:, hc * CH:(hc + 1) * CH], h32[:])
                v.drain()
                VC.inc(v.tensor_tensor(h_l[:, hc * CH:(hc + 1) * CH], h32[:],
                                       h_h[:, hc * CH:(hc + 1) * CH],
                                       ALU.subtract))
                h_free[bank] = VC.n
        vc_hdone = VC.n

        for hc in range(32):
            buf = hc % 2

            @SE
            def _(sy, hc=hc, buf=buf):
                sy.wait_ge(TE.h, w2_free[buf])
                WL.inc(sy.dma_start(
                    out=w2s_h[buf][:],
                    in_=w2h_d[hc * 128:(hc + 1) * 128, :]), 16)
                WL.inc(sy.dma_start(
                    out=w2s_l[buf][:],
                    in_=w2l_d[hc * 128:(hc + 1) * 128, :]), 16)
            ld_w2 = WL.n

            @TN
            def _(t, hc=hc, buf=buf, ld_w2=ld_w2, vc_hdone=vc_hdone):
                t.wait_ge(WL.h, ld_w2)
                if hc == 0:
                    t.wait_ge(VC.h, vc_hdone)
                    for b8 in range(8):
                        t.wait_ge(VC.h, y2_free[b8])
                for q in range(4):
                    hh = h_h[:, hc * CH + q * 128:hc * CH + (q + 1) * 128]
                    hl = h_l[:, hc * CH + q * 128:hc * CH + (q + 1) * 128]
                    for mc in range(2):
                        bank = q * 2 + mc
                        wh = w2s_h[buf][:, mc * 512:(mc + 1) * 512]
                        wl = w2s_l[buf][:, mc * 512:(mc + 1) * 512]
                        t.matmul(pb[bank][:], hh, wh, start=(hc == 0),
                                 stop=False, skip_group_check=True)
                        t.matmul(pb[bank][:], hh, wl, start=False,
                                 stop=False, skip_group_check=True)
                        TE.inc(t.matmul(pb[bank][:], hl, wh, start=False,
                                        stop=(hc == 31),
                                        skip_group_check=True))
                w2_free[buf] = TE.n
        te_y2 = TE.n

        for q in range(4):
            for mc in range(2):
                bank = q * 2 + mc

                @VE
                def _(v, bank=bank, mc=mc, te_y2=te_y2):
                    v.wait_ge(TE.h, te_y2)
                    v.wait_ge(SP.h, SP.n)
                    VC.inc(v.tensor_tensor(
                        yt_[:], pb[bank][:],
                        b2r_s[:, mc * 512:(mc + 1) * 512], ALU.add))
                    y2_free[bank] = VC.n
                vc_y = VC.n

                @GP
                def _(g, q=q, mc=mc, s0=s0, vc_y=vc_y):
                    g.wait_ge(VC.h, vc_y)
                    SP.inc(g.dma_start(
                        out=agy_i[s0 + q * 128:s0 + (q + 1) * 128,
                                  mc * 512:(mc + 1) * 512], in_=yt_[:]), 16)

        @GP
        def _(g, tc=tc, s0=s0):
            g.wait_ge(SP.h, SP.n)
            CC.inc(g.collective_compute(
                "AllGather", ALU.bypass, replica_groups=[list(range(NC))],
                ins=[agy_i[s0:s0 + CH, :]],
                outs=[agy_o[tc * CAP:(tc + 1) * CAP, :]]))
    cc_ally = CC.n

    y1t2 = [sbf(f"y1t{i}", [128, D]) for i in range(2)]
    y2t2 = [sbf(f"y2t{i}", [128, D]) for i in range(2)]
    x1r2 = [sbf(f"x1r{i}", [128, D]) for i in range(2)]
    fin2 = [sbf(f"fin{i}", [128, D]) for i in range(2)]
    fin0s = sbf("fin0s", [128, D])

    vc_fin2 = [0] * 18
    sp_st2 = [0] * 18
    for tt in range(16):
        b = tt % 2
        y1t = y1t2[b]; y2t = y2t2[b]; x1r = x1r2[b]
        fin = fin2[b]; fin0 = fin0s

        @GP
        def _(g, tt=tt, y1t=y1t, y2t=y2t, x1r=x1r):
            if tt == 0:
                g.wait_ge(CC.h, cc_ally)
            if tt > 1:
                g.wait_ge(VC.h, vc_fin2[tt])
            SP.inc(g.indirect_dma_start(
                out=y1t[:, :], out_offset=None, in_=agy_o[:, :],
                in_offset=IndirectOffsetOnAxis(ybst[:, tt:tt + 1], 0)), 16)
            SP.inc(g.indirect_dma_start(
                out=y2t[:, :], out_offset=None, in_=agy_o[:, :],
                in_offset=IndirectOffsetOnAxis(ybst[:, 16 + tt:16 + tt + 1],
                                               0)), 16)
            SP.inc(g.dma_start(out=x1r[:],
                               in_=x1_d[tt * 128:(tt + 1) * 128, :]), 16)
        sp_y = SP.n

        @VE
        def _(v, tt=tt, sp_y=sp_y, y1t=y1t, y2t=y2t, x1r=x1r, fin=fin,
              fin0=fin0):
            v.wait_ge(SP.h, sp_y)
            if tt > 1:
                v.wait_ge(SP.h, sp_st2[tt])
            v.scalar_tensor_tensor(fin0[:], y1t[:], g1st[:, tt:tt + 1],
                                   x1r[:], ALU.mult, ALU.add)
            VC.inc(v.scalar_tensor_tensor(fin[:], y2t[:], g2st[:, tt:tt + 1],
                                          fin0[:], ALU.mult, ALU.add))
        vc_fin2[tt + 2] = VC.n
        vc_fin = VC.n

        @GP
        def _(g, tt=tt, vc_fin=vc_fin, fin=fin):
            g.wait_ge(VC.h, vc_fin)
            SP.inc(g.dma_start(out=y_out[tt * 128:(tt + 1) * 128, :],
                               in_=fin[:]), 16)
        sp_st2[tt + 2] = SP.n

    @GP
    def _(g):
        g.wait_ge(SP.h, SP.n)

    ctxF.close()
    ctx.close()
    return nc


def prep_inputs(inputs):
    x = np.ascontiguousarray(np.asarray(inputs["L0_attn_x"], np.float32)
                             .reshape(NC * T, D))
    k1 = np.asarray(inputs["L0_attn_k1"], np.float32)
    k2 = np.asarray(inputs["L0_attn_k2"], np.float32)
    k3 = np.asarray(inputs["L0_attn_k3"], np.float32)
    a1 = np.asarray(inputs["L0_attn_a1"], np.float32).reshape(H, 64)
    a2 = np.asarray(inputs["L0_attn_a2"], np.float32).reshape(H, 64)
    b1 = np.asarray(inputs["L0_attn_b1"], np.float32).reshape(H, 64)
    b2 = np.asarray(inputs["L0_attn_b2"], np.float32).reshape(H, 64)
    c_ = np.asarray(inputs["L0_attn_c"], np.float32).reshape(H, 64)
    vw = np.asarray(inputs["L0_attn_value_weight"], np.float32)
    ow = np.asarray(inputs["L0_attn_output_weight"], np.float32)
    wg = np.asarray(inputs["L0_ffn_wg"], np.float32)
    w1 = np.asarray(inputs["L0_ffn_w1"], np.float32)
    fb1 = np.asarray(inputs["L0_ffn_b1"], np.float32)
    w2 = np.asarray(inputs["L0_ffn_w2"], np.float32)
    fb2 = np.asarray(inputs["L0_ffn_b2"], np.float32)

    wkt = np.ascontiguousarray(np.concatenate([k1, k2, k3], 0).T)
    wv2 = np.ascontiguousarray(vw.transpose(1, 0, 2).reshape(D, D))
    ow2 = np.ascontiguousarray(ow.transpose(0, 2, 1).reshape(D, D))

    cMm = np.zeros((128, 128), np.float32)
    for sc in range(8):
        for lh in range(2):
            h = sc * 2 + lh
            cMm[lh * 64:(lh + 1) * 64, sc * 16 + h] = c_[h]
    pc = {}
    for nm, arr in (("a1c", a1), ("b1c", b1), ("a2c", a2), ("b2c", b2)):
        pc[nm] = np.ascontiguousarray(arr.reshape(8, 128).T)

    trim = np.ascontiguousarray(np.triu(np.ones((128, 128), np.float32), 1))
    bselm = np.zeros((16, 1024), np.float32)
    for oc in range(8):
        for lh in range(2):
            h = oc * 2 + lh
            bselm[h, oc * 128 + lh * 64:oc * 128 + (lh + 1) * 64] = 1.0
    evec = np.ascontiguousarray(np.broadcast_to(
        np.tile(np.arange(8, dtype=np.float32) * 4096.0, 32), (128, 256)))
    e512 = np.ascontiguousarray(np.broadcast_to(
        np.tile(np.arange(8, dtype=np.float32) * 512.0, 32), (128, 256)))
    thr = np.ascontiguousarray(np.broadcast_to(
        np.tile(np.arange(1, 9, dtype=np.float32) * 512.0, 32), (128, 256)))
    ltm = np.zeros((8, 8), np.float32)
    for j in range(8):
        ltm[j, :j] = 1.0
    ltm = np.ascontiguousarray(np.broadcast_to(
        np.tile(ltm.reshape(64), 16), (128, 1024)))
    ones1 = np.ones((1, 128), np.float32)
    onesc = np.ones((128, 1), np.float32)

    in_maps = []
    for j in range(NC):
        b1mh = np.ascontiguousarray(fb1[j].reshape(32, 128).T)
        b2rh = np.ascontiguousarray(
            np.broadcast_to(fb2[j].reshape(1, D), (128, D)))
        b = j // 4
        nloc = np.ascontiguousarray(np.broadcast_to(
            ((np.arange(T) + (j % 4) * T).astype(np.float32) / 8191.0)
            .reshape(1, T), (128, T)))
        seqm = np.zeros((8, 1), np.float32)
        seqm[4 * b:j, 0] = 1.0
        offm = np.zeros((16, 2), np.float32)
        offm[0:2 * j:2, 0] = 1.0
        offm[0::2, 1] = 1.0
        offm[1:2 * j:2, 1] = 1.0
        offmr0 = np.ascontiguousarray(np.broadcast_to(offm[:, 0:1], (16, 128)))
        offmr1 = np.ascontiguousarray(np.broadcast_to(offm[:, 1:2], (16, 128)))
        gt = (np.arange(T, dtype=np.int64) + j * T + 1).astype(np.int32)
        gtok = np.ascontiguousarray(gt.reshape(16, 128).T)
        einv = np.ascontiguousarray(
            (j * 4096 + np.arange(32, dtype=np.int32)[None, :] * 128
             + np.arange(128, dtype=np.int32)[:, None]).astype(np.int32))
        in_maps.append({
            "x": x[j * T:(j + 1) * T],
            "wkt": wkt, "wv2": wv2, "ow2": ow2, "wgm": wg,
            "w1m": w1[j], "w2m": w2[j], "b1m": b1mh, "b2row": b2rh,
            "cM": cMm, "a1c": pc["a1c"], "b1c": pc["b1c"],
            "a2c": pc["a2c"], "b2c": pc["b2c"],
            "tri": trim, "bsel": bselm,
            "nvec": nloc, "seqm": seqm,
            "offm": offm, "offmr0": offmr0, "offmr1": offmr1,
            "gtok": gtok, "evec": evec, "e512": e512,
            "thr": thr, "ltm": ltm, "ones1": ones1, "onesc": onesc,
            "einv": einv,
        })
    return in_maps


def kernel(**inputs):
    nc = build_program()
    in_maps = prep_inputs(inputs)
    res = run_bass_kernel_spmd(nc, in_maps, core_ids=list(range(NC)))
    out = np.concatenate([res.results[j]["y_out"] for j in range(NC)], axis=0)
    return out.reshape(2, 8192, 1024)
